# version 08
# baseline (speedup 1.0000x reference)
import sys
if "/opt/trn_rl_repo" not in sys.path:
    sys.path.insert(0, "/opt/trn_rl_repo")
from contextlib import ExitStack

import numpy as np
import ml_dtypes

import concourse.bass as bass
import concourse.mybir as mybir
from concourse.bass_utils import run_bass_kernel_spmd

F32 = mybir.dt.float32
BF16 = mybir.dt.bfloat16
I32 = mybir.dt.int32
AF = mybir.ActivationFunctionType
ALU = mybir.AluOpType
AX = mybir.AxisListType

S = 4096
D = 1024
H = 16
DH = 64
VG = 66
DHID = 4096
NT = S // 128
DT = D // 128
HT = DHID // 128
EPS = 1e-5
PI = float(np.float32(np.pi))
TWO_PI = float(np.float32(2 * np.pi))
INV2PI = float(np.float32(1.0 / TWO_PI))

_CONFIG = {"n_sub": 4}
_ENGINES = ("sync", "scalar", "vector", "tensor", "gpsimd")


class Sem:
    __slots__ = ("h", "n", "name")

    def __init__(self, st, nc, name):
        self.h = st.enter_context(nc.semaphore(name))
        self.n = 0
        self.name = name


class Prog:
    def __init__(self, nc, st):
        self.nc = nc
        self.st = st
        self.streams = {e: [] for e in _ENGINES}

    def sem(self, name):
        s = Sem(self.st, self.nc, name)
        if not hasattr(self, "all_sems"):
            self.all_sems = []
        self.all_sems.append(s)
        return s

    def emit(self, engine, method, *args, waits=(), incs=(), **kw):
        ser = _CONFIG.get("serial")
        if ser == "aw_ve" and engine == "scalar":
            waits = list(waits) + [(s, s.n) for s in self.all_sems
                                   if s.name == "ve" and s.n > 0]
        elif ser == "vw_ae" and engine == "vector":
            waits = list(waits) + [(s, s.n) for s in self.all_sems
                                   if s.name == "ae" and s.n > 0]
        elif ser and ser.startswith("eng:"):
            _, eng_n, ctr = ser.split(":")
            if engine == eng_n:
                waits = list(waits) + [(s, s.n) for s in self.all_sems
                                       if s.n > 0 and s.name == ctr]
        elif ser and ser not in ("aw_ve", "vw_ae"):
            wl = tuple(ser.split(","))
            waits = list(waits) + [(s, s.n) for s in self.all_sems
                                   if s.n > 0 and s.name in wl]
        lw = [(s.h, int(v)) for (s, v) in waits if v > 0]
        li = []
        for (s, k) in incs:
            s.n += k
            li.append((s.h, k, s.n))
        self.streams[engine].append((method, args, kw, lw, li))

    def replay(self, engine, eng):
        for m, args, kw, lw, li in self.streams[engine]:
            for h, v in lw:
                eng.wait_ge(h, v)
            ins = eng.nop() if m == "nop" else getattr(eng, m)(*args, **kw)
            for i, (h, k, tgt) in enumerate(li):
                if i == 0:
                    ins.then_inc(h, k)
                    if len(li) > 1 and m == "nop" and not lw:
                        pass
                elif i == 1 and m != "nop":
                    eng.wait_ge(li[0][0], li[0][2])
                    eng.nop().then_inc(h, k)
                else:
                    eng.nop().then_inc(h, k)


class Bank:

    _states = {}

    def __init__(self, P, ap, key):
        self.ap = ap
        st_ = Bank._states.get((id(P), key))
        if st_ is None:
            st_ = {"p": P.sem(key + "_p"), "c": P.sem(key + "_c"), "u": 0}
            Bank._states[(id(P), key)] = st_
        self._s = st_

    @property
    def prod(self):
        return self._s["p"]

    @property
    def cons(self):
        return self._s["c"]

    @property
    def uses(self):
        return self._s["u"]

    def acquire(self):
        pe_waits = [(self.cons, self.uses)] if self.uses > 0 else []
        self._s["u"] += 1
        return pe_waits, [(self.prod, self.uses)]


def _alias(nc, arena, name, shape, dtype, offset):
    base = nc.lookup_mloc(arena).addr
    return nc.alloc_sbuf_tensor_at(name, shape, dtype, offset=base + offset)


class ArenaGroup:
    def __init__(self, nc, arena_h, size, base=0):
        self.nc = nc
        self.arena_h = arena_h
        self.size = size
        self.off = base

    def alloc(self, name, shape, dtype):
        per_part = int(np.prod(shape[1:])) * mybir.dt.size(dtype)
        off = self.off
        self.off = (off + per_part + 127) & ~127
        assert self.off <= self.size, (name, self.off, self.size)
        if not hasattr(self, "offsets"):
            self.offsets = {}
        self.offsets[name] = off
        return _alias(self.nc, self.arena_h, name, shape, dtype, off)

    def alias_at(self, at_name, name, shape, dtype):
        off = self.offsets[at_name]
        per_part = int(np.prod(shape[1:])) * mybir.dt.size(dtype)
        assert off + per_part <= self.size
        return _alias(self.nc, self.arena_h, name, shape, dtype, off)


def build(n_sub=4):
    Bank._states.clear()
    n_sub = max(1, min(4, int(n_sub)))
    nc = bass.Bass("TRN2", target_bir_lowering=False,
                   detect_race_conditions=False)

    def din(name, shape, dtype=F32):
        return nc.dram_tensor(name, shape, dtype, kind="ExternalInput").ap()

    def dscr(name, shape, dtype=F32):
        return nc.dram_tensor(name, shape, dtype, kind="Internal").ap()

    x_in = din("x", [S, D])
    y_out = nc.dram_tensor("y", [S, D], F32, kind="ExternalOutput").ap()
    dbgf = nc.dram_tensor("dbgf", [128, 256], F32, kind="ExternalOutput").ap()
    dbgb = nc.dram_tensor("dbgb", [128, 16384], BF16,
                          kind="ExternalOutput").ap()

    aq_w = din("aq_w", [128, DT * D])
    ak_w = din("ak_w", [128, DT * D])
    av_w = din("av_w", [128, DT * D])
    ao_w = din("ao_w", [128, DT * D])
    bq_w = din("bq_w", [128, DT * D])
    bk_w = din("bk_w", [128, DT * D])
    bv_w = din("bv_w", [128, DT * D])
    b_bq = din("b_bq", [128, DT])
    b_bk = din("b_bk", [128, DT])
    b_pb = din("b_pb", [128, DT])
    b_bv_row = din("b_bv_row", [1, D])
    invwl_in = din("invwl_c", [128, DT])
    f_w1 = din("f_w1", [HT, 128, DT * 128])
    f_w2 = din("f_w2", [HT, 128, DT * 128])
    f_h1 = din("f_h1", [HT, 128, HT * 128])
    f_h2 = din("f_h2", [HT, 128, HT * 128])
    f_w3 = din("f_w3", [DT, 128, HT * 128])
    f_s1 = din("f_s1", [128, HT])
    f_s2 = din("f_s2", [128, HT])
    f_sh1 = din("f_sh1", [128, HT])
    f_sh2 = din("f_sh2", [128, HT])
    f_s3 = din("f_s3", [128, DT])
    c_idb = din("c_idb", [128, 128], BF16)
    c_idf = din("c_idf", [128, 128], F32)
    c_stair = din("c_stair", [128, 128], BF16)

    xs = dscr("xs", [S, D])
    qt_s = dscr("qt_s", [DT, 128, S], BF16)
    cq_s = dscr("cq_s", [DT, 128, S], BF16)
    sq_s = dscr("sq_s", [DT, 128, S], BF16)
    ck2_s = dscr("ck2_s", [DT, 128, S // 2], BF16)
    sk2_s = dscr("sk2_s", [DT, 128, S // 2], BF16)
    v2_s = dscr("v2_s", [NT // 2, 128, D], BF16)
    po_s = dscr("po_s", [8, 128, 4 * D])
    pd_s = dscr("pd_s", [8, 128, 4])
    wsc_w1 = dscr("wsc_w1", [HT, 128, DT * 128], BF16)
    wsc_w2 = dscr("wsc_w2", [HT, 128, DT * 128], BF16)
    wsc_h1 = dscr("wsc_h1", [HT, 128, HT * 128], BF16)
    wsc_h2 = dscr("wsc_h2", [HT, 128, HT * 128], BF16)
    wsc_w3 = dscr("wsc_w3", [DT, 128, HT * 128], BF16)
    wsc_bq = dscr("wsc_bq", [128, DT * D], BF16)
    wsc_bk = dscr("wsc_bk", [128, DT * D], BF16)
    wsc_bv = dscr("wsc_bv", [128, DT * D], BF16)

    st = ExitStack()
    with st:
        P = Prog(nc, st)

        def sb(name, shape, dtype):
            return st.enter_context(nc.sbuf_tensor(name, shape, dtype))

        idb = sb("idb", [128, 128], BF16)
        idf = sb("idf", [128, 128], F32)
        stair = sb("stair", [128, 128], BF16)
        cpool = sb("cpool", [128, 8], F32)
        C_EXPA, C_EXPB, C_HALFPI = 0, 1, 2
        invwl = sb("invwl", [128, DT], F32)
        bqc = sb("bqc", [128, DT], F32)
        bkc = sb("bkc", [128, DT], F32)
        pbcn = sb("pbcn", [128, DT], F32)
        c2q = sb("c2q", [128, DT], F32)
        c2k = sb("c2k", [128, DT], F32)
        bvrow = sb("bvrow", [1, D], F32)
        bvrow_b = sb("bvrow_b", [1, D], BF16)
        ones_rf = sb("ones_rf", [1, 64], F32)
        ones_rb = sb("ones_rb", [1, 128], BF16)
        ones_cb = sb("ones_cb", [128, 1], BF16)
        s1c = sb("s1c", [128, HT], F32)
        s2c = sb("s2c", [128, HT], F32)
        sh1c = sb("sh1c", [128, HT], F32)
        sh2c = sb("sh2c", [128, HT], F32)
        s3c = sb("s3c", [128, DT], F32)
        sxh = sb("sxh", [128, 2 * NT], F32)
        ssq = sb("ssq", [128, NT], F32)
        sxs = sb("sxs", [128, NT], F32)
        stA = sb("stA", [128, NT], F32)
        rstd = sb("rstd", [128, NT], F32)
        nmr = sb("nmr", [128, NT], F32)

        rem = nc.sbuf_bytes_remaining
        ARENA = rem & ~127
        arena_h = st.enter_context(
            nc.sbuf_tensor("arena", [128, ARENA], mybir.dt.uint8))

        gA = ArenaGroup(nc, arena_h, ARENA)
        ktr = gA.alloc("ktr", [128, DT * S], BF16)
        vres = gA.alloc("vres", [128, NT * H * VG + 128], BF16)
        aw = gA.alloc("aw", [128, DT * D], BF16)
        dumpf = gA.alloc("dumpf", [128, D], BF16)
        gA_tail = gA.off
        g1a = ArenaGroup(nc, arena_h, ARENA, base=gA_tail)
        lnT_a = g1a.alloc("lnT_a", [128, DT * 512], BF16)
        xpanA = [g1a.alloc(f"xpanA{i}", [128, D], F32) for i in range(2)]
        lnb = g1a.alloc("lnb", [128, D], BF16)
        qtst = g1a.alloc("qtst", [128, DT * 512], BF16)
        g1b = ArenaGroup(nc, arena_h, ARENA, base=gA_tail)
        qtp = [g1b.alloc(f"qtp{i}", [128, DT * 512], BF16) for i in range(2)]
        pring = [g1b.alloc(f"pring{i}", [128, 1024], BF16) for i in range(3)]
        aoT = g1b.alloc("aoT", [128, DT * 512], BF16)
        bcs = g1b.alloc("bcs", [64, 512], F32)
        rcp = g1b.alloc("rcp", [1, 512], F32)
        xq = [g1b.alloc(f"xq{i}", [128, D], F32) for i in range(2)]
        xo = g1b.alloc("xo", [128, D], F32)

        gF = ArenaGroup(nc, arena_h, ARENA)
        aT = gF.alloc("aT", [128, HT * 1024], BF16)
        bT = gF.alloc("bT", [128, HT * 1024], BF16)
        lnT_f = gF.alloc("lnT_f", [128, DT * 1024], BF16)
        wring = [gF.alloc(f"wring{i}", [128, HT * 128], BF16) for i in range(2)]
        w12r = [gF.alloc(f"w12r{i}", [128, DT * 128], BF16) for i in range(4)]
        silub = gF.alloc("silub", [128, 2048], F32)
        xtf = [gF.alloc(f"xtf{i}", [128, D], F32) for i in range(2)]
        xof = gF.alloc("xof", [128, D], F32)
        lnbf = gF.alloc("lnbf", [128, D], BF16)
        dumpf2 = gF.alloc("dumpf2", [128, D], BF16)
        w3st = gF.alias_at("lnT_f", "w3st", [128, DT * 512], F32)

        gB = ArenaGroup(nc, arena_h, ARENA)
        ck1 = gB.alloc("ck1", [128, DT * 2048], BF16)
        sk1 = gB.alloc("sk1", [128, DT * 2048], BF16)
        v1r = gB.alloc("v1r", [128, (NT // 2) * D], BF16)
        resid_end = gB.off
        g3a = ArenaGroup(nc, arena_h, ARENA, base=resid_end)
        lnT_b = g3a.alloc("lnT_b", [128, DT * 512], BF16)
        bw = g3a.alloc("bw", [128, DT * D], BF16)
        xpanB = [g3a.alloc(f"xpanB{i}", [128, D], F32) for i in range(2)]
        theta = g3a.alloc("theta", [128, 512], F32)
        ubuf = g3a.alloc("ubuf", [128, 512], F32)
        kint = g3a.alloc("kint", [128, 512], I32)
        kflt = g3a.alloc("kflt", [128, 512], F32)
        rbuf = g3a.alloc("rbuf", [128, 512], F32)
        trst = [g3a.alloc(f"trst{i}", [128, DT * 512], BF16) for i in range(4)]
        vst = [g3a.alloc(f"vst{i}", [128, D], BF16) for i in range(2)]
        lnb3 = g3a.alloc("lnb3", [128, D], BF16)
        g3b = ArenaGroup(nc, arena_h, ARENA, base=resid_end)
        pbuf = g3b.alloc("pbuf", [128, 16 * 512], BF16)
        cqp = [g3b.alloc(f"cqp{i}", [128, DT * 512], BF16) for i in range(2)]
        sqp = [g3b.alloc(f"sqp{i}", [128, DT * 512], BF16) for i in range(2)]
        post = g3b.alloc("post", [128, 4 * D], F32)
        poin = g3b.alloc("poin", [128, 4 * D], F32)
        denst = g3b.alloc("denst", [128, 4], F32)
        denin = g3b.alloc("denin", [128, 4], F32)
        drc = g3b.alloc("drc", [128, 4], F32)
        xq3 = [g3b.alloc(f"xq3{i}", [128, D], F32) for i in range(2)]
        xo3 = g3b.alloc("xo3", [128, D], F32)
        dump3b = g3b.alloc("dump3b", [128, D], BF16)

        pA = st.enter_context(nc.psum_tensor("pA", [128, 1024], F32))
        pB = st.enter_context(nc.psum_tensor("pB", [128, 1024], F32))
        pC = st.enter_context(nc.psum_tensor("pC", [128, 1024], F32))
        pD = st.enter_context(nc.psum_tensor("pD", [128, 512], F32))
        ptp = st.enter_context(nc.psum_tensor("ptp", [128, 1024], BF16))

        dX = P.sem("dX")
        dS = P.sem("dS")
        dQ = P.sem("dQ")
        ph = P.sem("ph")
        ve = P.sem("ve")
        ae = P.sem("ae")
        ge = P.sem("ge")
        te = P.sem("te")

        x_in_t = x_in.rearrange("(t p) d -> p t d", p=128)
        xs_t = xs.rearrange("(t p) d -> p t d", p=128)
        y_t = y_out.rearrange("(t p) d -> p t d", p=128)
        sub_src = [x_in_t, xs_t, xs_t, xs_t]

        slot_sems = {}

        def slot_load(name, engine, out, in_, waits=()):
            s = slot_sems.get(name)
            if s is None:
                s = P.sem("sl_" + name)
                slot_sems[name] = s
            P.emit(engine, "dma_start", out=out, in_=in_, waits=waits,
                   incs=[(s, 16)])
            return [(s, s.n)]

        def slot_w(name):
            s = slot_sems.get(name)
            return [(s, s.n)] if s is not None and s.n > 0 else []

        def barrier():
            ws = [(s, s.n) for s in P.all_sems if s is not ph and s.n > 0]
            P.emit("vector", "nop", waits=ws)
            for e in _ENGINES:
                P.emit(e, "nop", incs=[(ph, 1)])
            tgt = ph.n
            for e in _ENGINES:
                P.emit(e, "nop", waits=[(ph, tgt)])

        def mm_chain2(bankA, bankB, triples, extra_pe_waits=()):
            pe_wA, ev_wA = bankA.acquire()
            pe_wB, ev_wB = bankB.acquire()
            n = len(triples)
            for i, (lhsT, rhsA, rhsB) in enumerate(triples):
                w = (list(pe_wA) + list(pe_wB) + list(extra_pe_waits))                     if i == 0 else []
                P.emit("tensor", "matmul", bankA.ap, lhsT, rhsA,
                       start=(i == 0), stop=(i == n - 1),
                       skip_group_check=True, waits=w,
                       incs=[(bankA.prod, 1)] if i == n - 1 else [])
                P.emit("tensor", "matmul", bankB.ap, lhsT, rhsB,
                       start=(i == 0), stop=(i == n - 1),
                       skip_group_check=True,
                       incs=[(bankB.prod, 1)] if i == n - 1 else [])
            return ev_wA, ev_wB

        def mm_chain(bank, pairs, extra_pe_waits=(), out_ap=None):
            pe_w, ev_w = bank.acquire()
            out = bank.ap if out_ap is None else out_ap
            n = len(pairs)
            for i, (lhsT, rhs) in enumerate(pairs):
                w = (list(pe_w) + list(extra_pe_waits)) if i == 0 else []
                inc = [(bank.prod, 1)] if i == n - 1 else []
                P.emit("tensor", "matmul", out, lhsT, rhs,
                       start=(i == 0), stop=(i == n - 1),
                       skip_group_check=True, waits=w, incs=inc)
            return ev_w

        P.emit("vector", "memset", ones_rf[:], 1.0)
        P.emit("vector", "memset", ones_rb[:], 1.0)
        P.emit("vector", "memset", ones_cb[:], 1.0)
        P.emit("vector", "memset", cpool[:, 0:1], 0.125)
        P.emit("vector", "memset", cpool[:, 1:2], 1.0 / 32.0)
        P.emit("vector", "memset", cpool[:, 2:3], PI / 2)

        for dst, src in [(idb, c_idb), (idf, c_idf), (stair, c_stair),
                         (invwl, invwl_in), (bqc, b_bq), (bkc, b_bk),
                         (pbcn, b_pb), (bvrow, b_bv_row),
                         (s1c, f_s1), (s2c, f_s2), (sh1c, f_sh1),
                         (sh2c, f_sh2), (s3c, f_s3)]:
            cw = slot_load("const", "sync", dst[:], src)
        const_w = cw

        P.emit("vector", "tensor_tensor", c2q[:], bqc[:], invwl[:], ALU.mult,
               waits=const_w, incs=[(ve, 1)])
        P.emit("vector", "tensor_tensor", c2q[:], c2q[:], pbcn[:], ALU.add,
               waits=[(ve, ve.n)], incs=[(ve, 1)])
        P.emit("vector", "tensor_tensor", c2k[:], bkc[:], invwl[:], ALU.mult,
               incs=[(ve, 1)])
        P.emit("vector", "tensor_tensor", c2k[:], c2k[:], pbcn[:], ALU.add,
               waits=[(ve, ve.n)], incs=[(ve, 1)])
        P.emit("vector", "tensor_copy", bvrow_b[:], bvrow[:], incs=[(ve, 1)])

        cast_units = []
        for srcw, dstw in ((f_w1, wsc_w1), (f_w2, wsc_w2),
                           (f_h1, wsc_h1), (f_h2, wsc_h2),
                           (f_w3, wsc_w3)):
            for t_ in range(srcw.shape[0]):
                cast_units.append((srcw[t_], dstw[t_]))
        for srcw, dstw in ((bq_w, wsc_bq), (bk_w, wsc_bk), (bv_w, wsc_bv)):
            cast_units.append((srcw, dstw))
        cast_pos = [0]

        def emit_cast_units(n):
            for _ in range(n):
                i = cast_pos[0]
                if i >= len(cast_units):
                    return
                cast_pos[0] = i + 1
                sin_, dout = cast_units[i]
                slot_load("wcast", "gpsimd", dout, sin_)

        p0_done = []
        for t in range(NT):
            slot = xpanA[t % 2]
            wfree = list(p0_done[t - 2]) if t >= 2 else []
            lw = slot_load(f"xpA{t % 2}", "sync", slot[:, 0:D],
                           x_in_t[:, t, :], waits=wfree)
            emit_cast_units(3)
            P.emit("vector", "reduce_sum", sxs[:, t:t + 1], slot[:, 0:D],
                   AX.X, waits=lw, incs=[(ve, 1)])
            P.emit("scalar", "activation", dumpf[:], slot[:, 0:D], AF.Square,
                   accum_out=ssq[:, t:t + 1], waits=lw, incs=[(ae, 1)])
            p0_done.append([(ve, ve.n), (ae, ae.n)])
        barrier()

        def stats_derive(first):
            def vstep(method, *args, waits=(), **kw):
                P.emit("vector", method, *args,
                       waits=list(waits) + [(ve, ve.n)], incs=[(ve, 1)], **kw)
            if not first:
                vstep("tensor_tensor", sxs[:], sxh[:, 0:2 * NT:2],
                      sxh[:, 1:2 * NT:2], ALU.add)
            vstep("tensor_scalar", sxs[:], sxs[:], 1.0 / D, None, ALU.mult)
            vstep("tensor_tensor", stA[:], sxs[:], sxs[:], ALU.mult)
            vstep("scalar_tensor_tensor", stA[:], ssq[:], 1.0 / D,
                  stA[:], ALU.mult, ALU.subtract)
            vstep("tensor_scalar", stA[:], stA[:], EPS, None, ALU.add)
            P.emit("scalar", "activation", stA[:], stA[:], AF.Sqrt,
                   waits=[(ve, ve.n)], incs=[(ae, 1)])
            vstep("reciprocal", rstd[:], stA[:], waits=[(ae, ae.n)])
            vstep("tensor_tensor", nmr[:], sxs[:], rstd[:], ALU.mult)
            vstep("tensor_scalar", nmr[:], nmr[:], -1.0, None, ALU.mult)
            return ve.n

        def ln_tile(xsrc_ap, t, lnbuf, stat_w, x_wait, pt_bank):
            P.emit("scalar", "activation", lnbuf[:], xsrc_ap, AF.Identity,
                   bias=nmr[:, t:t + 1], scale=rstd[:, t:t + 1],
                   waits=[(ve, stat_w), (pt_bank.prod, pt_bank.uses)] + x_wait,
                   incs=[(ae, 1)])
            return ae.n

        def transpose_tile(lnbuf, dest_view, bank_pt, ae_wait, evac_extra):
            pe_w, ev_w = bank_pt.acquire()
            for dt in range(DT):
                w = (list(pe_w) + [(ae, ae_wait)]) if dt == 0 else []
                inc = [(bank_pt.prod, 1)] if dt == DT - 1 else []
                P.emit("tensor", "transpose", ptp[:, dt * 128:(dt + 1) * 128],
                       lnbuf[:, dt * 128:(dt + 1) * 128], idb[:],
                       waits=w, incs=inc)
            P.emit("vector", "tensor_copy", dest_view,
                   ptp[:].rearrange("p (a b) -> p a b", a=DT),
                   waits=list(ev_w) + list(evac_extra),
                   incs=[(bank_pt.cons, 1), (ve, 1)])
            return ve.n

        def residual_store(k, t, halves, xold_slot, xout_buf, x_wait, dump):
            last = (k == n_sub - 1)
            for j, (pap, w, extra_incs, scal) in enumerate(halves):
                incs = [(ve, 1)] + list(extra_incs)
                acc = {} if last else {
                    "accum_out": sxh[:, 2 * t + j:2 * t + j + 1]}
                war = [(dS, dS.n)] if j == 0 else [(ve, ve.n)]
                P.emit("vector", "scalar_tensor_tensor",
                       xout_buf[:, j * 512:(j + 1) * 512], pap, scal,
                       xold_slot[:, j * 512:(j + 1) * 512], ALU.mult, ALU.add,
                       waits=list(w) + list(x_wait) + war, incs=incs, **acc)
            vdone = ve.n
            aw_ = []
            if not last:
                P.emit("scalar", "activation", dump[:], xout_buf[:], AF.Square,
                       accum_out=ssq[:, t:t + 1], waits=[(ve, vdone)],
                       incs=[(ae, 1)])
                aw_ = [(ae, ae.n)]
            dst = y_t if last else xs_t
            P.emit("sync", "dma_start", out=dst[:, t, :], in_=xout_buf[:],
                   waits=[(ve, vdone)] + aw_, incs=[(dS, 16)])

        DUMP = _CONFIG.get("dump")

        def dumpb(col, ap, waits):
            P.emit("sync", "dma_start", out=dbgb[:, col:col + ap.shape[-1]],
                   in_=ap, waits=waits, incs=[(dS, 16)])

        def phase_attn_a(k):
            xsrc = sub_src[k]
            stat_w = stats_derive(first=(k == 0))
            if DUMP:
                P.emit("sync", "dma_start", out=dbgf[:, 0:32], in_=rstd[:],
                       waits=[(ve, stat_w)], incs=[(dS, 16)])
                P.emit("sync", "dma_start", out=dbgf[:, 32:64], in_=nmr[:],
                       waits=[(ve, stat_w)], incs=[(dS, 16)])
                P.emit("sync", "dma_start", out=dbgf[:, 64:96], in_=sxs[:],
                       waits=[(ve, stat_w)], incs=[(dS, 16)])
                P.emit("sync", "dma_start", out=dbgf[:, 96:128], in_=ssq[:],
                       waits=[(ve, stat_w)], incs=[(dS, 16)])
            bA0 = Bank(P, pA[:, 0:512], "pA0")
            bA1 = Bank(P, pA[:, 512:1024], "pA1")
            bPT = Bank(P, ptp[:], "pt")
            P.emit("vector", "memset", vres[:, 64:NT * H * VG:VG], 1.0)
            P.emit("vector", "memset", vres[:, 65:NT * H * VG:VG], 0.0)
            P.emit("vector", "memset",
                   vres[:, NT * H * VG:NT * H * VG + 128], 0.0,
                   incs=[(ve, 1)])
            vones_done = ve.n

            aw_sem = P.sem("aw_sem")
            aw_free = P.sem("aw_free")
            qdma = P.sem("qdma")
            lnT_free_rec = 0
            for p in range(8):
                for tl in range(4):
                    t = 4 * p + tl
                    xp = xpanA[t % 2]
                    xw = [(ae, ae.n)] if t >= 2 else []
                    lw = slot_load(f"xpA{t % 2}", "sync", xp[:],
                                   xsrc[:, t, :], waits=xw)
                    ae_n = ln_tile(xp[:], t, lnb, stat_w, lw, bPT)
                    dest = lnT_a[:].rearrange("p (a s) -> p a s", a=DT)[
                        :, :, tl * 128:(tl + 1) * 128]
                    transpose_tile(lnb, dest, bPT, ae_n,
                                   [(te, lnT_free_rec)] if p > 0 else [])
                lnT_done = ve.n
                for wi, wdram in enumerate((aq_w, ak_w, av_w)):
                    nw = 3 * p + wi
                    P.emit("gpsimd", "dma_start", out=aw[:], in_=wdram,
                           waits=[(aw_free, nw)] if nw > 0 else [],
                           incs=[(aw_sem, 16)])
                    wld = aw_sem.n
                    if wi < 2:
                        for et in range(DT):
                            bk = bA0 if et % 2 == 0 else bA1
                            pairs = [
                                (aw[:, it * D + et * 128: it * D + et * 128 + 128],
                                 lnT_a[:, it * 512:(it + 1) * 512])
                                for it in range(DT)]
                            evw = mm_chain(bk, pairs,
                                           extra_pe_waits=[(aw_sem, wld),
                                                           (ve, lnT_done)])
                            if wi == 0:
                                P.emit("vector", "tensor_copy",
                                       qtst[:, et * 512:(et + 1) * 512],
                                       bk.ap,
                                       waits=list(evw) + [(qdma, qdma.n)],
                                       incs=[(bk.cons, 1), (ve, 1)])
                            else:
                                P.emit("vector", "tensor_copy",
                                       ktr[:, et * S + p * 512:
                                           et * S + (p + 1) * 512], bk.ap,
                                       waits=evw,
                                       incs=[(bk.cons, 1), (ve, 1)])
                        if wi == 0:
                            P.emit("gpsimd", "dma_start",
                                   out=qt_s.rearrange("a p s -> p a s")[
                                       :, :, p * 512:(p + 1) * 512],
                                   in_=qtst[:].rearrange("p (a s) -> p a s",
                                                         a=DT),
                                   waits=[(ve, ve.n)], incs=[(qdma, 16)])
                        P.emit("tensor", "nop",
                               waits=[(bA0.prod, bA0.uses),
                                      (bA1.prod, bA1.uses)],
                               incs=[(aw_free, 1)])
                        emit_cast_units(2)
                    else:
                        for tl in range(4):
                            t = 4 * p + tl
                            for eb in range(2):
                                bk = bA0 if (2 * tl + eb) % 2 == 0 else bA1
                                pairs = [
                                    (lnT_a[:, it * 512 + tl * 128:
                                           it * 512 + tl * 128 + 128],
                                     aw[:, it * D + eb * 512:
                                        it * D + (eb + 1) * 512])
                                    for it in range(DT)]
                                evw = mm_chain(bk, pairs,
                                               extra_pe_waits=[(aw_sem, wld),
                                                               (ve, lnT_done)])
                                dv = vres[:, 0:NT * H * VG].rearrange(
                                    "p (kt h g) -> p kt h g", kt=NT, h=H)[
                                    :, t, eb * 8:(eb + 1) * 8, 0:64]
                                P.emit("vector", "tensor_copy", dv,
                                       bk.ap.rearrange("p (h r) -> p h r", h=8),
                                       waits=list(evw) + [(ve, vones_done)],
                                       incs=[(bk.cons, 1), (ve, 1)])
                        P.emit("tensor", "nop",
                               waits=[(bA0.prod, bA0.uses),
                                      (bA1.prod, bA1.uses)],
                               incs=[(aw_free, 1), (te, 1)])
                        lnT_free_rec = te.n
            p1a_v = ve.n
            p1a_q = qdma.n
            if DUMP:
                w = [(ve, p1a_v)]
                dumpb(0, lnT_a[:], w)
                dumpb(4096, ktr[:, 0:512], w)
                dumpb(4608, vres[:, 0:65], w)
                dumpb(5664, qtst[:], w)
            if _CONFIG.get("dbg") == "1a":
                P.emit("vector", "nop", waits=[(ve, p1a_v)])
                P.emit("scalar", "nop", waits=[(qdma, p1a_q)])
                barrier()
                return

            P.emit("gpsimd", "dma_start", out=aw[:], in_=ao_w,
                   waits=[(aw_free, aw_free.n)], incs=[(aw_sem, 16)])
            ao_ld = aw_sem.n
            bSC = [Bank(P, pA[:], "pAf"), Bank(P, pB[:], "pBf")]
            bOp = [Bank(P, pC[:, 0:512], "pC0"),
                   Bank(P, pC[:, 512:1024], "pC1")]
            bBC = Bank(P, pD[0:64, :], "pD")
            bM = Bank(P, pD[:], "pD")
            pr_sem = P.sem("a_pr")
            pf_sem = P.sem("a_pf")
            pr_uses = 0
            panel_rec = []
            dbg = _CONFIG.get("dbg") or ""
            base = dbg.split("_")[0]
            np_1b = int(base[2:]) if base.startswith("1b") and len(base) > 2 else 8
            for p in range(np_1b):
                qp = qtp[p % 2]
                qw = [(te, panel_rec[p - 2])] if p >= 2 else []
                qldw = slot_load(
                    f"qtp{p % 2}", "sync",
                    qp[:].rearrange("p (a s) -> p a s", a=DT),
                    qt_s.rearrange("a p s -> p a s")[
                        :, :, p * 512:(p + 1) * 512],
                    waits=qw + ([(qdma, p1a_q), (ve, p1a_v)]
                                if p == 0 else []))
                emit_cast_units(12 if p > 0 else 0)
                nkk = 4 * (p + 1)
                for h in range(H):
                    et = h // 2
                    ro = (h % 2) * 64
                    bO = bOp[h % 2]
                    o_pe_w, o_ev_w = bO.acquire()
                    for j in range(nkk // 2):
                        kk0, kk1 = 2 * j, 2 * j + 1
                        scb = bSC[j % 2]
                        pe_w, ev_w = scb.acquire()
                        for idx, kk in enumerate((kk0, kk1)):
                            lhsT = ktr[ro:ro + 64,
                                       et * S + kk * 128: et * S + kk * 128 + 128]
                            rhs = qp[ro:ro + 64, et * 512:(et + 1) * 512]
                            w = (list(pe_w) + qldw + [(aw_sem, ao_ld)]
                                 ) if idx == 0 else []
                            P.emit("tensor", "matmul",
                                   scb.ap[:, idx * 512:(idx + 1) * 512],
                                   lhsT, rhs, start=True, stop=True,
                                   skip_group_check=True, waits=w,
                                   incs=[(scb.prod, 1)] if idx == 1 else [])
                        slot = pring[pr_uses % 3]
                        pw = [(pf_sem, pr_uses - 2)] if pr_uses >= 3 else []
                        diag = (kk1 >= 4 * p)
                        P.emit("scalar", "activation", slot[:], scb.ap, AF.Exp,
                               scale=cpool[:, C_EXPA:C_EXPA + 1],
                               waits=list(ev_w) + pw,
                               incs=([(scb.cons, 1), (ae, 1)] if diag else
                                     [(scb.cons, 1), (pr_sem, 1)]))
                        if diag:
                            expd = ae.n
                            first = True
                            for idx, kk in enumerate((kk0, kk1)):
                                if kk < 4 * p:
                                    continue
                                off = (kk - 4 * p) * 128
                                base = idx * 512
                                if off > 0:
                                    P.emit("vector", "memset",
                                           slot[:, base:base + off], 0.0,
                                           waits=[(ae, expd)] if first else [])
                                    first = False
                                P.emit("vector", "tensor_tensor",
                                       slot[:, base + off:base + off + 128],
                                       slot[:, base + off:base + off + 128],
                                       stair[:], ALU.mult,
                                       waits=[(ae, expd)] if first else [],
                                       incs=[(ve, 1)])
                                first = False
                            P.emit("vector", "nop", waits=[(ve, ve.n)],
                                   incs=[(pr_sem, 1)])
                        pr_uses += 1
                        if dbg.endswith("_sc"):
                            P.emit("vector", "nop", waits=[(pr_sem, pr_uses)],
                                   incs=[(pf_sem, 1)])
                            continue
                        for idx, kk in enumerate((kk0, kk1)):
                            vap = vres[:, kk * (H * VG) + h * VG:
                                       kk * (H * VG) + h * VG + 128]
                            w = [(pr_sem, pr_uses)]
                            if j == 0 and idx == 0:
                                w = w + o_pe_w
                            last = (j == nkk // 2 - 1 and idx == 1)
                            inc = []
                            if last:
                                inc.append((bO.prod, 1))
                            if idx == 1:
                                inc.append((pf_sem, 1))
                            P.emit("tensor", "matmul", bO.ap[0:128, :], vap,
                                   slot[:, idx * 512:(idx + 1) * 512],
                                   start=(j == 0 and idx == 0), stop=last,
                                   skip_group_check=True, waits=w, incs=inc)
                    if dbg.endswith("_sc"):
                        continue
                    if dbg.endswith("_av"):
                        P.emit("vector", "tensor_copy", bcs[:],
                               bO.ap[0:64, :], waits=o_ev_w,
                               incs=[(bO.cons, 1)])
                        continue
                    P.emit("vector", "reciprocal", rcp[:], bO.ap[64:65, :],
                           waits=list(o_ev_w) + [(bBC.prod, bBC.uses)],
                           incs=[(ve, 1)])
                    rcp_v = ve.n
                    bc_pe_w, bc_ev_w = bBC.acquire()
                    P.emit("tensor", "matmul", bBC.ap[0:64, :], ones_rf[:],
                           rcp[:], start=True, stop=True,
                           skip_group_check=True,
                           waits=list(bc_pe_w) + [(ve, rcp_v)],
                           incs=[(bBC.prod, 1)])
                    P.emit("vector", "tensor_copy", bcs[:], bBC.ap[0:64, :],
                           waits=list(bc_ev_w) + [(ve, ve.n)],
                           incs=[(bBC.cons, 1), (ve, 1)])
                    aoT_w = [(te, panel_rec[p - 1])] if p >= 1 else []
                    P.emit("vector", "tensor_tensor",
                           aoT[ro:ro + 64, et * 512:(et + 1) * 512],
                           bO.ap[0:64, :], bcs[:], ALU.mult,
                           waits=aoT_w + [(ve, ve.n)],
                           incs=[(bO.cons, 1), (ve, 1)])
                aoT_done = ve.n
                if dbg.endswith(("_sc", "_av", "_nr")):
                    P.emit("tensor", "nop",
                           waits=[(bSC[0].prod, bSC[0].uses),
                                  (bSC[1].prod, bSC[1].uses),
                                  (bO.prod, bO.uses)],
                           incs=[(te, 1)])
                    panel_rec.append(te.n)
                    continue
                for qs in range(4):
                    t = 4 * p + qs
                    xslot = xq[qs % 2]
                    xw = [(ve, ve.n)] if (p * 4 + qs) >= 2 else []
                    lw = slot_load(f"xq{qs % 2}", "sync", xslot[:],
                                   xsrc[:, t, :], waits=xw)
                    halves = []
                    for mb in range(2):
                        pairs = [
                            (aoT[:, vt * 512 + qs * 128:
                                 vt * 512 + qs * 128 + 128],
                             aw[:, vt * D + mb * 512: vt * D + (mb + 1) * 512])
                            for vt in range(DT)]
                        evw = mm_chain(bM, pairs,
                                       extra_pe_waits=[(ve, aoT_done)])
                        halves.append((bM.ap, evw, [(bM.cons, 1)], 1.0))
                    residual_store(k, t, halves, xslot, xo, lw, dumpf)
                P.emit("tensor", "nop",
                       waits=[(bM.prod, bM.uses)], incs=[(te, 1)])
                panel_rec.append(te.n)
            emit_cast_units(10 ** 9)
            barrier()

        def phase_ffn(k):
            xsrc = sub_src[k]
            stat_w = stats_derive(first=False)
            bPT = Bank(P, ptp[:], "pt")
            b1 = [Bank(P, pA[:, 0:512], "pA0"),
                  Bank(P, pA[:, 512:1024], "pA1")]
            b2 = [Bank(P, pB[:, 0:512], "pB0"),
                  Bank(P, pB[:, 512:1024], "pB1")]
            bTD = Bank(P, pD[:], "pD")
            bTC = Bank(P, pC[:, 0:512], "pC0")
            if "f_wf" not in slot_sems:
                slot_sems["f_wf"] = P.sem("f_wf")
                slot_sems["f_w12f"] = P.sem("f_w12f")
            wfree = slot_sems["f_wf"]
            w12free = slot_sems["f_w12f"]
            if not hasattr(P, "wload_n"):
                P.wload_n = [0]
                P.w12load_n = [0]
            wload_n = P.wload_n
            w12load_n = P.w12load_n
            rec_ln = 0
            rec_tr = 0
            dbgv = _CONFIG.get("dbg") or ""
            n_pan = 4
            stop_stage = None
            if dbgv.startswith("fp"):
                parts = dbgv[2:].split("_")
                n_pan = int(parts[0])
                if len(parts) > 1:
                    stop_stage = int(parts[1][1:])
            for p in range(n_pan):
                for tl in range(8):
                    t = 8 * p + tl
                    xslot = xtf[tl % 2]
                    xw = [(ve, ve.n)] if (t >= 2 or p > 0) else []
                    lw = slot_load(f"xtf{tl % 2}", "sync", xslot[:],
                                   xsrc[:, t, :], waits=xw)
                    ae_n = ln_tile(xslot[:], t, lnbf, stat_w, lw, bPT)
                    dest = lnT_f[:].rearrange("p (a s) -> p a s", a=DT)[
                        :, :, tl * 128:(tl + 1) * 128]
                    transpose_tile(lnbf, dest, bPT, ae_n,
                                   [(te, rec_tr)] if p > 0 else [])
                lnT_done = ve.n
                for ot in range(HT):
                    wldw = []
                    for wi, wdram in enumerate((wsc_w1, wsc_w2)):
                        i = w12load_n[0]
                        w12load_n[0] += 1
                        slot = w12r[i % 4]
                        wfw = [(w12free, i - 3)] if i >= 4 else []
                        wldw += slot_load(f"w12_{i % 4}", "gpsimd", slot[:],
                                          wdram[ot], waits=wfw)
                    s1b = w12r[(w12load_n[0] - 2) % 4]
                    s2b = w12r[(w12load_n[0] - 1) % 4]
                    evws = []
                    for wb, banks in ((s1b, b1), (s2b, b2)):
                        triples = [
                            (wb[:, it * 128:(it + 1) * 128],
                             lnT_f[:, it * 1024: it * 1024 + 512],
                             lnT_f[:, it * 1024 + 512: it * 1024 + 1024])
                            for it in range(DT)]
                        ev0, ev1 = mm_chain2(banks[0], banks[1], triples,
                                             extra_pe_waits=wldw +
                                             [(ve, lnT_done)])
                        evws.append((banks, ev0, ev1))
                    (bk1s, evw1a, evw1b), (bk2s, evw2a, evw2b) = evws
                    evws = [(bk1s[0], evw1a, bk2s[0], evw2a),
                            (bk1s[1], evw1b, bk2s[1], evw2b)]
                    P.emit("tensor", "nop",
                           waits=[(b1[0].prod, b1[0].uses),
                                  (b1[1].prod, b1[1].uses),
                                  (b2[0].prod, b2[0].uses),
                                  (b2[1].prod, b2[1].uses)],
                           incs=[(w12free, 2)])
                    sl = silub[:, (ot % 2) * 1024:(ot % 2) * 1024 + 1024]
                    for sb_i, (bk1, evw1, bk2, evw2) in enumerate(evws):
                        P.emit("scalar", "activation",
                               sl[:, sb_i * 512:(sb_i + 1) * 512], bk1.ap,
                               AF.Silu, scale=s1c[:, ot:ot + 1],
                               waits=list(evw1) + [(ve, ve.n)],
                               incs=[(bk1.cons, 1), (ae, 1)])
                        P.emit("vector", "scalar_tensor_tensor",
                               aT[:, ot * 1024 + sb_i * 512:
                                  ot * 1024 + (sb_i + 1) * 512],
                               bk2.ap, s2c[:, ot:ot + 1],
                               sl[:, sb_i * 512:(sb_i + 1) * 512],
                               ALU.mult, ALU.mult,
                               waits=list(evw2) + [(ae, ae.n)],
                               incs=[(bk2.cons, 1), (ve, 1)])
                P.emit("tensor", "nop",
                       waits=[(b1[0].prod, b1[0].uses),
                              (b1[1].prod, b1[1].uses),
                              (b2[0].prod, b2[0].uses),
                              (b2[1].prod, b2[1].uses)],
                       incs=[(te, 1)])
                rec_ln = te.n
                if _CONFIG.get("dbg") == "f1":
                    P.emit("vector", "nop", waits=[(ve, ve.n), (ae, ae.n)])
                    barrier()
                    return
                prev_done = ("ve", ve.n)
                if _CONFIG.get("dump") == "a2" and p == n_pan - 1:
                    dumpb(0, aT[:, 0:16384], [(ve, ve.n)])
                for si, (wdram, weng, src, dst, scl) in enumerate(
                        [(wsc_h1, "gpsimd", aT, bT, sh1c),
                         (wsc_h2, "gpsimd", bT, aT, sh2c)]):
                    for ot in range(HT):
                        slot = wring[ot % 2]
                        uses = wload_n[0]
                        wload_n[0] += 1
                        wfw = [(wfree, uses - 1)] if uses >= 2 else []
                        wldw = slot_load(f"wr{ot % 2}_{weng}", weng, slot[:],
                                         wdram[ot], waits=wfw)
                        banks = b1 if ot % 2 == 0 else b2
                        pd = [(ve, prev_done[1])] if prev_done[0] == "ve" \
                            else [(ae, prev_done[1])]
                        triples = [
                            (slot[:, it * 128:(it + 1) * 128],
                             src[:, it * 1024: it * 1024 + 512],
                             src[:, it * 1024 + 512: it * 1024 + 1024])
                            for it in range(HT)]
                        ev0, ev1 = mm_chain2(banks[0], banks[1], triples,
                                             extra_pe_waits=wldw + pd)
                        for sb_i, evw in ((0, ev0), (1, ev1)):
                            bk = banks[sb_i]
                            P.emit("scalar", "activation",
                                   dst[:, ot * 1024 + sb_i * 512:
                                       ot * 1024 + (sb_i + 1) * 512],
                                   bk.ap, AF.Gelu, scale=scl[:, ot:ot + 1],
                                   waits=evw, incs=[(bk.cons, 1), (ae, 1)])
                        P.emit("tensor", "nop",
                               waits=[(banks[0].prod, banks[0].uses),
                                      (banks[1].prod, banks[1].uses)],
                               incs=[(wfree, 1)])
                    if stop_stage == 2 + si and p == n_pan - 1:
                        P.emit("vector", "nop", waits=[(ve, ve.n), (ae, ae.n)])
                        barrier()
                        return
                    prev_done = ("ae", ae.n)
                s4banks = [b1[0], b1[1], b2[0], b2[1]]
                for sh in range(2):
                    for ot in range(DT):
                        slot = wring[ot % 2]
                        uses = wload_n[0]
                        wload_n[0] += 1
                        wfw = [(wfree, uses - 1)] if uses >= 2 else []
                        wldw = slot_load(f"wr{ot % 2}_gpsimd", "gpsimd",
                                         slot[:], wsc_w3[ot], waits=wfw)
                        bk = s4banks[ot % 4]
                        pairs = [
                            (slot[:, it * 128:(it + 1) * 128],
                             aT[:, it * 1024 + sh * 512:
                                it * 1024 + sh * 512 + 512])
                            for it in range(HT)]
                        evw = mm_chain(bk, pairs,
                                       extra_pe_waits=wldw +
                                       [(ae, prev_done[1])])
                        war = [(te, rec_tr)] if (p > 0 or sh > 0) else []
                        P.emit("vector", "tensor_scalar",
                               w3st[:, ot * 512:(ot + 1) * 512],
                               bk.ap, s3c[:, ot:ot + 1], None, ALU.mult,
                               waits=list(evw) + war,
                               incs=[(bk.cons, 1), (ve, 1)])
                        P.emit("tensor", "nop",
                               waits=[(bk.prod, bk.uses)],
                               incs=[(wfree, 1)])
                    w3st_done = ve.n
                    for tlh in range(4):
                        tl = sh * 4 + tlh
                        t = 8 * p + tl
                        xslot = xtf[tl % 2]
                        lw = slot_load(f"xtf{tl % 2}", "sync", xslot[:],
                                       xsrc[:, t, :], waits=[(ve, ve.n)])
                        halves = []
                        for half, bk in ((0, bTD), (1, bTC)):
                            pe_w, ev_w = bk.acquire()
                            for q in range(4):
                                dt = half * 4 + q
                                w = (list(pe_w) + [(ve, w3st_done)]) \
                                    if q == 0 else []
                                P.emit("tensor", "transpose",
                                       bk.ap[:, q * 128:(q + 1) * 128],
                                       w3st[:, dt * 512 + tlh * 128:
                                            dt * 512 + tlh * 128 + 128],
                                       idf[:], waits=w,
                                       incs=[(bk.prod, 1)] if q == 3 else [])
                            halves.append((bk.ap, ev_w, [(bk.cons, 1)], 1.0))
                        residual_store(k, t, halves, xslot, xof, lw,
                                       dumpf2)
                    P.emit("tensor", "nop",
                           waits=[(bTD.prod, bTD.uses),
                                  (bTC.prod, bTC.uses)],
                           incs=[(te, 1)])
                    rec_tr = te.n
            barrier()

        def phase_attn_b(k):
            xsrc = sub_src[k]
            stat_w = stats_derive(first=False)
            bPT = Bank(P, ptp[:], "pt")
            bA0 = Bank(P, pA[:, 0:512], "pA0")
            bA1 = Bank(P, pA[:, 512:1024], "pA1")
            bw_sem = P.sem("bw_sem")
            bw_free = P.sem("bw_free")
            sdma = P.sem("b_sdma")
            rec_ln3 = 0

            def trig(bk, evw, et, c2, sin_dst, cos_dst,
                     sin_war=(), cos_war=()):
                P.emit("vector", "tensor_scalar", theta[:], bk.ap,
                       invwl[:, et:et + 1], c2[:, et:et + 1],
                       ALU.mult, ALU.add,
                       waits=list(evw) + [(ge, ge.n), (ve, ve.n)],
                       incs=[(bk.cons, 1), (ve, 1)])
                th_v = ve.n
                for is_cos, dstap in ((0, sin_dst), (1, cos_dst)):
                    if is_cos:
                        P.emit("vector", "tensor_scalar", ubuf[:], theta[:],
                               INV2PI, 0.25, ALU.mult, ALU.add,
                               waits=[(ge, ge.n), (ve, th_v)],
                               incs=[(ve, 1)])
                    else:
                        P.emit("vector", "tensor_scalar", ubuf[:], theta[:],
                               INV2PI, None, ALU.mult,
                               waits=[(ge, ge.n), (ve, th_v)],
                               incs=[(ve, 1)])
                    uv = ve.n
                    P.emit("vector", "tensor_copy", kint[:], ubuf[:],
                           waits=[(ve, uv)], incs=[(ve, 1)])
                    P.emit("vector", "tensor_copy", kflt[:], kint[:],
                           waits=[(ve, ve.n)], incs=[(ve, 1)])
                    P.emit("vector", "scalar_tensor_tensor", rbuf[:], kflt[:],
                           -TWO_PI, theta[:], ALU.mult, ALU.add,
                           waits=[(ve, ve.n), (ae, ae.n)], incs=[(ve, 1)])
                    hi = (PI / 2) if is_cos else PI
                    lo = (-1.5 * PI) if is_cos else -PI
                    P.emit("vector", "tensor_scalar", ubuf[:], rbuf[:],
                           float(np.float32(hi)), None, ALU.is_gt,
                           waits=[(ve, ve.n), (ge, ge.n)], incs=[(ve, 1)])
                    P.emit("vector", "scalar_tensor_tensor", kflt[:], ubuf[:],
                           -TWO_PI, rbuf[:], ALU.mult, ALU.add,
                           waits=[(ve, ve.n)], incs=[(ve, 1)])
                    P.emit("vector", "tensor_scalar", ubuf[:], kflt[:],
                           float(np.float32(lo)), None, ALU.is_lt,
                           waits=[(ve, ve.n)], incs=[(ve, 1)])
                    P.emit("vector", "scalar_tensor_tensor", rbuf[:], ubuf[:],
                           TWO_PI, kflt[:], ALU.mult, ALU.add,
                           waits=[(ve, ve.n)], incs=[(ve, 1)])
                    rv = ve.n
                    kwargs = {}
                    war = list(cos_war if is_cos else sin_war)
                    if is_cos:
                        kwargs["bias"] = cpool[:, C_HALFPI:C_HALFPI + 1]
                    P.emit("scalar", "activation", dstap, rbuf[:], AF.Sin,
                           waits=[(ve, rv)] + war,
                           incs=[(ae, 1)], **kwargs)
                return ae.n

            for p in range(8):
                for tl in range(4):
                    t = 4 * p + tl
                    xp = xpanB[t % 2]
                    xw = [(ae, ae.n)] if t >= 2 else []
                    lw = slot_load(f"xpB{t % 2}", "sync", xp[:],
                                   xsrc[:, t, :], waits=xw)
                    ae_n = ln_tile(xp[:], t, lnb3, stat_w, lw, bPT)
                    dest = lnT_b[:].rearrange("p (a s) -> p a s", a=DT)[
                        :, :, tl * 128:(tl + 1) * 128]
                    transpose_tile(lnb3, dest, bPT, ae_n,
                                   [(te, rec_ln3)] if p > 0 else [])
                lnT_done = ve.n
                for wi, wdram in enumerate((wsc_bq, wsc_bk, wsc_bv)):
                    nw = 3 * p + wi
                    P.emit("gpsimd", "dma_start", out=bw[:], in_=wdram,
                           waits=[(bw_free, nw)] if nw > 0 else [],
                           incs=[(bw_sem, 16)])
                    wld = bw_sem.n
                    if wi < 2:
                        c2 = c2q if wi == 0 else c2k
                        for et in range(DT):
                            bk = bA0 if et % 2 == 0 else bA1
                            pairs = [
                                (bw[:, it * D + et * 128:
                                    it * D + et * 128 + 128],
                                 lnT_b[:, it * 512:(it + 1) * 512])
                                for it in range(DT)]
                            evw = mm_chain(bk, pairs,
                                           extra_pe_waits=[(bw_sem, wld),
                                                           (ve, lnT_done)])
                            if wi == 0:
                                sin_dst = trst[1][:, et * 512:(et + 1) * 512]
                                cos_dst = trst[0][:, et * 512:(et + 1) * 512]
                                sw, cw2 = slot_w("st_sq"), slot_w("st_cq")
                            elif p < 4:
                                sin_dst = sk1[:, et * 2048 + p * 512:
                                              et * 2048 + (p + 1) * 512]
                                cos_dst = ck1[:, et * 2048 + p * 512:
                                              et * 2048 + (p + 1) * 512]
                                sw, cw2 = [], []
                            else:
                                sin_dst = trst[3][:, et * 512:(et + 1) * 512]
                                cos_dst = trst[2][:, et * 512:(et + 1) * 512]
                                sw, cw2 = slot_w("st_sk"), slot_w("st_ck")
                            trig(bk, evw, et, c2, sin_dst, cos_dst,
                                 sin_war=sw, cos_war=cw2)
                        P.emit("tensor", "nop",
                               waits=[(bA0.prod, bA0.uses),
                                      (bA1.prod, bA1.uses)],
                               incs=[(bw_free, 1)])
                        if wi == 0:
                            slot_load("st_cq", "scalar",
                                      cq_s.rearrange("a p s -> p a s")[
                                          :, :, p * 512:(p + 1) * 512],
                                      trst[0][:].rearrange(
                                          "p (a s) -> p a s", a=DT),
                                      waits=[(ae, ae.n)])
                            slot_load("st_sq", "scalar",
                                      sq_s.rearrange("a p s -> p a s")[
                                          :, :, p * 512:(p + 1) * 512],
                                      trst[1][:].rearrange(
                                          "p (a s) -> p a s", a=DT),
                                      waits=[(ae, ae.n)])
                        elif p >= 4:
                            slot_load("st_ck", "scalar",
                                      ck2_s.rearrange("a p s -> p a s")[
                                          :, :, (p - 4) * 512:(p - 3) * 512],
                                      trst[2][:].rearrange(
                                          "p (a s) -> p a s", a=DT),
                                      waits=[(ae, ae.n)])
                            slot_load("st_sk", "scalar",
                                      sk2_s.rearrange("a p s -> p a s")[
                                          :, :, (p - 4) * 512:(p - 3) * 512],
                                      trst[3][:].rearrange(
                                          "p (a s) -> p a s", a=DT),
                                      waits=[(ae, ae.n)])
                    else:
                        for tl in range(4):
                            t = 4 * p + tl
                            for eb in range(2):
                                bk = bA0 if (2 * tl + eb) % 2 == 0 else bA1
                                pe_w, evw0 = bk.acquire()
                                pairs = [
                                    (lnT_b[:, it * 512 + tl * 128:
                                           it * 512 + tl * 128 + 128],
                                     bw[:, it * D + eb * 512:
                                        it * D + (eb + 1) * 512])
                                    for it in range(DT)]
                                for i, (lh, rh) in enumerate(pairs):
                                    w = (list(pe_w) + [(bw_sem, wld),
                                                       (ve, lnT_done)]
                                         ) if i == 0 else []
                                    P.emit("tensor", "matmul", bk.ap, lh, rh,
                                           start=(i == 0), stop=False,
                                           skip_group_check=True, waits=w)
                                P.emit("tensor", "matmul", bk.ap, ones_rb[:],
                                       bvrow_b[:, eb * 512:(eb + 1) * 512],
                                       start=False, stop=True,
                                       skip_group_check=True,
                                       incs=[(bk.prod, 1)])
                                if p < 4:
                                    dstv = v1r[:, t * D + eb * 512:
                                               t * D + (eb + 1) * 512]
                                    P.emit("vector", "tensor_copy", dstv,
                                           bk.ap, waits=evw0,
                                           incs=[(bk.cons, 1), (ve, 1)])
                                else:
                                    dstv = vst[tl % 2][:, eb * 512:
                                                       (eb + 1) * 512]
                                    P.emit("vector", "tensor_copy", dstv,
                                           bk.ap,
                                           waits=list(evw0) +
                                           slot_w(f"st_v{tl % 2}"),
                                           incs=[(bk.cons, 1), (ve, 1)])
                            if p >= 4:
                                slot_load(f"st_v{tl % 2}", "sync",
                                          v2_s[t - 16], vst[tl % 2][:],
                                          waits=[(ve, ve.n)])
                        P.emit("tensor", "nop",
                               waits=[(bA0.prod, bA0.uses),
                                      (bA1.prod, bA1.uses)],
                               incs=[(bw_free, 1), (te, 1)])
                        rec_ln3 = te.n
            barrier()

            bSC = [Bank(P, pA[:, 0:512], "pA0"),
                   Bank(P, pA[:, 512:1024], "pA1")]
            bAV = [Bank(P, pB[:, 0:512], "pB0"),
                   Bank(P, pB[:, 512:1024], "pB1"),
                   Bank(P, pC[:, 0:512], "pC0"),
                   Bank(P, pC[:, 512:1024], "pC1")]
            bDEN = Bank(P, pD[:, 0:4], "pD4")
            psem = P.sem("b_ps")
            pfree = P.sem("b_pf")
            rec3b = []
            po_store = [0] * 8
            for kh in range(2):
                kldw = []
                if kh == 1:
                    kw = [(te, rec3b[-1])]
                    kldw += slot_load(
                        "kck", "sync",
                        ck1[:].rearrange("p (a s) -> p a s", a=DT),
                        ck2_s.rearrange("a p s -> p a s"), waits=kw)
                    kldw += slot_load(
                        "ksk", "sync",
                        sk1[:].rearrange("p (a s) -> p a s", a=DT),
                        sk2_s.rearrange("a p s -> p a s"), waits=kw)
                    kldw += slot_load(
                        "kv1", "sync",
                        v1r[:].rearrange("p (t d) -> p t d", t=NT // 2),
                        v2_s.rearrange("t p d -> p t d"), waits=kw)
                for p in range(8):
                    idx = kh * 8 + p
                    cqs = cqp[idx % 2]
                    sqs = sqp[idx % 2]
                    qw = [(te, rec3b[idx - 2])] if idx >= 2 else []
                    if idx == 0:
                        qw = qw + slot_w("st_cq") + slot_w("st_sq")
                    qldw = slot_load(
                        f"cq{idx % 2}", "sync",
                        cqs[:].rearrange("p (a s) -> p a s", a=DT),
                        cq_s.rearrange("a p s -> p a s")[
                            :, :, p * 512:(p + 1) * 512], waits=qw)
                    qldw += slot_load(
                        f"sq{idx % 2}", "sync",
                        sqs[:].rearrange("p (a s) -> p a s", a=DT),
                        sq_s.rearrange("a p s -> p a s")[
                            :, :, p * 512:(p + 1) * 512], waits=qw)
                    poldw = []
                    if kh == 1:
                        poldw = slot_load(
                            "pin", "sync",
                            poin[:].rearrange("p (q d) -> p q d", q=4),
                            po_s[p].rearrange("p (q d) -> p q d", q=4),
                            waits=[(ve, ve.n)] + slot_w("pst"))
                        poldw += slot_load("din", "sync", denin[:],
                                           pd_s[p], waits=slot_w("pdst"))
                    d_pe_w, d_ev_w = bDEN.acquire()
                    for kk in range(16):
                        scb = bSC[kk % 2]
                        pairs = []
                        for et in range(DT):
                            pairs.append(
                                (ck1[:, et * 2048 + kk * 128:
                                     et * 2048 + kk * 128 + 128],
                                 cqs[:, et * 512:(et + 1) * 512]))
                            pairs.append(
                                (sk1[:, et * 2048 + kk * 128:
                                     et * 2048 + kk * 128 + 128],
                                 sqs[:, et * 512:(et + 1) * 512]))
                        evw = mm_chain(scb, pairs,
                                       extra_pe_waits=qldw + kldw)
                        pw = [(pfree, (idx - 1) * 16 + kk + 1)] if idx >= 1 \
                            else []
                        P.emit("scalar", "activation",
                               pbuf[:, kk * 512:(kk + 1) * 512], scb.ap,
                               AF.Exp, scale=cpool[:, C_EXPB:C_EXPB + 1],
                               waits=list(evw) + pw,
                               incs=[(scb.cons, 1), (psem, 1)])
                        for qs in range(4):
                            w = [(psem, psem.n)]
                            if kk == 0 and qs == 0:
                                w = w + d_pe_w
                            last_d = (kk == 15 and qs == 3)
                            P.emit("tensor", "matmul",
                                   bDEN.ap[:, qs:qs + 1],
                                   pbuf[:, kk * 512 + qs * 128:
                                        kk * 512 + (qs + 1) * 128],
                                   ones_cb[:], start=(kk == 0),
                                   stop=(kk == 15), skip_group_check=True,
                                   waits=w,
                                   incs=[(bDEN.prod, 1)] if last_d else [])
                    av_start = [(psem, psem.n)]
                    ev_by_qs = [[None, None] for _ in range(4)]
                    for eh in range(2):
                        for qs in range(4):
                            bk = bAV[qs]
                            pe_w, evw = bk.acquire()
                            ev_by_qs[qs][eh] = (bk, evw)
                            for kk in range(16):
                                w = (list(pe_w) + av_start) if kk == 0 else []
                                last = (kk == 15)
                                P.emit("tensor", "matmul", bk.ap,
                                       pbuf[:, kk * 512 + qs * 128:
                                            kk * 512 + (qs + 1) * 128],
                                       v1r[:, kk * D + eh * 512:
                                           kk * D + (eh + 1) * 512],
                                       start=(kk == 0), stop=last,
                                       skip_group_check=True, waits=w,
                                       incs=[(bk.prod, 1)] if last else [])
                            dsl = post[:, qs * 1024 + eh * 512:
                                       qs * 1024 + (eh + 1) * 512]
                            if kh == 0:
                                P.emit("vector", "tensor_copy", dsl, bk.ap,
                                       waits=list(evw) + slot_w("pst"),
                                       incs=[(bk.cons, 1), (ve, 1)])
                            else:
                                P.emit("vector", "tensor_tensor", dsl, bk.ap,
                                       poin[:, qs * 1024 + eh * 512:
                                            qs * 1024 + (eh + 1) * 512],
                                       ALU.add,
                                       waits=list(evw) + poldw,
                                       incs=[(bk.cons, 1), (ve, 1)])
                    post_v = ve.n
                    if kh == 0:
                        P.emit("vector", "tensor_copy", denst[:], bDEN.ap,
                               waits=d_ev_w, incs=[(bDEN.cons, 1), (ve, 1)])
                        dv = ve.n
                        slot_load("pst", "sync",
                                  po_s[p].rearrange("p (q d) -> p q d", q=4),
                                  post[:].rearrange("p (q d) -> p q d", q=4),
                                  waits=[(ve, post_v)])
                        slot_load("pdst", "sync", pd_s[p], denst[:],
                                  waits=[(ve, dv)])
                    else:
                        P.emit("vector", "tensor_tensor", denst[:], bDEN.ap,
                               denin[:], ALU.add,
                               waits=list(d_ev_w) + poldw + [(ve, ve.n)],
                               incs=[(ve, 1)])
                        P.emit("vector", "nop", waits=[(ve, ve.n)],
                               incs=[(bDEN.cons, 1)])
                        P.emit("vector", "reciprocal", drc[:], denst[:],
                               waits=[(ve, ve.n)], incs=[(ve, 1)])
                        drc_v = ve.n
                        for qs in range(4):
                            t = 4 * p + qs
                            xslot = xq3[qs % 2]
                            xw = [(ve, ve.n)] if (p * 4 + qs) >= 2 else []
                            lw = slot_load(f"xq3_{qs % 2}", "sync", xslot[:],
                                           xsrc[:, t, :], waits=xw)
                            halves = [
                                (post[:, qs * 1024:qs * 1024 + 512],
                                 [(ve, post_v), (ve, drc_v)], [],
                                 drc[:, qs:qs + 1]),
                                (post[:, qs * 1024 + 512:(qs + 1) * 1024],
                                 [(ve, post_v), (ve, drc_v)], [],
                                 drc[:, qs:qs + 1]),
                            ]
                            residual_store(k, t, halves, xslot, xo3,
                                           lw, dump3b)
                    P.emit("tensor", "nop",
                           waits=[(bAV[0].prod, bAV[0].uses),
                                  (bAV[1].prod, bAV[1].uses),
                                  (bAV[2].prod, bAV[2].uses),
                                  (bAV[3].prod, bAV[3].uses),
                                  (bDEN.prod, bDEN.uses)],
                           incs=[(pfree, 16), (te, 1)])
                    rec3b.append(te.n)
            barrier()

        if _CONFIG.get("dbg") == "p0":
            pass
        else:
            phase_attn_a(0)
        if n_sub >= 2:
            phase_ffn(1)
        if n_sub >= 3:
            phase_attn_b(2)
        if n_sub >= 4:
            phase_ffn(3)

        P.emit("sync", "nop", waits=[(dS, dS.n)])

        with nc.Block() as block:
            block.sync(lambda e: P.replay("sync", e))
            block.gpsimd(lambda e: P.replay("gpsimd", e))
            block.vector(lambda e: P.replay("vector", e))
            block.scalar(lambda e: P.replay("scalar", e))
            block.tensor(lambda e: P.replay("tensor", e))

    return nc


def _wT_sb(w):
    wt = np.ascontiguousarray(w.T)
    din, dout = wt.shape
    it = din // 128
    return np.ascontiguousarray(
        wt.reshape(it, 128, dout).transpose(1, 0, 2).reshape(128, it * dout))


def _w_slab(w, ot_tiles):
    wt = np.ascontiguousarray(w.T)
    din, dout = wt.shape
    itn = din // 128
    assert dout // 128 == ot_tiles
    a = wt.reshape(itn, 128, ot_tiles, 128).transpose(2, 1, 0, 3)
    return np.ascontiguousarray(a.reshape(ot_tiles, 128, itn * 128))


def _col128(v):
    return np.ascontiguousarray(v.reshape(-1, 128).T)


def kernel(**inputs):
    f32 = np.float32
    x = np.ascontiguousarray(inputs["L0_attn_a_in_features"], dtype=f32)

    common = {}
    common["aq_w"] = _wT_sb(f32(inputs["L0_attn_a_q_proj_weight"]))
    common["ak_w"] = _wT_sb(f32(inputs["L0_attn_a_k_proj_weight"]))
    common["av_w"] = _wT_sb(f32(inputs["L0_attn_a_v_proj_weight"]))
    common["ao_w"] = _wT_sb(f32(inputs["L0_attn_a_o_proj_weight"]))
    common["bq_w"] = _wT_sb(f32(inputs["L0_attn_b_Wq"]))
    common["bk_w"] = _wT_sb(f32(inputs["L0_attn_b_Wk"]))
    common["bv_w"] = _wT_sb(f32(inputs["L0_attn_b_Wv"]))
    common["b_bq"] = _col128(f32(inputs["L0_attn_b_bq"]))
    common["b_bk"] = _col128(f32(inputs["L0_attn_b_bk"]))
    common["b_pb"] = _col128(f32(inputs["L0_attn_b_phase_bias"]))
    common["b_bv_row"] = f32(inputs["L0_attn_b_bv"]).reshape(1, D)
    common["f_w1"] = _w_slab(f32(inputs["L0_ffn_w1_w"]), HT)
    common["f_w2"] = _w_slab(f32(inputs["L0_ffn_w2_w"]), HT)
    common["f_h1"] = _w_slab(f32(inputs["L0_ffn_h1_w"]), HT)
    common["f_h2"] = _w_slab(f32(inputs["L0_ffn_h2_w"]), HT)
    common["f_w3"] = _w_slab(f32(inputs["L0_ffn_w3_w"]), DT)
    common["f_s1"] = _col128(f32(inputs["L0_ffn_w1_s"]))
    common["f_s2"] = _col128(f32(inputs["L0_ffn_w2_s"]))
    common["f_sh1"] = _col128(f32(inputs["L0_ffn_h1_s"]))
    common["f_sh2"] = _col128(f32(inputs["L0_ffn_h2_s"]))
    common["f_s3"] = _col128(f32(inputs["L0_ffn_w3_s"]))

    wavelengths = (np.arange(1, D + 1, dtype=np.float64)
                   * (2.0 * np.pi / D)).astype(f32)
    invwl = (1.0 / (wavelengths + np.float32(1e-8))).astype(f32)
    common["invwl_c"] = _col128(invwl)
    common["c_idb"] = np.eye(128, dtype=ml_dtypes.bfloat16)
    common["c_idf"] = np.eye(128, dtype=f32)
    common["c_stair"] = (np.arange(128)[None, :]
                         >= np.arange(128)[:, None]).astype(ml_dtypes.bfloat16)

    nc = build(_CONFIG["n_sub"])
    in_maps = []
    for b in range(8):
        m = dict(common)
        m["x"] = np.ascontiguousarray(x[b])
        in_maps.append(m)
    res = run_bass_kernel_spmd(nc, in_maps, core_ids=list(range(8)))
    out = np.stack([res.results[b]["y"] for b in range(8)], axis=0)
    return out


# version 09
# speedup vs baseline: 1.1862x; 1.1862x over previous
import sys
if "/opt/trn_rl_repo" not in sys.path:
    sys.path.insert(0, "/opt/trn_rl_repo")
from contextlib import ExitStack

import numpy as np
import ml_dtypes

import concourse.bass as bass
import concourse.mybir as mybir
from concourse.bass_utils import run_bass_kernel_spmd

F32 = mybir.dt.float32
BF16 = mybir.dt.bfloat16
I32 = mybir.dt.int32
AF = mybir.ActivationFunctionType
ALU = mybir.AluOpType
AX = mybir.AxisListType

S = 4096
D = 1024
H = 16
DH = 64
VG = 66
DHID = 4096
NT = S // 128
DT = D // 128
HT = DHID // 128
EPS = 1e-5
PI = float(np.float32(np.pi))
TWO_PI = float(np.float32(2 * np.pi))
INV2PI = float(np.float32(1.0 / TWO_PI))

_CONFIG = {"n_sub": 4}
_ENGINES = ("sync", "scalar", "vector", "tensor", "gpsimd")


class Sem:
    __slots__ = ("h", "n", "name")

    def __init__(self, st, nc, name):
        self.h = st.enter_context(nc.semaphore(name))
        self.n = 0
        self.name = name


class Prog:
    def __init__(self, nc, st):
        self.nc = nc
        self.st = st
        self.streams = {e: [] for e in _ENGINES}

    def sem(self, name):
        s = Sem(self.st, self.nc, name)
        if not hasattr(self, "all_sems"):
            self.all_sems = []
        self.all_sems.append(s)
        return s

    def emit(self, engine, method, *args, waits=(), incs=(), **kw):
        ser = _CONFIG.get("serial")
        if ser == "aw_ve" and engine == "scalar":
            waits = list(waits) + [(s, s.n) for s in self.all_sems
                                   if s.name == "ve" and s.n > 0]
        elif ser == "vw_ae" and engine == "vector":
            waits = list(waits) + [(s, s.n) for s in self.all_sems
                                   if s.name == "ae" and s.n > 0]
        elif ser and ser.startswith("eng:"):
            _, eng_n, ctr = ser.split(":")
            if engine == eng_n:
                waits = list(waits) + [(s, s.n) for s in self.all_sems
                                       if s.n > 0 and s.name == ctr]
        elif ser and ser not in ("aw_ve", "vw_ae"):
            wl = tuple(ser.split(","))
            waits = list(waits) + [(s, s.n) for s in self.all_sems
                                   if s.n > 0 and s.name in wl]
        lw = [(s.h, int(v)) for (s, v) in waits if v > 0]
        li = []
        for (s, k) in incs:
            s.n += k
            li.append((s.h, k, s.n))
        self.streams[engine].append((method, args, kw, lw, li))

    def replay(self, engine, eng):
        for m, args, kw, lw, li in self.streams[engine]:
            for h, v in lw:
                eng.wait_ge(h, v)
            ins = eng.nop() if m == "nop" else getattr(eng, m)(*args, **kw)
            for i, (h, k, tgt) in enumerate(li):
                if i == 0:
                    ins.then_inc(h, k)
                    if len(li) > 1 and m == "nop" and not lw:
                        pass
                elif i == 1 and m != "nop":
                    eng.wait_ge(li[0][0], li[0][2])
                    eng.nop().then_inc(h, k)
                else:
                    eng.nop().then_inc(h, k)


class Bank:

    _states = {}

    def __init__(self, P, ap, key):
        self.ap = ap
        st_ = Bank._states.get((id(P), key))
        if st_ is None:
            st_ = {"p": P.sem(key + "_p"), "c": P.sem(key + "_c"), "u": 0}
            Bank._states[(id(P), key)] = st_
        self._s = st_

    @property
    def prod(self):
        return self._s["p"]

    @property
    def cons(self):
        return self._s["c"]

    @property
    def uses(self):
        return self._s["u"]

    def acquire(self):
        pe_waits = [(self.cons, self.uses)] if self.uses > 0 else []
        self._s["u"] += 1
        return pe_waits, [(self.prod, self.uses)]


def _alias(nc, arena, name, shape, dtype, offset):
    base = nc.lookup_mloc(arena).addr
    return nc.alloc_sbuf_tensor_at(name, shape, dtype, offset=base + offset)


class ArenaGroup:
    def __init__(self, nc, arena_h, size, base=0):
        self.nc = nc
        self.arena_h = arena_h
        self.size = size
        self.off = base

    def alloc(self, name, shape, dtype):
        per_part = int(np.prod(shape[1:])) * mybir.dt.size(dtype)
        off = self.off
        self.off = (off + per_part + 127) & ~127
        assert self.off <= self.size, (name, self.off, self.size)
        if not hasattr(self, "offsets"):
            self.offsets = {}
        self.offsets[name] = off
        return _alias(self.nc, self.arena_h, name, shape, dtype, off)

    def alias_at(self, at_name, name, shape, dtype):
        off = self.offsets[at_name]
        per_part = int(np.prod(shape[1:])) * mybir.dt.size(dtype)
        assert off + per_part <= self.size
        return _alias(self.nc, self.arena_h, name, shape, dtype, off)


def build(n_sub=4):
    Bank._states.clear()
    n_sub = max(1, min(4, int(n_sub)))
    nc = bass.Bass("TRN2", target_bir_lowering=False,
                   detect_race_conditions=False)

    def din(name, shape, dtype=F32):
        return nc.dram_tensor(name, shape, dtype, kind="ExternalInput").ap()

    def dscr(name, shape, dtype=F32):
        return nc.dram_tensor(name, shape, dtype, kind="Internal").ap()

    x_in = din("x", [S, D])
    y_out = nc.dram_tensor("y", [S, D], F32, kind="ExternalOutput").ap()
    dbgf = nc.dram_tensor("dbgf", [128, 256], F32, kind="ExternalOutput").ap()
    dbgb = nc.dram_tensor("dbgb", [128, 16384], BF16,
                          kind="ExternalOutput").ap()

    aq_w = din("aq_w", [128, DT * D])
    ak_w = din("ak_w", [128, DT * D])
    av_w = din("av_w", [128, DT * D])
    ao_w = din("ao_w", [128, DT * D])
    bq_w = din("bq_w", [128, DT * D])
    bk_w = din("bk_w", [128, DT * D])
    bv_w = din("bv_w", [128, DT * D])
    b_bq = din("b_bq", [128, DT])
    b_bk = din("b_bk", [128, DT])
    b_pb = din("b_pb", [128, DT])
    b_bv_row = din("b_bv_row", [1, D])
    invwl_in = din("invwl_c", [128, DT])
    f_w1 = din("f_w1", [HT, 128, DT * 128])
    f_w2 = din("f_w2", [HT, 128, DT * 128])
    f_h1 = din("f_h1", [HT, 128, HT * 128])
    f_h2 = din("f_h2", [HT, 128, HT * 128])
    f_w3 = din("f_w3", [DT, 128, HT * 128])
    f_s1 = din("f_s1", [128, HT])
    f_s2 = din("f_s2", [128, HT])
    f_sh1 = din("f_sh1", [128, HT])
    f_sh2 = din("f_sh2", [128, HT])
    f_s3 = din("f_s3", [128, DT])
    c_idb = din("c_idb", [128, 128], BF16)
    c_idf = din("c_idf", [128, 128], F32)
    c_stair = din("c_stair", [128, 128], BF16)

    xs = dscr("xs", [S, D])
    qt_s = dscr("qt_s", [DT, 128, S], BF16)
    cq_s = dscr("cq_s", [DT, 128, S], BF16)
    sq_s = dscr("sq_s", [DT, 128, S], BF16)
    ck2_s = dscr("ck2_s", [DT, 128, S // 2], BF16)
    sk2_s = dscr("sk2_s", [DT, 128, S // 2], BF16)
    v2_s = dscr("v2_s", [NT // 2, 128, D], BF16)
    po_s = dscr("po_s", [8, 128, 4 * D])
    pd_s = dscr("pd_s", [8, 128, 4])
    wsc_w1 = dscr("wsc_w1", [HT, 128, DT * 128], BF16)
    wsc_w2 = dscr("wsc_w2", [HT, 128, DT * 128], BF16)
    wsc_h1 = dscr("wsc_h1", [HT, 128, HT * 128], BF16)
    wsc_h2 = dscr("wsc_h2", [HT, 128, HT * 128], BF16)
    wsc_w3 = dscr("wsc_w3", [DT, 128, HT * 128], BF16)
    wsc_bq = dscr("wsc_bq", [128, DT * D], BF16)
    wsc_bk = dscr("wsc_bk", [128, DT * D], BF16)
    wsc_bv = dscr("wsc_bv", [128, DT * D], BF16)

    st = ExitStack()
    with st:
        P = Prog(nc, st)

        def sb(name, shape, dtype):
            return st.enter_context(nc.sbuf_tensor(name, shape, dtype))

        idb = sb("idb", [128, 128], BF16)
        idf = sb("idf", [128, 128], F32)
        stair = sb("stair", [128, 128], BF16)
        cpool = sb("cpool", [128, 8], F32)
        C_EXPA, C_EXPB, C_HALFPI = 0, 1, 2
        invwl = sb("invwl", [128, DT], F32)
        bqc = sb("bqc", [128, DT], F32)
        bkc = sb("bkc", [128, DT], F32)
        pbcn = sb("pbcn", [128, DT], F32)
        c2q = sb("c2q", [128, DT], F32)
        c2k = sb("c2k", [128, DT], F32)
        bvrow = sb("bvrow", [1, D], F32)
        bvrow_b = sb("bvrow_b", [1, D], BF16)
        ones_rf = sb("ones_rf", [1, 64], F32)
        ones_rb = sb("ones_rb", [1, 128], BF16)
        ones_cb = sb("ones_cb", [128, 1], BF16)
        s1c = sb("s1c", [128, HT], F32)
        s2c = sb("s2c", [128, HT], F32)
        sh1c = sb("sh1c", [128, HT], F32)
        sh2c = sb("sh2c", [128, HT], F32)
        s3c = sb("s3c", [128, DT], F32)
        sxh = sb("sxh", [128, 2 * NT], F32)
        ssq = sb("ssq", [128, NT], F32)
        sxs = sb("sxs", [128, NT], F32)
        stA = sb("stA", [128, NT], F32)
        rstd = sb("rstd", [128, NT], F32)
        nmr = sb("nmr", [128, NT], F32)

        rem = nc.sbuf_bytes_remaining
        ARENA = rem & ~127
        arena_h = st.enter_context(
            nc.sbuf_tensor("arena", [128, ARENA], mybir.dt.uint8))

        gA = ArenaGroup(nc, arena_h, ARENA)
        ktr = gA.alloc("ktr", [128, DT * S], BF16)
        vres = gA.alloc("vres", [128, NT * H * VG + 128], BF16)
        aw = gA.alloc("aw", [128, DT * D], BF16)
        dumpf = gA.alloc("dumpf", [128, D], BF16)
        gA_tail = gA.off
        g1a = ArenaGroup(nc, arena_h, ARENA, base=gA_tail)
        lnT_a = g1a.alloc("lnT_a", [128, DT * 512], BF16)
        xpanA = [g1a.alloc(f"xpanA{i}", [128, D], F32) for i in range(2)]
        lnb = g1a.alloc("lnb", [128, D], BF16)
        qtst = g1a.alloc("qtst", [128, DT * 512], BF16)
        g1b = ArenaGroup(nc, arena_h, ARENA, base=gA_tail)
        qtp = [g1b.alloc(f"qtp{i}", [128, DT * 512], BF16) for i in range(2)]
        pring = [g1b.alloc(f"pring{i}", [128, 1024], BF16) for i in range(3)]
        aoT = g1b.alloc("aoT", [128, DT * 512], BF16)
        bcs = g1b.alloc("bcs", [64, 512], F32)
        rcp = g1b.alloc("rcp", [1, 512], F32)
        xq = [g1b.alloc(f"xq{i}", [128, D], F32) for i in range(2)]
        xo = g1b.alloc("xo", [128, D], F32)

        gF = ArenaGroup(nc, arena_h, ARENA)
        aT = gF.alloc("aT", [128, HT * 1024], BF16)
        bT = gF.alloc("bT", [128, HT * 1024], BF16)
        lnT_f = gF.alloc("lnT_f", [128, DT * 1024], BF16)
        wring = [gF.alloc(f"wring{i}", [128, HT * 128], BF16) for i in range(2)]
        w12r = [gF.alloc(f"w12r{i}", [128, DT * 128], BF16) for i in range(4)]
        silub = gF.alloc("silub", [128, 2048], F32)
        xtf = [gF.alloc(f"xtf{i}", [128, D], F32) for i in range(2)]
        xof = gF.alloc("xof", [128, D], F32)
        lnbf = gF.alloc("lnbf", [128, D], BF16)
        dumpf2 = gF.alloc("dumpf2", [128, D], BF16)
        w3st = gF.alias_at("lnT_f", "w3st", [128, DT * 512], F32)

        gB = ArenaGroup(nc, arena_h, ARENA)
        ck1 = gB.alloc("ck1", [128, DT * 2048], BF16)
        sk1 = gB.alloc("sk1", [128, DT * 2048], BF16)
        v1r = gB.alloc("v1r", [128, (NT // 2) * D], BF16)
        resid_end = gB.off
        g3a = ArenaGroup(nc, arena_h, ARENA, base=resid_end)
        lnT_b = g3a.alloc("lnT_b", [128, DT * 512], BF16)
        bw = g3a.alloc("bw", [128, DT * D], BF16)
        xpanB = [g3a.alloc(f"xpanB{i}", [128, D], F32) for i in range(2)]
        theta = g3a.alloc("theta", [128, 512], F32)
        ubuf = g3a.alloc("ubuf", [128, 512], F32)
        kint = g3a.alloc("kint", [128, 512], I32)
        kflt = g3a.alloc("kflt", [128, 512], F32)
        rbuf = g3a.alloc("rbuf", [128, 512], F32)
        trst = [g3a.alloc(f"trst{i}", [128, DT * 512], BF16) for i in range(4)]
        vst = [g3a.alloc(f"vst{i}", [128, D], BF16) for i in range(2)]
        lnb3 = g3a.alloc("lnb3", [128, D], BF16)
        g3b = ArenaGroup(nc, arena_h, ARENA, base=resid_end)
        pbuf = g3b.alloc("pbuf", [128, 16 * 512], BF16)
        cqp = [g3b.alloc(f"cqp{i}", [128, DT * 512], BF16) for i in range(2)]
        sqp = [g3b.alloc(f"sqp{i}", [128, DT * 512], BF16) for i in range(2)]
        post = g3b.alloc("post", [128, 4 * D], F32)
        poin = g3b.alloc("poin", [128, 4 * D], F32)
        denst = g3b.alloc("denst", [128, 4], F32)
        denin = g3b.alloc("denin", [128, 4], F32)
        drc = g3b.alloc("drc", [128, 4], F32)
        xq3 = [g3b.alloc(f"xq3{i}", [128, D], F32) for i in range(2)]
        xo3 = g3b.alloc("xo3", [128, D], F32)
        dump3b = g3b.alloc("dump3b", [128, D], BF16)

        pA = st.enter_context(nc.psum_tensor("pA", [128, 1024], F32))
        pB = st.enter_context(nc.psum_tensor("pB", [128, 1024], F32))
        pC = st.enter_context(nc.psum_tensor("pC", [128, 1024], F32))
        pD = st.enter_context(nc.psum_tensor("pD", [128, 512], F32))
        ptp = st.enter_context(nc.psum_tensor("ptp", [128, 1024], BF16))

        dX = P.sem("dX")
        dS = P.sem("dS")
        dQ = P.sem("dQ")
        ph = P.sem("ph")
        ve = P.sem("ve")
        ae = P.sem("ae")
        ge = P.sem("ge")
        te = P.sem("te")

        x_in_t = x_in.rearrange("(t p) d -> p t d", p=128)
        xs_t = xs.rearrange("(t p) d -> p t d", p=128)
        y_t = y_out.rearrange("(t p) d -> p t d", p=128)
        sub_src = [x_in_t, xs_t, xs_t, xs_t]

        slot_sems = {}

        def slot_load(name, engine, out, in_, waits=()):
            s = slot_sems.get(name)
            if s is None:
                s = P.sem("sl_" + name)
                slot_sems[name] = s
            P.emit(engine, "dma_start", out=out, in_=in_, waits=waits,
                   incs=[(s, 16)])
            return [(s, s.n)]

        def slot_w(name):
            s = slot_sems.get(name)
            return [(s, s.n)] if s is not None and s.n > 0 else []

        def barrier():
            ws = [(s, s.n) for s in P.all_sems if s is not ph and s.n > 0]
            P.emit("vector", "nop", waits=ws)
            for e in _ENGINES:
                P.emit(e, "nop", incs=[(ph, 1)])
            tgt = ph.n
            for e in _ENGINES:
                P.emit(e, "nop", waits=[(ph, tgt)])

        def mm_chain2(bankA, bankB, triples, extra_pe_waits=()):
            pe_wA, ev_wA = bankA.acquire()
            pe_wB, ev_wB = bankB.acquire()
            n = len(triples)
            for i, (lhsT, rhsA, rhsB) in enumerate(triples):
                w = (list(pe_wA) + list(pe_wB) + list(extra_pe_waits))                     if i == 0 else []
                P.emit("tensor", "matmul", bankA.ap, lhsT, rhsA,
                       start=(i == 0), stop=(i == n - 1),
                       skip_group_check=True, waits=w,
                       incs=[(bankA.prod, 1)] if i == n - 1 else [])
                P.emit("tensor", "matmul", bankB.ap, lhsT, rhsB,
                       start=(i == 0), stop=(i == n - 1),
                       skip_group_check=True,
                       incs=[(bankB.prod, 1)] if i == n - 1 else [])
            return ev_wA, ev_wB

        def mm_chain(bank, pairs, extra_pe_waits=(), out_ap=None):
            pe_w, ev_w = bank.acquire()
            out = bank.ap if out_ap is None else out_ap
            n = len(pairs)
            for i, (lhsT, rhs) in enumerate(pairs):
                w = (list(pe_w) + list(extra_pe_waits)) if i == 0 else []
                inc = [(bank.prod, 1)] if i == n - 1 else []
                P.emit("tensor", "matmul", out, lhsT, rhs,
                       start=(i == 0), stop=(i == n - 1),
                       skip_group_check=True, waits=w, incs=inc)
            return ev_w

        P.emit("vector", "memset", ones_rf[:], 1.0)
        P.emit("vector", "memset", ones_rb[:], 1.0)
        P.emit("vector", "memset", ones_cb[:], 1.0)
        P.emit("vector", "memset", cpool[:, 0:1], 0.125)
        P.emit("vector", "memset", cpool[:, 1:2], 1.0 / 32.0)
        P.emit("vector", "memset", cpool[:, 2:3], PI / 2)

        for dst, src in [(idb, c_idb), (idf, c_idf), (stair, c_stair),
                         (invwl, invwl_in), (bqc, b_bq), (bkc, b_bk),
                         (pbcn, b_pb), (bvrow, b_bv_row),
                         (s1c, f_s1), (s2c, f_s2), (sh1c, f_sh1),
                         (sh2c, f_sh2), (s3c, f_s3)]:
            cw = slot_load("const", "sync", dst[:], src)
        const_w = cw

        P.emit("vector", "tensor_tensor", c2q[:], bqc[:], invwl[:], ALU.mult,
               waits=const_w, incs=[(ve, 1)])
        P.emit("vector", "tensor_tensor", c2q[:], c2q[:], pbcn[:], ALU.add,
               waits=[(ve, ve.n)], incs=[(ve, 1)])
        P.emit("vector", "tensor_tensor", c2k[:], bkc[:], invwl[:], ALU.mult,
               incs=[(ve, 1)])
        P.emit("vector", "tensor_tensor", c2k[:], c2k[:], pbcn[:], ALU.add,
               waits=[(ve, ve.n)], incs=[(ve, 1)])
        P.emit("vector", "tensor_copy", bvrow_b[:], bvrow[:], incs=[(ve, 1)])

        cast_units = []
        for srcw, dstw in ((f_w1, wsc_w1), (f_w2, wsc_w2),
                           (f_h1, wsc_h1), (f_h2, wsc_h2),
                           (f_w3, wsc_w3)):
            for t_ in range(srcw.shape[0]):
                cast_units.append((srcw[t_], dstw[t_]))
        for srcw, dstw in ((bq_w, wsc_bq), (bk_w, wsc_bk), (bv_w, wsc_bv)):
            cast_units.append((srcw, dstw))
        cast_pos = [0]

        def emit_cast_units(n):
            for _ in range(n):
                i = cast_pos[0]
                if i >= len(cast_units):
                    return
                cast_pos[0] = i + 1
                sin_, dout = cast_units[i]
                slot_load("wcast", "gpsimd", dout, sin_)

        p0_done = []
        for t in range(NT):
            slot = xpanA[t % 2]
            wfree = list(p0_done[t - 2]) if t >= 2 else []
            lw = slot_load(f"xpA{t % 2}", "sync", slot[:, 0:D],
                           x_in_t[:, t, :], waits=wfree)
            emit_cast_units(3)
            P.emit("vector", "reduce_sum", sxs[:, t:t + 1], slot[:, 0:D],
                   AX.X, waits=lw, incs=[(ve, 1)])
            P.emit("scalar", "activation", dumpf[:], slot[:, 0:D], AF.Square,
                   accum_out=ssq[:, t:t + 1], waits=lw, incs=[(ae, 1)])
            p0_done.append([(ve, ve.n), (ae, ae.n)])
        barrier()

        def stats_derive(first):
            def vstep(method, *args, waits=(), **kw):
                P.emit("vector", method, *args,
                       waits=list(waits) + [(ve, ve.n)], incs=[(ve, 1)], **kw)
            if not first:
                vstep("tensor_tensor", sxs[:], sxh[:, 0:2 * NT:2],
                      sxh[:, 1:2 * NT:2], ALU.add)
            vstep("tensor_scalar", sxs[:], sxs[:], 1.0 / D, None, ALU.mult)
            vstep("tensor_tensor", stA[:], sxs[:], sxs[:], ALU.mult)
            vstep("scalar_tensor_tensor", stA[:], ssq[:], 1.0 / D,
                  stA[:], ALU.mult, ALU.subtract)
            vstep("tensor_scalar", stA[:], stA[:], EPS, None, ALU.add)
            P.emit("scalar", "activation", stA[:], stA[:], AF.Sqrt,
                   waits=[(ve, ve.n)], incs=[(ae, 1)])
            vstep("reciprocal", rstd[:], stA[:], waits=[(ae, ae.n)])
            vstep("tensor_tensor", nmr[:], sxs[:], rstd[:], ALU.mult)
            vstep("tensor_scalar", nmr[:], nmr[:], -1.0, None, ALU.mult)
            return ve.n

        def ln_tile(xsrc_ap, t, lnbuf, stat_w, x_wait, pt_bank):
            P.emit("scalar", "activation", lnbuf[:], xsrc_ap, AF.Identity,
                   bias=nmr[:, t:t + 1], scale=rstd[:, t:t + 1],
                   waits=[(ve, stat_w), (pt_bank.prod, pt_bank.uses)] + x_wait,
                   incs=[(ae, 1)])
            return ae.n

        def transpose_tile(lnbuf, dest_view, bank_pt, ae_wait, evac_extra):
            pe_w, ev_w = bank_pt.acquire()
            for dt in range(DT):
                w = (list(pe_w) + [(ae, ae_wait)]) if dt == 0 else []
                inc = [(bank_pt.prod, 1)] if dt == DT - 1 else []
                P.emit("tensor", "transpose", ptp[:, dt * 128:(dt + 1) * 128],
                       lnbuf[:, dt * 128:(dt + 1) * 128], idb[:],
                       waits=w, incs=inc)
            P.emit("vector", "tensor_copy", dest_view,
                   ptp[:].rearrange("p (a b) -> p a b", a=DT),
                   waits=list(ev_w) + list(evac_extra),
                   incs=[(bank_pt.cons, 1), (ve, 1)])
            return ve.n

        def residual_store(k, t, halves, xold_slot, xout_buf, x_wait, dump):
            last = (k == n_sub - 1)
            for j, (pap, w, extra_incs, scal) in enumerate(halves):
                incs = [(ve, 1)] + list(extra_incs)
                acc = {} if last else {
                    "accum_out": sxh[:, 2 * t + j:2 * t + j + 1]}
                war = [(dS, dS.n)] if j == 0 else [(ve, ve.n)]
                P.emit("vector", "scalar_tensor_tensor",
                       xout_buf[:, j * 512:(j + 1) * 512], pap, scal,
                       xold_slot[:, j * 512:(j + 1) * 512], ALU.mult, ALU.add,
                       waits=list(w) + list(x_wait) + war, incs=incs, **acc)
            vdone = ve.n
            aw_ = []
            if not last:
                P.emit("scalar", "activation", dump[:], xout_buf[:], AF.Square,
                       accum_out=ssq[:, t:t + 1], waits=[(ve, vdone)],
                       incs=[(ae, 1)])
                aw_ = [(ae, ae.n)]
            dst = y_t if last else xs_t
            P.emit("sync", "dma_start", out=dst[:, t, :], in_=xout_buf[:],
                   waits=[(ve, vdone)] + aw_, incs=[(dS, 16)])

        DUMP = _CONFIG.get("dump")

        def dumpb(col, ap, waits):
            P.emit("sync", "dma_start", out=dbgb[:, col:col + ap.shape[-1]],
                   in_=ap, waits=waits, incs=[(dS, 16)])

        def phase_attn_a(k):
            xsrc = sub_src[k]
            stat_w = stats_derive(first=(k == 0))
            if DUMP:
                P.emit("sync", "dma_start", out=dbgf[:, 0:32], in_=rstd[:],
                       waits=[(ve, stat_w)], incs=[(dS, 16)])
                P.emit("sync", "dma_start", out=dbgf[:, 32:64], in_=nmr[:],
                       waits=[(ve, stat_w)], incs=[(dS, 16)])
                P.emit("sync", "dma_start", out=dbgf[:, 64:96], in_=sxs[:],
                       waits=[(ve, stat_w)], incs=[(dS, 16)])
                P.emit("sync", "dma_start", out=dbgf[:, 96:128], in_=ssq[:],
                       waits=[(ve, stat_w)], incs=[(dS, 16)])
            bA0 = Bank(P, pA[:, 0:512], "pA0")
            bA1 = Bank(P, pA[:, 512:1024], "pA1")
            bPT = Bank(P, ptp[:], "pt")
            P.emit("vector", "memset", vres[:, 64:NT * H * VG:VG], 1.0)
            P.emit("vector", "memset", vres[:, 65:NT * H * VG:VG], 0.0)
            P.emit("vector", "memset",
                   vres[:, NT * H * VG:NT * H * VG + 128], 0.0,
                   incs=[(ve, 1)])
            vones_done = ve.n

            aw_sem = P.sem("aw_sem")
            aw_free = P.sem("aw_free")
            qdma = P.sem("qdma")
            lnT_free_rec = 0
            for p in range(8):
                for tl in range(4):
                    t = 4 * p + tl
                    xp = xpanA[t % 2]
                    xw = [(ae, ae.n)] if t >= 2 else []
                    lw = slot_load(f"xpA{t % 2}", "sync", xp[:],
                                   xsrc[:, t, :], waits=xw)
                    ae_n = ln_tile(xp[:], t, lnb, stat_w, lw, bPT)
                    dest = lnT_a[:].rearrange("p (a s) -> p a s", a=DT)[
                        :, :, tl * 128:(tl + 1) * 128]
                    transpose_tile(lnb, dest, bPT, ae_n,
                                   [(te, lnT_free_rec)] if p > 0 else [])
                lnT_done = ve.n
                for wi, wdram in enumerate((aq_w, ak_w, av_w)):
                    nw = 3 * p + wi
                    P.emit("gpsimd", "dma_start", out=aw[:], in_=wdram,
                           waits=[(aw_free, nw)] if nw > 0 else [],
                           incs=[(aw_sem, 16)])
                    wld = aw_sem.n
                    if wi < 2:
                        for et in range(DT):
                            bk = bA0 if et % 2 == 0 else bA1
                            pairs = [
                                (aw[:, it * D + et * 128: it * D + et * 128 + 128],
                                 lnT_a[:, it * 512:(it + 1) * 512])
                                for it in range(DT)]
                            evw = mm_chain(bk, pairs,
                                           extra_pe_waits=[(aw_sem, wld),
                                                           (ve, lnT_done)])
                            if wi == 0:
                                P.emit("vector", "tensor_copy",
                                       qtst[:, et * 512:(et + 1) * 512],
                                       bk.ap,
                                       waits=list(evw) + [(qdma, qdma.n)],
                                       incs=[(bk.cons, 1), (ve, 1)])
                            else:
                                P.emit("vector", "tensor_copy",
                                       ktr[:, et * S + p * 512:
                                           et * S + (p + 1) * 512], bk.ap,
                                       waits=evw,
                                       incs=[(bk.cons, 1), (ve, 1)])
                        if wi == 0:
                            P.emit("gpsimd", "dma_start",
                                   out=qt_s.rearrange("a p s -> p a s")[
                                       :, :, p * 512:(p + 1) * 512],
                                   in_=qtst[:].rearrange("p (a s) -> p a s",
                                                         a=DT),
                                   waits=[(ve, ve.n)], incs=[(qdma, 16)])
                        P.emit("tensor", "nop",
                               waits=[(bA0.prod, bA0.uses),
                                      (bA1.prod, bA1.uses)],
                               incs=[(aw_free, 1)])
                        emit_cast_units(2)
                    else:
                        for tl in range(4):
                            t = 4 * p + tl
                            for eb in range(2):
                                bk = bA0 if (2 * tl + eb) % 2 == 0 else bA1
                                pairs = [
                                    (lnT_a[:, it * 512 + tl * 128:
                                           it * 512 + tl * 128 + 128],
                                     aw[:, it * D + eb * 512:
                                        it * D + (eb + 1) * 512])
                                    for it in range(DT)]
                                evw = mm_chain(bk, pairs,
                                               extra_pe_waits=[(aw_sem, wld),
                                                               (ve, lnT_done)])
                                dv = vres[:, 0:NT * H * VG].rearrange(
                                    "p (kt h g) -> p kt h g", kt=NT, h=H)[
                                    :, t, eb * 8:(eb + 1) * 8, 0:64]
                                P.emit("vector", "tensor_copy", dv,
                                       bk.ap.rearrange("p (h r) -> p h r", h=8),
                                       waits=list(evw) + [(ve, vones_done)],
                                       incs=[(bk.cons, 1), (ve, 1)])
                        P.emit("tensor", "nop",
                               waits=[(bA0.prod, bA0.uses),
                                      (bA1.prod, bA1.uses)],
                               incs=[(aw_free, 1), (te, 1)])
                        lnT_free_rec = te.n
            p1a_v = ve.n
            p1a_q = qdma.n
            if DUMP:
                w = [(ve, p1a_v)]
                dumpb(0, lnT_a[:], w)
                dumpb(4096, ktr[:, 0:512], w)
                dumpb(4608, vres[:, 0:65], w)
                dumpb(5664, qtst[:], w)
            if _CONFIG.get("dbg") == "1a":
                P.emit("vector", "nop", waits=[(ve, p1a_v)])
                P.emit("scalar", "nop", waits=[(qdma, p1a_q)])
                barrier()
                return

            P.emit("gpsimd", "dma_start", out=aw[:], in_=ao_w,
                   waits=[(aw_free, aw_free.n)], incs=[(aw_sem, 16)])
            ao_ld = aw_sem.n
            bSC = [Bank(P, pA[:], "pAf"), Bank(P, pB[:], "pBf")]
            bOp = [Bank(P, pC[:, 0:512], "pC0"),
                   Bank(P, pC[:, 512:1024], "pC1")]
            bBC = Bank(P, pD[0:64, :], "pD")
            bM = Bank(P, pD[:], "pD")
            pr_sem = P.sem("a_pr")
            pf_sem = P.sem("a_pf")
            pr_uses = 0
            panel_rec = []
            dbg = _CONFIG.get("dbg") or ""
            base = dbg.split("_")[0]
            np_1b = int(base[2:]) if base.startswith("1b") and len(base) > 2 else 8
            for p in range(np_1b):
                qp = qtp[p % 2]
                qw = [(te, panel_rec[p - 2])] if p >= 2 else []
                qldw = slot_load(
                    f"qtp{p % 2}", "sync",
                    qp[:].rearrange("p (a s) -> p a s", a=DT),
                    qt_s.rearrange("a p s -> p a s")[
                        :, :, p * 512:(p + 1) * 512],
                    waits=qw + ([(qdma, p1a_q), (ve, p1a_v)]
                                if p == 0 else []))
                emit_cast_units(12 if p > 0 else 0)
                nkk = 4 * (p + 1)
                pending_norm = [None]

                def emit_norm(p=p):
                    if pending_norm[0] is None:
                        return
                    h_, o_ev_w_ = pending_norm[0]
                    pending_norm[0] = None
                    bO_ = bOp[h_ % 2]
                    et_ = h_ // 2
                    ro_ = (h_ % 2) * 64
                    P.emit("vector", "reciprocal", rcp[:], bO_.ap[64:65, :],
                           waits=list(o_ev_w_) + [(bBC.prod, bBC.uses)],
                           incs=[(ve, 1)])
                    rcp_v = ve.n
                    bc_pe_w, bc_ev_w = bBC.acquire()
                    P.emit("tensor", "matmul", bBC.ap[0:64, :], ones_rf[:],
                           rcp[:], start=True, stop=True,
                           skip_group_check=True,
                           waits=list(bc_pe_w) + [(ve, rcp_v)],
                           incs=[(bBC.prod, 1)])
                    P.emit("vector", "tensor_copy", bcs[:], bBC.ap[0:64, :],
                           waits=list(bc_ev_w) + [(ve, ve.n)],
                           incs=[(bBC.cons, 1), (ve, 1)])
                    aoT_w = [(te, panel_rec[p - 1])] if p >= 1 else []
                    P.emit("vector", "tensor_tensor",
                           aoT[ro_:ro_ + 64, et_ * 512:(et_ + 1) * 512],
                           bO_.ap[0:64, :], bcs[:], ALU.mult,
                           waits=aoT_w + [(ve, ve.n)],
                           incs=[(bO_.cons, 1), (ve, 1)])

                for h in range(H):
                    et = h // 2
                    ro = (h % 2) * 64
                    bO = bOp[h % 2]
                    o_pe_w, o_ev_w = bO.acquire()
                    for j in range(nkk // 2):
                        kk0, kk1 = 2 * j, 2 * j + 1
                        scb = bSC[j % 2]
                        pe_w, ev_w = scb.acquire()
                        for idx, kk in enumerate((kk0, kk1)):
                            lhsT = ktr[ro:ro + 64,
                                       et * S + kk * 128: et * S + kk * 128 + 128]
                            rhs = qp[ro:ro + 64, et * 512:(et + 1) * 512]
                            w = (list(pe_w) + qldw + [(aw_sem, ao_ld)]
                                 ) if idx == 0 else []
                            P.emit("tensor", "matmul",
                                   scb.ap[:, idx * 512:(idx + 1) * 512],
                                   lhsT, rhs, start=True, stop=True,
                                   skip_group_check=True, waits=w,
                                   incs=[(scb.prod, 1)] if idx == 1 else [])
                        slot = pring[pr_uses % 3]
                        pw = [(pf_sem, pr_uses - 2)] if pr_uses >= 3 else []
                        diag = (kk1 >= 4 * p)
                        P.emit("scalar", "activation", slot[:], scb.ap, AF.Exp,
                               scale=cpool[:, C_EXPA:C_EXPA + 1],
                               waits=list(ev_w) + pw,
                               incs=([(scb.cons, 1), (ae, 1)] if diag else
                                     [(scb.cons, 1), (pr_sem, 1)]))
                        if diag:
                            expd = ae.n
                            first = True
                            for idx, kk in enumerate((kk0, kk1)):
                                if kk < 4 * p:
                                    continue
                                off = (kk - 4 * p) * 128
                                base = idx * 512
                                if off > 0:
                                    P.emit("vector", "memset",
                                           slot[:, base:base + off], 0.0,
                                           waits=[(ae, expd)] if first else [])
                                    first = False
                                P.emit("vector", "tensor_tensor",
                                       slot[:, base + off:base + off + 128],
                                       slot[:, base + off:base + off + 128],
                                       stair[:], ALU.mult,
                                       waits=[(ae, expd)] if first else [],
                                       incs=[(ve, 1)])
                                first = False
                            P.emit("vector", "nop", waits=[(ve, ve.n)],
                                   incs=[(pr_sem, 1)])
                        pr_uses += 1
                        if dbg.endswith("_sc"):
                            P.emit("vector", "nop", waits=[(pr_sem, pr_uses)],
                                   incs=[(pf_sem, 1)])
                            continue
                        for idx, kk in enumerate((kk0, kk1)):
                            vap = vres[:, kk * (H * VG) + h * VG:
                                       kk * (H * VG) + h * VG + 128]
                            w = [(pr_sem, pr_uses)]
                            if j == 0 and idx == 0:
                                w = w + o_pe_w
                            last = (j == nkk // 2 - 1 and idx == 1)
                            inc = []
                            if last:
                                inc.append((bO.prod, 1))
                            if idx == 1:
                                inc.append((pf_sem, 1))
                            P.emit("tensor", "matmul", bO.ap[0:128, :], vap,
                                   slot[:, idx * 512:(idx + 1) * 512],
                                   start=(j == 0 and idx == 0), stop=last,
                                   skip_group_check=True, waits=w, incs=inc)
                        if j == 0:
                            emit_norm()
                    if dbg.endswith("_sc"):
                        continue
                    if dbg.endswith("_av"):
                        P.emit("vector", "tensor_copy", bcs[:],
                               bO.ap[0:64, :], waits=o_ev_w,
                               incs=[(bO.cons, 1)])
                        continue
                    pending_norm[0] = (h, o_ev_w)
                emit_norm()
                aoT_done = ve.n
                if dbg.endswith(("_sc", "_av", "_nr")):
                    P.emit("tensor", "nop",
                           waits=[(bSC[0].prod, bSC[0].uses),
                                  (bSC[1].prod, bSC[1].uses),
                                  (bO.prod, bO.uses)],
                           incs=[(te, 1)])
                    panel_rec.append(te.n)
                    continue
                for qs in range(4):
                    t = 4 * p + qs
                    xslot = xq[qs % 2]
                    xw = [(ve, ve.n)] if (p * 4 + qs) >= 2 else []
                    lw = slot_load(f"xq{qs % 2}", "sync", xslot[:],
                                   xsrc[:, t, :], waits=xw)
                    halves = []
                    for mb in range(2):
                        pairs = [
                            (aoT[:, vt * 512 + qs * 128:
                                 vt * 512 + qs * 128 + 128],
                             aw[:, vt * D + mb * 512: vt * D + (mb + 1) * 512])
                            for vt in range(DT)]
                        evw = mm_chain(bM, pairs,
                                       extra_pe_waits=[(ve, aoT_done)])
                        halves.append((bM.ap, evw, [(bM.cons, 1)], 1.0))
                    residual_store(k, t, halves, xslot, xo, lw, dumpf)
                P.emit("tensor", "nop",
                       waits=[(bM.prod, bM.uses)], incs=[(te, 1)])
                panel_rec.append(te.n)
            emit_cast_units(10 ** 9)
            barrier()

        def phase_ffn(k):
            xsrc = sub_src[k]
            stat_w = stats_derive(first=False)
            bPT = Bank(P, ptp[:], "pt")
            b1 = [Bank(P, pA[:, 0:512], "pA0"),
                  Bank(P, pA[:, 512:1024], "pA1")]
            b2 = [Bank(P, pB[:, 0:512], "pB0"),
                  Bank(P, pB[:, 512:1024], "pB1")]
            bTD = Bank(P, pD[:], "pD")
            bTC = Bank(P, pC[:, 0:512], "pC0")
            if "f_wf" not in slot_sems:
                slot_sems["f_wf"] = P.sem("f_wf")
                slot_sems["f_w12f"] = P.sem("f_w12f")
            wfree = slot_sems["f_wf"]
            w12free = slot_sems["f_w12f"]
            if not hasattr(P, "wload_n"):
                P.wload_n = [0]
                P.w12load_n = [0]
            wload_n = P.wload_n
            w12load_n = P.w12load_n
            rec_ln = 0
            rec_tr = 0
            dbgv = _CONFIG.get("dbg") or ""
            n_pan = 4
            stop_stage = None
            if dbgv.startswith("fp"):
                parts = dbgv[2:].split("_")
                n_pan = int(parts[0])
                if len(parts) > 1:
                    stop_stage = int(parts[1][1:])
            for p in range(n_pan):
                for tl in range(8):
                    t = 8 * p + tl
                    xslot = xtf[tl % 2]
                    xw = [(ve, ve.n)] if (t >= 2 or p > 0) else []
                    lw = slot_load(f"xtf{tl % 2}", "sync", xslot[:],
                                   xsrc[:, t, :], waits=xw)
                    ae_n = ln_tile(xslot[:], t, lnbf, stat_w, lw, bPT)
                    dest = lnT_f[:].rearrange("p (a s) -> p a s", a=DT)[
                        :, :, tl * 128:(tl + 1) * 128]
                    transpose_tile(lnbf, dest, bPT, ae_n,
                                   [(te, rec_tr)] if p > 0 else [])
                lnT_done = ve.n
                for ot in range(HT):
                    wldw = []
                    for wi, wdram in enumerate((wsc_w1, wsc_w2)):
                        i = w12load_n[0]
                        w12load_n[0] += 1
                        slot = w12r[i % 4]
                        wfw = [(w12free, i - 3)] if i >= 4 else []
                        wldw += slot_load(f"w12_{i % 4}", "gpsimd", slot[:],
                                          wdram[ot], waits=wfw)
                    s1b = w12r[(w12load_n[0] - 2) % 4]
                    s2b = w12r[(w12load_n[0] - 1) % 4]
                    evws = []
                    for wb, banks in ((s1b, b1), (s2b, b2)):
                        triples = [
                            (wb[:, it * 128:(it + 1) * 128],
                             lnT_f[:, it * 1024: it * 1024 + 512],
                             lnT_f[:, it * 1024 + 512: it * 1024 + 1024])
                            for it in range(DT)]
                        ev0, ev1 = mm_chain2(banks[0], banks[1], triples,
                                             extra_pe_waits=wldw +
                                             [(ve, lnT_done)])
                        evws.append((banks, ev0, ev1))
                    (bk1s, evw1a, evw1b), (bk2s, evw2a, evw2b) = evws
                    evws = [(bk1s[0], evw1a, bk2s[0], evw2a),
                            (bk1s[1], evw1b, bk2s[1], evw2b)]
                    P.emit("tensor", "nop",
                           waits=[(b1[0].prod, b1[0].uses),
                                  (b1[1].prod, b1[1].uses),
                                  (b2[0].prod, b2[0].uses),
                                  (b2[1].prod, b2[1].uses)],
                           incs=[(w12free, 2)])
                    sl = silub[:, (ot % 2) * 1024:(ot % 2) * 1024 + 1024]
                    for sb_i, (bk1, evw1, bk2, evw2) in enumerate(evws):
                        P.emit("scalar", "activation",
                               sl[:, sb_i * 512:(sb_i + 1) * 512], bk1.ap,
                               AF.Silu, scale=s1c[:, ot:ot + 1],
                               waits=list(evw1) + [(ve, ve.n)],
                               incs=[(bk1.cons, 1), (ae, 1)])
                        P.emit("vector", "scalar_tensor_tensor",
                               aT[:, ot * 1024 + sb_i * 512:
                                  ot * 1024 + (sb_i + 1) * 512],
                               bk2.ap, s2c[:, ot:ot + 1],
                               sl[:, sb_i * 512:(sb_i + 1) * 512],
                               ALU.mult, ALU.mult,
                               waits=list(evw2) + [(ae, ae.n)],
                               incs=[(bk2.cons, 1), (ve, 1)])
                P.emit("tensor", "nop",
                       waits=[(b1[0].prod, b1[0].uses),
                              (b1[1].prod, b1[1].uses),
                              (b2[0].prod, b2[0].uses),
                              (b2[1].prod, b2[1].uses)],
                       incs=[(te, 1)])
                rec_ln = te.n
                if _CONFIG.get("dbg") == "f1":
                    P.emit("vector", "nop", waits=[(ve, ve.n), (ae, ae.n)])
                    barrier()
                    return
                prev_done = ("ve", ve.n)
                if _CONFIG.get("dump") == "a2" and p == n_pan - 1:
                    dumpb(0, aT[:, 0:16384], [(ve, ve.n)])
                for si, (wdram, weng, src, dst, scl) in enumerate(
                        [(wsc_h1, "gpsimd", aT, bT, sh1c),
                         (wsc_h2, "gpsimd", bT, aT, sh2c)]):
                    for ot in range(HT):
                        slot = wring[ot % 2]
                        uses = wload_n[0]
                        wload_n[0] += 1
                        wfw = [(wfree, uses - 1)] if uses >= 2 else []
                        wldw = slot_load(f"wr{ot % 2}_{weng}", weng, slot[:],
                                         wdram[ot], waits=wfw)
                        banks = b1 if ot % 2 == 0 else b2
                        pd = [(ve, prev_done[1])] if prev_done[0] == "ve" \
                            else [(ae, prev_done[1])]
                        triples = [
                            (slot[:, it * 128:(it + 1) * 128],
                             src[:, it * 1024: it * 1024 + 512],
                             src[:, it * 1024 + 512: it * 1024 + 1024])
                            for it in range(HT)]
                        ev0, ev1 = mm_chain2(banks[0], banks[1], triples,
                                             extra_pe_waits=wldw + pd)
                        for sb_i, evw in ((0, ev0), (1, ev1)):
                            bk = banks[sb_i]
                            P.emit("scalar", "activation",
                                   dst[:, ot * 1024 + sb_i * 512:
                                       ot * 1024 + (sb_i + 1) * 512],
                                   bk.ap, AF.Gelu, scale=scl[:, ot:ot + 1],
                                   waits=evw, incs=[(bk.cons, 1), (ae, 1)])
                        P.emit("tensor", "nop",
                               waits=[(banks[0].prod, banks[0].uses),
                                      (banks[1].prod, banks[1].uses)],
                               incs=[(wfree, 1)])
                    if stop_stage == 2 + si and p == n_pan - 1:
                        P.emit("vector", "nop", waits=[(ve, ve.n), (ae, ae.n)])
                        barrier()
                        return
                    prev_done = ("ae", ae.n)
                s4banks = [b1[0], b1[1], b2[0], b2[1]]
                for sh in range(2):
                    for ot in range(DT):
                        slot = wring[ot % 2]
                        uses = wload_n[0]
                        wload_n[0] += 1
                        wfw = [(wfree, uses - 1)] if uses >= 2 else []
                        wldw = slot_load(f"wr{ot % 2}_gpsimd", "gpsimd",
                                         slot[:], wsc_w3[ot], waits=wfw)
                        bk = s4banks[ot % 4]
                        pairs = [
                            (slot[:, it * 128:(it + 1) * 128],
                             aT[:, it * 1024 + sh * 512:
                                it * 1024 + sh * 512 + 512])
                            for it in range(HT)]
                        evw = mm_chain(bk, pairs,
                                       extra_pe_waits=wldw +
                                       [(ae, prev_done[1])])
                        war = [(te, rec_tr)] if (p > 0 or sh > 0) else []
                        P.emit("vector", "tensor_scalar",
                               w3st[:, ot * 512:(ot + 1) * 512],
                               bk.ap, s3c[:, ot:ot + 1], None, ALU.mult,
                               waits=list(evw) + war,
                               incs=[(bk.cons, 1), (ve, 1)])
                        P.emit("tensor", "nop",
                               waits=[(bk.prod, bk.uses)],
                               incs=[(wfree, 1)])
                    w3st_done = ve.n
                    for tlh in range(4):
                        tl = sh * 4 + tlh
                        t = 8 * p + tl
                        xslot = xtf[tl % 2]
                        lw = slot_load(f"xtf{tl % 2}", "sync", xslot[:],
                                       xsrc[:, t, :], waits=[(ve, ve.n)])
                        halves = []
                        for half, bk in ((0, bTD), (1, bTC)):
                            pe_w, ev_w = bk.acquire()
                            for q in range(4):
                                dt = half * 4 + q
                                w = (list(pe_w) + [(ve, w3st_done)]) \
                                    if q == 0 else []
                                P.emit("tensor", "transpose",
                                       bk.ap[:, q * 128:(q + 1) * 128],
                                       w3st[:, dt * 512 + tlh * 128:
                                            dt * 512 + tlh * 128 + 128],
                                       idf[:], waits=w,
                                       incs=[(bk.prod, 1)] if q == 3 else [])
                            halves.append((bk.ap, ev_w, [(bk.cons, 1)], 1.0))
                        residual_store(k, t, halves, xslot, xof, lw,
                                       dumpf2)
                    P.emit("tensor", "nop",
                           waits=[(bTD.prod, bTD.uses),
                                  (bTC.prod, bTC.uses)],
                           incs=[(te, 1)])
                    rec_tr = te.n
            barrier()

        def phase_attn_b(k):
            xsrc = sub_src[k]
            stat_w = stats_derive(first=False)
            bPT = Bank(P, ptp[:], "pt")
            bA0 = Bank(P, pA[:, 0:512], "pA0")
            bA1 = Bank(P, pA[:, 512:1024], "pA1")
            bw_sem = P.sem("bw_sem")
            bw_free = P.sem("bw_free")
            sdma = P.sem("b_sdma")
            rec_ln3 = 0

            def trig(bk, evw, et, c2, sin_dst, cos_dst,
                     sin_war=(), cos_war=()):
                P.emit("vector", "tensor_scalar", theta[:], bk.ap,
                       invwl[:, et:et + 1], c2[:, et:et + 1],
                       ALU.mult, ALU.add,
                       waits=list(evw) + [(ge, ge.n), (ve, ve.n)],
                       incs=[(bk.cons, 1), (ve, 1)])
                th_v = ve.n
                for is_cos, dstap in ((0, sin_dst), (1, cos_dst)):
                    if is_cos:
                        P.emit("vector", "tensor_scalar", ubuf[:], theta[:],
                               INV2PI, 0.25, ALU.mult, ALU.add,
                               waits=[(ge, ge.n), (ve, th_v)],
                               incs=[(ve, 1)])
                    else:
                        P.emit("vector", "tensor_scalar", ubuf[:], theta[:],
                               INV2PI, None, ALU.mult,
                               waits=[(ge, ge.n), (ve, th_v)],
                               incs=[(ve, 1)])
                    uv = ve.n
                    P.emit("vector", "tensor_copy", kint[:], ubuf[:],
                           waits=[(ve, uv)], incs=[(ve, 1)])
                    P.emit("vector", "tensor_copy", kflt[:], kint[:],
                           waits=[(ve, ve.n)], incs=[(ve, 1)])
                    P.emit("vector", "scalar_tensor_tensor", rbuf[:], kflt[:],
                           -TWO_PI, theta[:], ALU.mult, ALU.add,
                           waits=[(ve, ve.n), (ae, ae.n)], incs=[(ve, 1)])
                    hi = (PI / 2) if is_cos else PI
                    lo = (-1.5 * PI) if is_cos else -PI
                    P.emit("vector", "tensor_scalar", ubuf[:], rbuf[:],
                           float(np.float32(hi)), None, ALU.is_gt,
                           waits=[(ve, ve.n), (ge, ge.n)], incs=[(ve, 1)])
                    P.emit("vector", "scalar_tensor_tensor", kflt[:], ubuf[:],
                           -TWO_PI, rbuf[:], ALU.mult, ALU.add,
                           waits=[(ve, ve.n)], incs=[(ve, 1)])
                    P.emit("vector", "tensor_scalar", ubuf[:], kflt[:],
                           float(np.float32(lo)), None, ALU.is_lt,
                           waits=[(ve, ve.n)], incs=[(ve, 1)])
                    P.emit("vector", "scalar_tensor_tensor", rbuf[:], ubuf[:],
                           TWO_PI, kflt[:], ALU.mult, ALU.add,
                           waits=[(ve, ve.n)], incs=[(ve, 1)])
                    rv = ve.n
                    kwargs = {}
                    war = list(cos_war if is_cos else sin_war)
                    if is_cos:
                        kwargs["bias"] = cpool[:, C_HALFPI:C_HALFPI + 1]
                    P.emit("scalar", "activation", dstap, rbuf[:], AF.Sin,
                           waits=[(ve, rv)] + war,
                           incs=[(ae, 1)], **kwargs)
                return ae.n

            for p in range(8):
                for tl in range(4):
                    t = 4 * p + tl
                    xp = xpanB[t % 2]
                    xw = [(ae, ae.n)] if t >= 2 else []
                    lw = slot_load(f"xpB{t % 2}", "sync", xp[:],
                                   xsrc[:, t, :], waits=xw)
                    ae_n = ln_tile(xp[:], t, lnb3, stat_w, lw, bPT)
                    dest = lnT_b[:].rearrange("p (a s) -> p a s", a=DT)[
                        :, :, tl * 128:(tl + 1) * 128]
                    transpose_tile(lnb3, dest, bPT, ae_n,
                                   [(te, rec_ln3)] if p > 0 else [])
                lnT_done = ve.n
                for wi, wdram in enumerate((wsc_bq, wsc_bk, wsc_bv)):
                    nw = 3 * p + wi
                    P.emit("gpsimd", "dma_start", out=bw[:], in_=wdram,
                           waits=[(bw_free, nw)] if nw > 0 else [],
                           incs=[(bw_sem, 16)])
                    wld = bw_sem.n
                    if wi < 2:
                        c2 = c2q if wi == 0 else c2k
                        for et in range(DT):
                            bk = bA0 if et % 2 == 0 else bA1
                            pairs = [
                                (bw[:, it * D + et * 128:
                                    it * D + et * 128 + 128],
                                 lnT_b[:, it * 512:(it + 1) * 512])
                                for it in range(DT)]
                            evw = mm_chain(bk, pairs,
                                           extra_pe_waits=[(bw_sem, wld),
                                                           (ve, lnT_done)])
                            if wi == 0:
                                sin_dst = trst[1][:, et * 512:(et + 1) * 512]
                                cos_dst = trst[0][:, et * 512:(et + 1) * 512]
                                sw, cw2 = slot_w("st_sq"), slot_w("st_cq")
                            elif p < 4:
                                sin_dst = sk1[:, et * 2048 + p * 512:
                                              et * 2048 + (p + 1) * 512]
                                cos_dst = ck1[:, et * 2048 + p * 512:
                                              et * 2048 + (p + 1) * 512]
                                sw, cw2 = [], []
                            else:
                                sin_dst = trst[3][:, et * 512:(et + 1) * 512]
                                cos_dst = trst[2][:, et * 512:(et + 1) * 512]
                                sw, cw2 = slot_w("st_sk"), slot_w("st_ck")
                            trig(bk, evw, et, c2, sin_dst, cos_dst,
                                 sin_war=sw, cos_war=cw2)
                        P.emit("tensor", "nop",
                               waits=[(bA0.prod, bA0.uses),
                                      (bA1.prod, bA1.uses)],
                               incs=[(bw_free, 1)])
                        if wi == 0:
                            slot_load("st_cq", "scalar",
                                      cq_s.rearrange("a p s -> p a s")[
                                          :, :, p * 512:(p + 1) * 512],
                                      trst[0][:].rearrange(
                                          "p (a s) -> p a s", a=DT),
                                      waits=[(ae, ae.n)])
                            slot_load("st_sq", "scalar",
                                      sq_s.rearrange("a p s -> p a s")[
                                          :, :, p * 512:(p + 1) * 512],
                                      trst[1][:].rearrange(
                                          "p (a s) -> p a s", a=DT),
                                      waits=[(ae, ae.n)])
                        elif p >= 4:
                            slot_load("st_ck", "scalar",
                                      ck2_s.rearrange("a p s -> p a s")[
                                          :, :, (p - 4) * 512:(p - 3) * 512],
                                      trst[2][:].rearrange(
                                          "p (a s) -> p a s", a=DT),
                                      waits=[(ae, ae.n)])
                            slot_load("st_sk", "scalar",
                                      sk2_s.rearrange("a p s -> p a s")[
                                          :, :, (p - 4) * 512:(p - 3) * 512],
                                      trst[3][:].rearrange(
                                          "p (a s) -> p a s", a=DT),
                                      waits=[(ae, ae.n)])
                    else:
                        for tl in range(4):
                            t = 4 * p + tl
                            for eb in range(2):
                                bk = bA0 if (2 * tl + eb) % 2 == 0 else bA1
                                pe_w, evw0 = bk.acquire()
                                pairs = [
                                    (lnT_b[:, it * 512 + tl * 128:
                                           it * 512 + tl * 128 + 128],
                                     bw[:, it * D + eb * 512:
                                        it * D + (eb + 1) * 512])
                                    for it in range(DT)]
                                for i, (lh, rh) in enumerate(pairs):
                                    w = (list(pe_w) + [(bw_sem, wld),
                                                       (ve, lnT_done)]
                                         ) if i == 0 else []
                                    P.emit("tensor", "matmul", bk.ap, lh, rh,
                                           start=(i == 0), stop=False,
                                           skip_group_check=True, waits=w)
                                P.emit("tensor", "matmul", bk.ap, ones_rb[:],
                                       bvrow_b[:, eb * 512:(eb + 1) * 512],
                                       start=False, stop=True,
                                       skip_group_check=True,
                                       incs=[(bk.prod, 1)])
                                if p < 4:
                                    dstv = v1r[:, t * D + eb * 512:
                                               t * D + (eb + 1) * 512]
                                    P.emit("vector", "tensor_copy", dstv,
                                           bk.ap, waits=evw0,
                                           incs=[(bk.cons, 1), (ve, 1)])
                                else:
                                    dstv = vst[tl % 2][:, eb * 512:
                                                       (eb + 1) * 512]
                                    P.emit("vector", "tensor_copy", dstv,
                                           bk.ap,
                                           waits=list(evw0) +
                                           slot_w(f"st_v{tl % 2}"),
                                           incs=[(bk.cons, 1), (ve, 1)])
                            if p >= 4:
                                slot_load(f"st_v{tl % 2}", "sync",
                                          v2_s[t - 16], vst[tl % 2][:],
                                          waits=[(ve, ve.n)])
                        P.emit("tensor", "nop",
                               waits=[(bA0.prod, bA0.uses),
                                      (bA1.prod, bA1.uses)],
                               incs=[(bw_free, 1), (te, 1)])
                        rec_ln3 = te.n
            barrier()

            bSC = [Bank(P, pA[:, 0:512], "pA0"),
                   Bank(P, pA[:, 512:1024], "pA1")]
            bAV = [Bank(P, pB[:, 0:512], "pB0"),
                   Bank(P, pB[:, 512:1024], "pB1"),
                   Bank(P, pC[:, 0:512], "pC0"),
                   Bank(P, pC[:, 512:1024], "pC1")]
            bDEN = Bank(P, pD[:, 0:4], "pD4")
            psem = P.sem("b_ps")
            pfree = P.sem("b_pf")
            rec3b = []
            po_store = [0] * 8
            for kh in range(2):
                kldw = []
                if kh == 1:
                    kw = [(te, rec3b[-1])]
                    kldw += slot_load(
                        "kck", "sync",
                        ck1[:].rearrange("p (a s) -> p a s", a=DT),
                        ck2_s.rearrange("a p s -> p a s"), waits=kw)
                    kldw += slot_load(
                        "ksk", "sync",
                        sk1[:].rearrange("p (a s) -> p a s", a=DT),
                        sk2_s.rearrange("a p s -> p a s"), waits=kw)
                    kldw += slot_load(
                        "kv1", "sync",
                        v1r[:].rearrange("p (t d) -> p t d", t=NT // 2),
                        v2_s.rearrange("t p d -> p t d"), waits=kw)
                for p in range(8):
                    idx = kh * 8 + p
                    cqs = cqp[idx % 2]
                    sqs = sqp[idx % 2]
                    qw = [(te, rec3b[idx - 2])] if idx >= 2 else []
                    if idx == 0:
                        qw = qw + slot_w("st_cq") + slot_w("st_sq")
                    qldw = slot_load(
                        f"cq{idx % 2}", "sync",
                        cqs[:].rearrange("p (a s) -> p a s", a=DT),
                        cq_s.rearrange("a p s -> p a s")[
                            :, :, p * 512:(p + 1) * 512], waits=qw)
                    qldw += slot_load(
                        f"sq{idx % 2}", "sync",
                        sqs[:].rearrange("p (a s) -> p a s", a=DT),
                        sq_s.rearrange("a p s -> p a s")[
                            :, :, p * 512:(p + 1) * 512], waits=qw)
                    poldw = []
                    if kh == 1:
                        poldw = slot_load(
                            "pin", "sync",
                            poin[:].rearrange("p (q d) -> p q d", q=4),
                            po_s[p].rearrange("p (q d) -> p q d", q=4),
                            waits=[(ve, ve.n)] + slot_w("pst"))
                        poldw += slot_load("din", "sync", denin[:],
                                           pd_s[p], waits=slot_w("pdst"))
                    d_pe_w, d_ev_w = bDEN.acquire()
                    for kk in range(16):
                        scb = bSC[kk % 2]
                        pairs = []
                        for et in range(DT):
                            pairs.append(
                                (ck1[:, et * 2048 + kk * 128:
                                     et * 2048 + kk * 128 + 128],
                                 cqs[:, et * 512:(et + 1) * 512]))
                            pairs.append(
                                (sk1[:, et * 2048 + kk * 128:
                                     et * 2048 + kk * 128 + 128],
                                 sqs[:, et * 512:(et + 1) * 512]))
                        evw = mm_chain(scb, pairs,
                                       extra_pe_waits=qldw + kldw)
                        pw = [(pfree, (idx - 1) * 16 + kk + 1)] if idx >= 1 \
                            else []
                        P.emit("scalar", "activation",
                               pbuf[:, kk * 512:(kk + 1) * 512], scb.ap,
                               AF.Exp, scale=cpool[:, C_EXPB:C_EXPB + 1],
                               waits=list(evw) + pw,
                               incs=[(scb.cons, 1), (psem, 1)])
                        for qs in range(4):
                            w = [(psem, psem.n)]
                            if kk == 0 and qs == 0:
                                w = w + d_pe_w
                            last_d = (kk == 15 and qs == 3)
                            P.emit("tensor", "matmul",
                                   bDEN.ap[:, qs:qs + 1],
                                   pbuf[:, kk * 512 + qs * 128:
                                        kk * 512 + (qs + 1) * 128],
                                   ones_cb[:], start=(kk == 0),
                                   stop=(kk == 15), skip_group_check=True,
                                   waits=w,
                                   incs=[(bDEN.prod, 1)] if last_d else [])
                    av_start = [(psem, psem.n)]
                    ev_by_qs = [[None, None] for _ in range(4)]
                    for eh in range(2):
                        for qs in range(4):
                            bk = bAV[qs]
                            pe_w, evw = bk.acquire()
                            ev_by_qs[qs][eh] = (bk, evw)
                            for kk in range(16):
                                w = (list(pe_w) + av_start) if kk == 0 else []
                                last = (kk == 15)
                                P.emit("tensor", "matmul", bk.ap,
                                       pbuf[:, kk * 512 + qs * 128:
                                            kk * 512 + (qs + 1) * 128],
                                       v1r[:, kk * D + eh * 512:
                                           kk * D + (eh + 1) * 512],
                                       start=(kk == 0), stop=last,
                                       skip_group_check=True, waits=w,
                                       incs=[(bk.prod, 1)] if last else [])
                            dsl = post[:, qs * 1024 + eh * 512:
                                       qs * 1024 + (eh + 1) * 512]
                            if kh == 0:
                                P.emit("vector", "tensor_copy", dsl, bk.ap,
                                       waits=list(evw) + slot_w("pst"),
                                       incs=[(bk.cons, 1), (ve, 1)])
                            else:
                                P.emit("vector", "tensor_tensor", dsl, bk.ap,
                                       poin[:, qs * 1024 + eh * 512:
                                            qs * 1024 + (eh + 1) * 512],
                                       ALU.add,
                                       waits=list(evw) + poldw,
                                       incs=[(bk.cons, 1), (ve, 1)])
                    post_v = ve.n
                    if kh == 0:
                        P.emit("vector", "tensor_copy", denst[:], bDEN.ap,
                               waits=d_ev_w, incs=[(bDEN.cons, 1), (ve, 1)])
                        dv = ve.n
                        slot_load("pst", "sync",
                                  po_s[p].rearrange("p (q d) -> p q d", q=4),
                                  post[:].rearrange("p (q d) -> p q d", q=4),
                                  waits=[(ve, post_v)])
                        slot_load("pdst", "sync", pd_s[p], denst[:],
                                  waits=[(ve, dv)])
                    else:
                        P.emit("vector", "tensor_tensor", denst[:], bDEN.ap,
                               denin[:], ALU.add,
                               waits=list(d_ev_w) + poldw + [(ve, ve.n)],
                               incs=[(ve, 1)])
                        P.emit("vector", "nop", waits=[(ve, ve.n)],
                               incs=[(bDEN.cons, 1)])
                        P.emit("vector", "reciprocal", drc[:], denst[:],
                               waits=[(ve, ve.n)], incs=[(ve, 1)])
                        drc_v = ve.n
                        for qs in range(4):
                            t = 4 * p + qs
                            xslot = xq3[qs % 2]
                            xw = [(ve, ve.n)] if (p * 4 + qs) >= 2 else []
                            lw = slot_load(f"xq3_{qs % 2}", "sync", xslot[:],
                                           xsrc[:, t, :], waits=xw)
                            halves = [
                                (post[:, qs * 1024:qs * 1024 + 512],
                                 [(ve, post_v), (ve, drc_v)], [],
                                 drc[:, qs:qs + 1]),
                                (post[:, qs * 1024 + 512:(qs + 1) * 1024],
                                 [(ve, post_v), (ve, drc_v)], [],
                                 drc[:, qs:qs + 1]),
                            ]
                            residual_store(k, t, halves, xslot, xo3,
                                           lw, dump3b)
                    P.emit("tensor", "nop",
                           waits=[(bAV[0].prod, bAV[0].uses),
                                  (bAV[1].prod, bAV[1].uses),
                                  (bAV[2].prod, bAV[2].uses),
                                  (bAV[3].prod, bAV[3].uses),
                                  (bDEN.prod, bDEN.uses)],
                           incs=[(pfree, 16), (te, 1)])
                    rec3b.append(te.n)
            barrier()

        if _CONFIG.get("dbg") == "p0":
            pass
        else:
            phase_attn_a(0)
        if n_sub >= 2:
            phase_ffn(1)
        if n_sub >= 3:
            phase_attn_b(2)
        if n_sub >= 4:
            phase_ffn(3)

        P.emit("sync", "nop", waits=[(dS, dS.n)])

        with nc.Block() as block:
            block.sync(lambda e: P.replay("sync", e))
            block.gpsimd(lambda e: P.replay("gpsimd", e))
            block.vector(lambda e: P.replay("vector", e))
            block.scalar(lambda e: P.replay("scalar", e))
            block.tensor(lambda e: P.replay("tensor", e))

    return nc


def _wT_sb(w):
    wt = np.ascontiguousarray(w.T)
    din, dout = wt.shape
    it = din // 128
    return np.ascontiguousarray(
        wt.reshape(it, 128, dout).transpose(1, 0, 2).reshape(128, it * dout))


def _w_slab(w, ot_tiles):
    wt = np.ascontiguousarray(w.T)
    din, dout = wt.shape
    itn = din // 128
    assert dout // 128 == ot_tiles
    a = wt.reshape(itn, 128, ot_tiles, 128).transpose(2, 1, 0, 3)
    return np.ascontiguousarray(a.reshape(ot_tiles, 128, itn * 128))


def _col128(v):
    return np.ascontiguousarray(v.reshape(-1, 128).T)


def kernel(**inputs):
    f32 = np.float32
    x = np.ascontiguousarray(inputs["L0_attn_a_in_features"], dtype=f32)

    common = {}
    common["aq_w"] = _wT_sb(f32(inputs["L0_attn_a_q_proj_weight"]))
    common["ak_w"] = _wT_sb(f32(inputs["L0_attn_a_k_proj_weight"]))
    common["av_w"] = _wT_sb(f32(inputs["L0_attn_a_v_proj_weight"]))
    common["ao_w"] = _wT_sb(f32(inputs["L0_attn_a_o_proj_weight"]))
    common["bq_w"] = _wT_sb(f32(inputs["L0_attn_b_Wq"]))
    common["bk_w"] = _wT_sb(f32(inputs["L0_attn_b_Wk"]))
    common["bv_w"] = _wT_sb(f32(inputs["L0_attn_b_Wv"]))
    common["b_bq"] = _col128(f32(inputs["L0_attn_b_bq"]))
    common["b_bk"] = _col128(f32(inputs["L0_attn_b_bk"]))
    common["b_pb"] = _col128(f32(inputs["L0_attn_b_phase_bias"]))
    common["b_bv_row"] = f32(inputs["L0_attn_b_bv"]).reshape(1, D)
    common["f_w1"] = _w_slab(f32(inputs["L0_ffn_w1_w"]), HT)
    common["f_w2"] = _w_slab(f32(inputs["L0_ffn_w2_w"]), HT)
    common["f_h1"] = _w_slab(f32(inputs["L0_ffn_h1_w"]), HT)
    common["f_h2"] = _w_slab(f32(inputs["L0_ffn_h2_w"]), HT)
    common["f_w3"] = _w_slab(f32(inputs["L0_ffn_w3_w"]), DT)
    common["f_s1"] = _col128(f32(inputs["L0_ffn_w1_s"]))
    common["f_s2"] = _col128(f32(inputs["L0_ffn_w2_s"]))
    common["f_sh1"] = _col128(f32(inputs["L0_ffn_h1_s"]))
    common["f_sh2"] = _col128(f32(inputs["L0_ffn_h2_s"]))
    common["f_s3"] = _col128(f32(inputs["L0_ffn_w3_s"]))

    wavelengths = (np.arange(1, D + 1, dtype=np.float64)
                   * (2.0 * np.pi / D)).astype(f32)
    invwl = (1.0 / (wavelengths + np.float32(1e-8))).astype(f32)
    common["invwl_c"] = _col128(invwl)
    common["c_idb"] = np.eye(128, dtype=ml_dtypes.bfloat16)
    common["c_idf"] = np.eye(128, dtype=f32)
    common["c_stair"] = (np.arange(128)[None, :]
                         >= np.arange(128)[:, None]).astype(ml_dtypes.bfloat16)

    nc = build(_CONFIG["n_sub"])
    in_maps = []
    for b in range(8):
        m = dict(common)
        m["x"] = np.ascontiguousarray(x[b])
        in_maps.append(m)
    res = run_bass_kernel_spmd(nc, in_maps, core_ids=list(range(8)))
    out = np.stack([res.results[b]["y"] for b in range(8)], axis=0)
    return out


# version 10
# speedup vs baseline: 1.2783x; 1.0777x over previous
import sys
if "/opt/trn_rl_repo" not in sys.path:
    sys.path.insert(0, "/opt/trn_rl_repo")
from contextlib import ExitStack

import numpy as np
import ml_dtypes

import concourse.bass as bass
import concourse.mybir as mybir
from concourse.bass_utils import run_bass_kernel_spmd

F32 = mybir.dt.float32
BF16 = mybir.dt.bfloat16
I32 = mybir.dt.int32
AF = mybir.ActivationFunctionType
ALU = mybir.AluOpType
AX = mybir.AxisListType

S = 4096
D = 1024
H = 16
DH = 64
VG = 66
DHID = 4096
NT = S // 128
DT = D // 128
HT = DHID // 128
EPS = 1e-5
PI = float(np.float32(np.pi))
TWO_PI = float(np.float32(2 * np.pi))
INV2PI = float(np.float32(1.0 / TWO_PI))

_CONFIG = {"n_sub": 4}
_ENGINES = ("sync", "scalar", "vector", "tensor", "gpsimd")


class Sem:
    __slots__ = ("h", "n", "name")

    def __init__(self, st, nc, name):
        self.h = st.enter_context(nc.semaphore(name))
        self.n = 0
        self.name = name


class Prog:
    def __init__(self, nc, st):
        self.nc = nc
        self.st = st
        self.streams = {e: [] for e in _ENGINES}

    def sem(self, name):
        s = Sem(self.st, self.nc, name)
        if not hasattr(self, "all_sems"):
            self.all_sems = []
        self.all_sems.append(s)
        return s

    def emit(self, engine, method, *args, waits=(), incs=(), **kw):
        ser = _CONFIG.get("serial")
        if ser == "aw_ve" and engine == "scalar":
            waits = list(waits) + [(s, s.n) for s in self.all_sems
                                   if s.name == "ve" and s.n > 0]
        elif ser == "vw_ae" and engine == "vector":
            waits = list(waits) + [(s, s.n) for s in self.all_sems
                                   if s.name == "ae" and s.n > 0]
        elif ser and ser.startswith("eng:"):
            _, eng_n, ctr = ser.split(":")
            if engine == eng_n:
                waits = list(waits) + [(s, s.n) for s in self.all_sems
                                       if s.n > 0 and s.name == ctr]
        elif ser and ser not in ("aw_ve", "vw_ae"):
            wl = tuple(ser.split(","))
            waits = list(waits) + [(s, s.n) for s in self.all_sems
                                   if s.n > 0 and s.name in wl]
        lw = [(s.h, int(v)) for (s, v) in waits if v > 0]
        li = []
        for (s, k) in incs:
            s.n += k
            li.append((s.h, k, s.n))
        self.streams[engine].append((method, args, kw, lw, li))

    def replay(self, engine, eng):
        for m, args, kw, lw, li in self.streams[engine]:
            for h, v in lw:
                eng.wait_ge(h, v)
            ins = eng.nop() if m == "nop" else getattr(eng, m)(*args, **kw)
            for i, (h, k, tgt) in enumerate(li):
                if i == 0:
                    ins.then_inc(h, k)
                    if len(li) > 1 and m == "nop" and not lw:
                        pass
                elif i == 1 and m != "nop":
                    eng.wait_ge(li[0][0], li[0][2])
                    eng.nop().then_inc(h, k)
                else:
                    eng.nop().then_inc(h, k)


class Bank:

    _states = {}

    def __init__(self, P, ap, key):
        self.ap = ap
        st_ = Bank._states.get((id(P), key))
        if st_ is None:
            st_ = {"p": P.sem(key + "_p"), "c": P.sem(key + "_c"), "u": 0}
            Bank._states[(id(P), key)] = st_
        self._s = st_

    @property
    def prod(self):
        return self._s["p"]

    @property
    def cons(self):
        return self._s["c"]

    @property
    def uses(self):
        return self._s["u"]

    def acquire(self):
        pe_waits = [(self.cons, self.uses)] if self.uses > 0 else []
        self._s["u"] += 1
        return pe_waits, [(self.prod, self.uses)]


def _alias(nc, arena, name, shape, dtype, offset):
    base = nc.lookup_mloc(arena).addr
    return nc.alloc_sbuf_tensor_at(name, shape, dtype, offset=base + offset)


class ArenaGroup:
    def __init__(self, nc, arena_h, size, base=0):
        self.nc = nc
        self.arena_h = arena_h
        self.size = size
        self.off = base

    def alloc(self, name, shape, dtype):
        per_part = int(np.prod(shape[1:])) * mybir.dt.size(dtype)
        off = self.off
        self.off = (off + per_part + 127) & ~127
        assert self.off <= self.size, (name, self.off, self.size)
        if not hasattr(self, "offsets"):
            self.offsets = {}
        self.offsets[name] = off
        return _alias(self.nc, self.arena_h, name, shape, dtype, off)

    def alias_at(self, at_name, name, shape, dtype):
        off = self.offsets[at_name]
        per_part = int(np.prod(shape[1:])) * mybir.dt.size(dtype)
        assert off + per_part <= self.size
        return _alias(self.nc, self.arena_h, name, shape, dtype, off)


def build(n_sub=4):
    Bank._states.clear()
    n_sub = max(1, min(4, int(n_sub)))
    nc = bass.Bass("TRN2", target_bir_lowering=False,
                   detect_race_conditions=False)

    def din(name, shape, dtype=F32):
        return nc.dram_tensor(name, shape, dtype, kind="ExternalInput").ap()

    def dscr(name, shape, dtype=F32):
        return nc.dram_tensor(name, shape, dtype, kind="Internal").ap()

    x_in = din("x", [S, D])
    y_out = nc.dram_tensor("y", [S, D], F32, kind="ExternalOutput").ap()
    dbgf = nc.dram_tensor("dbgf", [128, 256], F32, kind="ExternalOutput").ap()
    dbgb = nc.dram_tensor("dbgb", [128, 16384], BF16,
                          kind="ExternalOutput").ap()

    aq_w = din("aq_w", [128, DT * D])
    ak_w = din("ak_w", [128, DT * D])
    av_w = din("av_w", [128, DT * D])
    ao_w = din("ao_w", [128, DT * D])
    bq_w = din("bq_w", [128, DT * D])
    bk_w = din("bk_w", [128, DT * D])
    bv_w = din("bv_w", [128, DT * D])
    b_bq = din("b_bq", [128, DT])
    b_bk = din("b_bk", [128, DT])
    b_pb = din("b_pb", [128, DT])
    b_bv_row = din("b_bv_row", [1, D])
    invwl_in = din("invwl_c", [128, DT])
    f_w1 = din("f_w1", [HT, 128, DT * 128])
    f_w2 = din("f_w2", [HT, 128, DT * 128])
    f_h1 = din("f_h1", [HT, 128, HT * 128])
    f_h2 = din("f_h2", [HT, 128, HT * 128])
    f_w3 = din("f_w3", [DT, 128, HT * 128])
    f_s1 = din("f_s1", [128, HT])
    f_s2 = din("f_s2", [128, HT])
    f_sh1 = din("f_sh1", [128, HT])
    f_sh2 = din("f_sh2", [128, HT])
    f_s3 = din("f_s3", [128, DT])
    c_idb = din("c_idb", [128, 128], BF16)
    c_idf = din("c_idf", [128, 128], F32)
    c_stair = din("c_stair", [128, 128], BF16)

    xs = dscr("xs", [S, D])
    qt_s = dscr("qt_s", [DT, 128, S], BF16)
    cq_s = dscr("cq_s", [DT, 128, S], BF16)
    sq_s = dscr("sq_s", [DT, 128, S], BF16)
    ck2_s = dscr("ck2_s", [DT, 128, S // 2], BF16)
    sk2_s = dscr("sk2_s", [DT, 128, S // 2], BF16)
    v2_s = dscr("v2_s", [NT // 2, 128, D], BF16)
    po_s = dscr("po_s", [8, 128, 4 * D])
    pd_s = dscr("pd_s", [8, 128, 4])
    wsc_w1 = dscr("wsc_w1", [HT, 128, DT * 128], BF16)
    wsc_w2 = dscr("wsc_w2", [HT, 128, DT * 128], BF16)
    wsc_h1 = dscr("wsc_h1", [HT, 128, HT * 128], BF16)
    wsc_h2 = dscr("wsc_h2", [HT, 128, HT * 128], BF16)
    wsc_w3 = dscr("wsc_w3", [DT, 128, HT * 128], BF16)
    wsc_bq = dscr("wsc_bq", [128, DT * D], BF16)
    wsc_bk = dscr("wsc_bk", [128, DT * D], BF16)
    wsc_bv = dscr("wsc_bv", [128, DT * D], BF16)

    st = ExitStack()
    with st:
        P = Prog(nc, st)

        def sb(name, shape, dtype):
            return st.enter_context(nc.sbuf_tensor(name, shape, dtype))

        idb = sb("idb", [128, 128], BF16)
        idf = sb("idf", [128, 128], F32)
        stair = sb("stair", [128, 128], BF16)
        cpool = sb("cpool", [128, 8], F32)
        C_EXPA, C_EXPB, C_HALFPI = 0, 1, 2
        invwl = sb("invwl", [128, DT], F32)
        bqc = sb("bqc", [128, DT], F32)
        bkc = sb("bkc", [128, DT], F32)
        pbcn = sb("pbcn", [128, DT], F32)
        c2q = sb("c2q", [128, DT], F32)
        c2k = sb("c2k", [128, DT], F32)
        bvrow = sb("bvrow", [1, D], F32)
        bvrow_b = sb("bvrow_b", [1, D], BF16)
        ones_rf = sb("ones_rf", [1, 64], F32)
        ones_rb = sb("ones_rb", [1, 128], BF16)
        ones_cb = sb("ones_cb", [128, 1], BF16)
        s1c = sb("s1c", [128, HT], F32)
        s2c = sb("s2c", [128, HT], F32)
        sh1c = sb("sh1c", [128, HT], F32)
        sh2c = sb("sh2c", [128, HT], F32)
        s3c = sb("s3c", [128, DT], F32)
        sxh = sb("sxh", [128, 2 * NT], F32)
        ssq = sb("ssq", [128, NT], F32)
        sxs = sb("sxs", [128, NT], F32)
        stA = sb("stA", [128, NT], F32)
        rstd = sb("rstd", [128, NT], F32)
        nmr = sb("nmr", [128, NT], F32)

        rem = nc.sbuf_bytes_remaining
        ARENA = rem & ~127
        arena_h = st.enter_context(
            nc.sbuf_tensor("arena", [128, ARENA], mybir.dt.uint8))

        gA = ArenaGroup(nc, arena_h, ARENA)
        ktr = gA.alloc("ktr", [128, DT * S], BF16)
        vres = gA.alloc("vres", [128, NT * H * VG + 128], BF16)
        aw = gA.alloc("aw", [128, DT * D], BF16)
        dumpf = gA.alloc("dumpf", [128, D], BF16)
        gA_tail = gA.off
        g1a = ArenaGroup(nc, arena_h, ARENA, base=gA_tail)
        lnT_a = g1a.alloc("lnT_a", [128, DT * 512], BF16)
        xpanA = [g1a.alloc(f"xpanA{i}", [128, D], F32) for i in range(2)]
        lnb = g1a.alloc("lnb", [128, D], BF16)
        qtst = g1a.alloc("qtst", [128, DT * 512], BF16)
        g1b = ArenaGroup(nc, arena_h, ARENA, base=gA_tail)
        qtp = [g1b.alloc(f"qtp{i}", [128, DT * 512], BF16) for i in range(2)]
        pring = [g1b.alloc(f"pring{i}", [128, 1024], BF16) for i in range(3)]
        aoT = g1b.alloc("aoT", [128, DT * 512], BF16)
        bcs = g1b.alloc("bcs", [64, 512], F32)
        rcp = g1b.alloc("rcp", [1, 512], F32)
        xq = [g1b.alloc(f"xq{i}", [128, D], F32) for i in range(2)]
        xo = g1b.alloc("xo", [128, D], F32)

        gF = ArenaGroup(nc, arena_h, ARENA)
        aT = gF.alloc("aT", [128, HT * 1024], BF16)
        bT = gF.alloc("bT", [128, HT * 1024], BF16)
        lnT_f = gF.alloc("lnT_f", [128, DT * 1024], BF16)
        wring = [gF.alloc(f"wring{i}", [128, HT * 128], BF16) for i in range(2)]
        w12r = [gF.alloc(f"w12r{i}", [128, DT * 128], BF16) for i in range(4)]
        silub = gF.alloc("silub", [128, 2048], F32)
        xtf = [gF.alloc(f"xtf{i}", [128, D], F32) for i in range(2)]
        xof = gF.alloc("xof", [128, D], F32)
        lnbf = gF.alloc("lnbf", [128, D], BF16)
        dumpf2 = gF.alloc("dumpf2", [128, D], BF16)
        w3st = gF.alias_at("lnT_f", "w3st", [128, DT * 512], F32)

        gB = ArenaGroup(nc, arena_h, ARENA)
        ck1 = gB.alloc("ck1", [128, DT * 2048], BF16)
        sk1 = gB.alloc("sk1", [128, DT * 2048], BF16)
        v1r = gB.alloc("v1r", [128, (NT // 2) * D], BF16)
        resid_end = gB.off
        g3a = ArenaGroup(nc, arena_h, ARENA, base=resid_end)
        lnT_b = g3a.alloc("lnT_b", [128, DT * 512], BF16)
        bw = g3a.alloc("bw", [128, DT * D], BF16)
        xpanB = [g3a.alloc(f"xpanB{i}", [128, D], F32) for i in range(2)]
        theta = g3a.alloc("theta", [128, 512], F32)
        ubuf = g3a.alloc("ubuf", [128, 512], F32)
        kint = g3a.alloc("kint", [128, 512], I32)
        kflt = g3a.alloc("kflt", [128, 512], F32)
        rbuf = g3a.alloc("rbuf", [128, 512], F32)
        trst = [g3a.alloc(f"trst{i}", [128, DT * 512], BF16) for i in range(4)]
        vst = [g3a.alloc(f"vst{i}", [128, D], BF16) for i in range(2)]
        lnb3 = g3a.alloc("lnb3", [128, D], BF16)
        g3b = ArenaGroup(nc, arena_h, ARENA, base=resid_end)
        pbuf = g3b.alloc("pbuf", [128, 16 * 512], BF16)
        cqp = [g3b.alloc(f"cqp{i}", [128, DT * 512], BF16) for i in range(2)]
        sqp = [g3b.alloc(f"sqp{i}", [128, DT * 512], BF16) for i in range(2)]
        post = g3b.alloc("post", [128, 4 * D], F32)
        poin = g3b.alloc("poin", [128, 4 * D], F32)
        denst = g3b.alloc("denst", [128, 4], F32)
        denin = g3b.alloc("denin", [128, 4], F32)
        drc = g3b.alloc("drc", [128, 4], F32)
        xq3 = [g3b.alloc(f"xq3{i}", [128, D], F32) for i in range(2)]
        xo3 = g3b.alloc("xo3", [128, D], F32)
        dump3b = g3b.alloc("dump3b", [128, D], BF16)

        pA = st.enter_context(nc.psum_tensor("pA", [128, 1024], F32))
        pB = st.enter_context(nc.psum_tensor("pB", [128, 1024], F32))
        pC = st.enter_context(nc.psum_tensor("pC", [128, 1024], F32))
        pD = st.enter_context(nc.psum_tensor("pD", [128, 512], F32))
        ptp = st.enter_context(nc.psum_tensor("ptp", [128, 1024], BF16))

        dX = P.sem("dX")
        dS = P.sem("dS")
        dQ = P.sem("dQ")
        ph = P.sem("ph")
        ve = P.sem("ve")
        ae = P.sem("ae")
        ge = P.sem("ge")
        te = P.sem("te")

        x_in_t = x_in.rearrange("(t p) d -> p t d", p=128)
        xs_t = xs.rearrange("(t p) d -> p t d", p=128)
        y_t = y_out.rearrange("(t p) d -> p t d", p=128)
        sub_src = [x_in_t, xs_t, xs_t, xs_t]

        slot_sems = {}

        def slot_load(name, engine, out, in_, waits=()):
            s = slot_sems.get(name)
            if s is None:
                s = P.sem("sl_" + name)
                slot_sems[name] = s
            P.emit(engine, "dma_start", out=out, in_=in_, waits=waits,
                   incs=[(s, 16)])
            return [(s, s.n)]

        def slot_w(name):
            s = slot_sems.get(name)
            return [(s, s.n)] if s is not None and s.n > 0 else []

        def barrier():
            ws = [(s, s.n) for s in P.all_sems if s is not ph and s.n > 0]
            P.emit("vector", "nop", waits=ws)
            for e in _ENGINES:
                P.emit(e, "nop", incs=[(ph, 1)])
            tgt = ph.n
            for e in _ENGINES:
                P.emit(e, "nop", waits=[(ph, tgt)])

        def mm_chain2(bankA, bankB, triples, extra_pe_waits=()):
            pe_wA, ev_wA = bankA.acquire()
            pe_wB, ev_wB = bankB.acquire()
            n = len(triples)
            for i, (lhsT, rhsA, rhsB) in enumerate(triples):
                w = (list(pe_wA) + list(pe_wB) + list(extra_pe_waits))                     if i == 0 else []
                P.emit("tensor", "matmul", bankA.ap, lhsT, rhsA,
                       start=(i == 0), stop=(i == n - 1),
                       skip_group_check=True, waits=w,
                       incs=[(bankA.prod, 1)] if i == n - 1 else [])
                P.emit("tensor", "matmul", bankB.ap, lhsT, rhsB,
                       start=(i == 0), stop=(i == n - 1),
                       skip_group_check=True,
                       incs=[(bankB.prod, 1)] if i == n - 1 else [])
            return ev_wA, ev_wB

        def mm_chain(bank, pairs, extra_pe_waits=(), out_ap=None):
            pe_w, ev_w = bank.acquire()
            out = bank.ap if out_ap is None else out_ap
            n = len(pairs)
            for i, (lhsT, rhs) in enumerate(pairs):
                w = (list(pe_w) + list(extra_pe_waits)) if i == 0 else []
                inc = [(bank.prod, 1)] if i == n - 1 else []
                P.emit("tensor", "matmul", out, lhsT, rhs,
                       start=(i == 0), stop=(i == n - 1),
                       skip_group_check=True, waits=w, incs=inc)
            return ev_w

        P.emit("vector", "memset", ones_rf[:], 1.0)
        P.emit("vector", "memset", ones_rb[:], 1.0)
        P.emit("vector", "memset", ones_cb[:], 1.0)
        P.emit("vector", "memset", cpool[:, 0:1], 0.125)
        P.emit("vector", "memset", cpool[:, 1:2], 1.0 / 32.0)
        P.emit("vector", "memset", cpool[:, 2:3], PI / 2)

        for dst, src in [(idb, c_idb), (idf, c_idf), (stair, c_stair),
                         (invwl, invwl_in), (bqc, b_bq), (bkc, b_bk),
                         (pbcn, b_pb), (bvrow, b_bv_row),
                         (s1c, f_s1), (s2c, f_s2), (sh1c, f_sh1),
                         (sh2c, f_sh2), (s3c, f_s3)]:
            cw = slot_load("const", "sync", dst[:], src)
        const_w = cw

        P.emit("vector", "tensor_tensor", c2q[:], bqc[:], invwl[:], ALU.mult,
               waits=const_w, incs=[(ve, 1)])
        P.emit("vector", "tensor_tensor", c2q[:], c2q[:], pbcn[:], ALU.add,
               waits=[(ve, ve.n)], incs=[(ve, 1)])
        P.emit("vector", "tensor_tensor", c2k[:], bkc[:], invwl[:], ALU.mult,
               incs=[(ve, 1)])
        P.emit("vector", "tensor_tensor", c2k[:], c2k[:], pbcn[:], ALU.add,
               waits=[(ve, ve.n)], incs=[(ve, 1)])
        P.emit("vector", "tensor_copy", bvrow_b[:], bvrow[:], incs=[(ve, 1)])

        cast_units = []
        for srcw, dstw in ((f_w1, wsc_w1), (f_w2, wsc_w2),
                           (f_h1, wsc_h1), (f_h2, wsc_h2),
                           (f_w3, wsc_w3)):
            for t_ in range(srcw.shape[0]):
                cast_units.append((srcw[t_], dstw[t_]))
        for srcw, dstw in ((bq_w, wsc_bq), (bk_w, wsc_bk), (bv_w, wsc_bv)):
            cast_units.append((srcw, dstw))
        cast_pos = [0]

        def emit_cast_units(n):
            for _ in range(n):
                i = cast_pos[0]
                if i >= len(cast_units):
                    return
                cast_pos[0] = i + 1
                sin_, dout = cast_units[i]
                slot_load("wcast", "gpsimd", dout, sin_)

        p0_done = []
        for t in range(NT):
            slot = xpanA[t % 2]
            wfree = list(p0_done[t - 2]) if t >= 2 else []
            lw = slot_load(f"xpA{t % 2}", "sync", slot[:, 0:D],
                           x_in_t[:, t, :], waits=wfree)
            emit_cast_units(3)
            P.emit("vector", "reduce_sum", sxs[:, t:t + 1], slot[:, 0:D],
                   AX.X, waits=lw, incs=[(ve, 1)])
            P.emit("scalar", "activation", dumpf[:], slot[:, 0:D], AF.Square,
                   accum_out=ssq[:, t:t + 1], waits=lw, incs=[(ae, 1)])
            p0_done.append([(ve, ve.n), (ae, ae.n)])
        barrier()

        def stats_derive(first):
            def vstep(method, *args, waits=(), **kw):
                P.emit("vector", method, *args,
                       waits=list(waits) + [(ve, ve.n)], incs=[(ve, 1)], **kw)
            if not first:
                vstep("tensor_tensor", sxs[:], sxh[:, 0:2 * NT:2],
                      sxh[:, 1:2 * NT:2], ALU.add)
            vstep("tensor_scalar", sxs[:], sxs[:], 1.0 / D, None, ALU.mult)
            vstep("tensor_tensor", stA[:], sxs[:], sxs[:], ALU.mult)
            vstep("scalar_tensor_tensor", stA[:], ssq[:], 1.0 / D,
                  stA[:], ALU.mult, ALU.subtract)
            vstep("tensor_scalar", stA[:], stA[:], EPS, None, ALU.add)
            P.emit("scalar", "activation", stA[:], stA[:], AF.Sqrt,
                   waits=[(ve, ve.n)], incs=[(ae, 1)])
            vstep("reciprocal", rstd[:], stA[:], waits=[(ae, ae.n)])
            vstep("tensor_tensor", nmr[:], sxs[:], rstd[:], ALU.mult)
            vstep("tensor_scalar", nmr[:], nmr[:], -1.0, None, ALU.mult)
            return ve.n

        def ln_tile(xsrc_ap, t, lnbuf, stat_w, x_wait, pt_bank):
            P.emit("scalar", "activation", lnbuf[:], xsrc_ap, AF.Identity,
                   bias=nmr[:, t:t + 1], scale=rstd[:, t:t + 1],
                   waits=[(ve, stat_w), (pt_bank.prod, pt_bank.uses)] + x_wait,
                   incs=[(ae, 1)])
            return ae.n

        def transpose_tile(lnbuf, dest_view, bank_pt, ae_wait, evac_extra):
            pe_w, ev_w = bank_pt.acquire()
            for dt in range(DT):
                w = (list(pe_w) + [(ae, ae_wait)]) if dt == 0 else []
                inc = [(bank_pt.prod, 1)] if dt == DT - 1 else []
                P.emit("tensor", "transpose", ptp[:, dt * 128:(dt + 1) * 128],
                       lnbuf[:, dt * 128:(dt + 1) * 128], idb[:],
                       waits=w, incs=inc)
            P.emit("vector", "tensor_copy", dest_view,
                   ptp[:].rearrange("p (a b) -> p a b", a=DT),
                   waits=list(ev_w) + list(evac_extra),
                   incs=[(bank_pt.cons, 1), (ve, 1)])
            return ve.n

        def residual_store(k, t, halves, xold_slot, xout_buf, x_wait, dump):
            last = (k == n_sub - 1)
            for j, (pap, w, extra_incs, scal) in enumerate(halves):
                incs = [(ve, 1)] + list(extra_incs)
                acc = {} if last else {
                    "accum_out": sxh[:, 2 * t + j:2 * t + j + 1]}
                war = [(dS, dS.n)] if j == 0 else [(ve, ve.n)]
                P.emit("vector", "scalar_tensor_tensor",
                       xout_buf[:, j * 512:(j + 1) * 512], pap, scal,
                       xold_slot[:, j * 512:(j + 1) * 512], ALU.mult, ALU.add,
                       waits=list(w) + list(x_wait) + war, incs=incs, **acc)
            vdone = ve.n
            aw_ = []
            if not last:
                P.emit("scalar", "activation", dump[:], xout_buf[:], AF.Square,
                       accum_out=ssq[:, t:t + 1], waits=[(ve, vdone)],
                       incs=[(ae, 1)])
                aw_ = [(ae, ae.n)]
            dst = y_t if last else xs_t
            P.emit("sync", "dma_start", out=dst[:, t, :], in_=xout_buf[:],
                   waits=[(ve, vdone)] + aw_, incs=[(dS, 16)])

        DUMP = _CONFIG.get("dump")

        def dumpb(col, ap, waits):
            P.emit("sync", "dma_start", out=dbgb[:, col:col + ap.shape[-1]],
                   in_=ap, waits=waits, incs=[(dS, 16)])

        def phase_attn_a(k):
            xsrc = sub_src[k]
            stat_w = stats_derive(first=(k == 0))
            if DUMP:
                P.emit("sync", "dma_start", out=dbgf[:, 0:32], in_=rstd[:],
                       waits=[(ve, stat_w)], incs=[(dS, 16)])
                P.emit("sync", "dma_start", out=dbgf[:, 32:64], in_=nmr[:],
                       waits=[(ve, stat_w)], incs=[(dS, 16)])
                P.emit("sync", "dma_start", out=dbgf[:, 64:96], in_=sxs[:],
                       waits=[(ve, stat_w)], incs=[(dS, 16)])
                P.emit("sync", "dma_start", out=dbgf[:, 96:128], in_=ssq[:],
                       waits=[(ve, stat_w)], incs=[(dS, 16)])
            bA0 = Bank(P, pA[:, 0:512], "pA0")
            bA1 = Bank(P, pA[:, 512:1024], "pA1")
            bPT = Bank(P, ptp[:], "pt")
            P.emit("vector", "memset", vres[:, 64:NT * H * VG:VG], 1.0)
            P.emit("vector", "memset", vres[:, 65:NT * H * VG:VG], 0.0)
            P.emit("vector", "memset",
                   vres[:, NT * H * VG:NT * H * VG + 128], 0.0,
                   incs=[(ve, 1)])
            vones_done = ve.n

            aw_sem = P.sem("aw_sem")
            aw_free = P.sem("aw_free")
            qdma = P.sem("qdma")
            lnT_free_rec = 0
            for p in range(8):
                for tl in range(4):
                    t = 4 * p + tl
                    xp = xpanA[t % 2]
                    xw = [(ae, ae.n)] if t >= 2 else []
                    lw = slot_load(f"xpA{t % 2}", "sync", xp[:],
                                   xsrc[:, t, :], waits=xw)
                    ae_n = ln_tile(xp[:], t, lnb, stat_w, lw, bPT)
                    dest = lnT_a[:].rearrange("p (a s) -> p a s", a=DT)[
                        :, :, tl * 128:(tl + 1) * 128]
                    transpose_tile(lnb, dest, bPT, ae_n,
                                   [(te, lnT_free_rec)] if p > 0 else [])
                lnT_done = ve.n
                for wi, wdram in enumerate((aq_w, ak_w, av_w)):
                    nw = 3 * p + wi
                    P.emit("gpsimd", "dma_start", out=aw[:], in_=wdram,
                           waits=[(aw_free, nw)] if nw > 0 else [],
                           incs=[(aw_sem, 16)])
                    wld = aw_sem.n
                    if wi < 2:
                        for et in range(DT):
                            bk = bA0 if et % 2 == 0 else bA1
                            pairs = [
                                (aw[:, it * D + et * 128: it * D + et * 128 + 128],
                                 lnT_a[:, it * 512:(it + 1) * 512])
                                for it in range(DT)]
                            evw = mm_chain(bk, pairs,
                                           extra_pe_waits=[(aw_sem, wld),
                                                           (ve, lnT_done)])
                            if wi == 0:
                                P.emit("vector", "tensor_copy",
                                       qtst[:, et * 512:(et + 1) * 512],
                                       bk.ap,
                                       waits=list(evw) + [(qdma, qdma.n)],
                                       incs=[(bk.cons, 1), (ve, 1)])
                            else:
                                P.emit("vector", "tensor_copy",
                                       ktr[:, et * S + p * 512:
                                           et * S + (p + 1) * 512], bk.ap,
                                       waits=evw,
                                       incs=[(bk.cons, 1), (ve, 1)])
                        if wi == 0:
                            P.emit("gpsimd", "dma_start",
                                   out=qt_s.rearrange("a p s -> p a s")[
                                       :, :, p * 512:(p + 1) * 512],
                                   in_=qtst[:].rearrange("p (a s) -> p a s",
                                                         a=DT),
                                   waits=[(ve, ve.n)], incs=[(qdma, 16)])
                        P.emit("tensor", "nop",
                               waits=[(bA0.prod, bA0.uses),
                                      (bA1.prod, bA1.uses)],
                               incs=[(aw_free, 1)])
                        emit_cast_units(2)
                    else:
                        for tl in range(4):
                            t = 4 * p + tl
                            for eb in range(2):
                                bk = bA0 if (2 * tl + eb) % 2 == 0 else bA1
                                pairs = [
                                    (lnT_a[:, it * 512 + tl * 128:
                                           it * 512 + tl * 128 + 128],
                                     aw[:, it * D + eb * 512:
                                        it * D + (eb + 1) * 512])
                                    for it in range(DT)]
                                evw = mm_chain(bk, pairs,
                                               extra_pe_waits=[(aw_sem, wld),
                                                               (ve, lnT_done)])
                                dv = vres[:, 0:NT * H * VG].rearrange(
                                    "p (kt h g) -> p kt h g", kt=NT, h=H)[
                                    :, t, eb * 8:(eb + 1) * 8, 0:64]
                                P.emit("vector", "tensor_copy", dv,
                                       bk.ap.rearrange("p (h r) -> p h r", h=8),
                                       waits=list(evw) + [(ve, vones_done)],
                                       incs=[(bk.cons, 1), (ve, 1)])
                        P.emit("tensor", "nop",
                               waits=[(bA0.prod, bA0.uses),
                                      (bA1.prod, bA1.uses)],
                               incs=[(aw_free, 1), (te, 1)])
                        lnT_free_rec = te.n
            p1a_v = ve.n
            p1a_q = qdma.n
            if DUMP:
                w = [(ve, p1a_v)]
                dumpb(0, lnT_a[:], w)
                dumpb(4096, ktr[:, 0:512], w)
                dumpb(4608, vres[:, 0:65], w)
                dumpb(5664, qtst[:], w)
            if _CONFIG.get("dbg") == "1a":
                P.emit("vector", "nop", waits=[(ve, p1a_v)])
                P.emit("scalar", "nop", waits=[(qdma, p1a_q)])
                barrier()
                return

            P.emit("gpsimd", "dma_start", out=aw[:], in_=ao_w,
                   waits=[(aw_free, aw_free.n)], incs=[(aw_sem, 16)])
            ao_ld = aw_sem.n
            bSC = [Bank(P, pA[:], "pAf"), Bank(P, pB[:], "pBf")]
            bOp = [Bank(P, pC[:, 0:512], "pC0"),
                   Bank(P, pC[:, 512:1024], "pC1")]
            bBC = Bank(P, pD[0:64, :], "pD")
            bM = Bank(P, pD[:], "pD")
            pr_sem = P.sem("a_pr")
            pf_sem = P.sem("a_pf")
            pr_uses = 0
            panel_rec = []
            dbg = _CONFIG.get("dbg") or ""
            base = dbg.split("_")[0]
            np_1b = int(base[2:]) if base.startswith("1b") and len(base) > 2 else 8
            for p in range(np_1b):
                qp = qtp[p % 2]
                qw = [(te, panel_rec[p - 2])] if p >= 2 else []
                qldw = slot_load(
                    f"qtp{p % 2}", "sync",
                    qp[:].rearrange("p (a s) -> p a s", a=DT),
                    qt_s.rearrange("a p s -> p a s")[
                        :, :, p * 512:(p + 1) * 512],
                    waits=qw + ([(qdma, p1a_q), (ve, p1a_v)]
                                if p == 0 else []))
                emit_cast_units(12 if p > 0 else 0)
                nkk = 4 * (p + 1)
                pending_norm = [None]

                def emit_norm(p=p):
                    if pending_norm[0] is None:
                        return
                    h_, o_ev_w_ = pending_norm[0]
                    pending_norm[0] = None
                    bO_ = bOp[h_ % 2]
                    et_ = h_ // 2
                    ro_ = (h_ % 2) * 64
                    P.emit("vector", "reciprocal", rcp[:], bO_.ap[64:65, :],
                           waits=list(o_ev_w_) + [(bBC.prod, bBC.uses)],
                           incs=[(ve, 1)])
                    rcp_v = ve.n
                    bc_pe_w, bc_ev_w = bBC.acquire()
                    P.emit("tensor", "matmul", bBC.ap[0:64, :], ones_rf[:],
                           rcp[:], start=True, stop=True,
                           skip_group_check=True,
                           waits=list(bc_pe_w) + [(ve, rcp_v)],
                           incs=[(bBC.prod, 1)])
                    P.emit("vector", "tensor_copy", bcs[:], bBC.ap[0:64, :],
                           waits=list(bc_ev_w) + [(ve, ve.n)],
                           incs=[(bBC.cons, 1), (ve, 1)])
                    aoT_w = [(te, panel_rec[p - 1])] if p >= 1 else []
                    P.emit("vector", "tensor_tensor",
                           aoT[ro_:ro_ + 64, et_ * 512:(et_ + 1) * 512],
                           bO_.ap[0:64, :], bcs[:], ALU.mult,
                           waits=aoT_w + [(ve, ve.n)],
                           incs=[(bO_.cons, 1), (ve, 1)])

                pipe = not dbg.endswith(("_sc", "_av"))
                pending_av = [None]

                def flush_av():
                    if pending_av[0] is not None:
                        go = pending_av[0]
                        pending_av[0] = None
                        go()

                def mk_av(h, bO, slot, kk0, kk1, pr_t, o_pe_w,
                          is_first, is_last):
                    def go():
                        for idx, kk in enumerate((kk0, kk1)):
                            vap = vres[:, kk * (H * VG) + h * VG:
                                       kk * (H * VG) + h * VG + 128]
                            w = [(pr_sem, pr_t)]
                            if is_first and idx == 0:
                                w = w + list(o_pe_w)
                            last = (is_last and idx == 1)
                            inc = []
                            if last:
                                inc.append((bO.prod, 1))
                            if idx == 1:
                                inc.append((pf_sem, 1))
                            P.emit("tensor", "matmul", bO.ap[0:128, :], vap,
                                   slot[:, idx * 512:(idx + 1) * 512],
                                   start=(is_first and idx == 0), stop=last,
                                   skip_group_check=True, waits=w, incs=inc)
                    return go

                for h in range(H):
                    et = h // 2
                    ro = (h % 2) * 64
                    bO = bOp[h % 2]
                    o_pe_w, o_ev_w = bO.acquire()
                    for j in range(nkk // 2):
                        kk0, kk1 = 2 * j, 2 * j + 1
                        scb = bSC[j % 2]
                        pe_w, ev_w = scb.acquire()
                        for idx, kk in enumerate((kk0, kk1)):
                            lhsT = ktr[ro:ro + 64,
                                       et * S + kk * 128: et * S + kk * 128 + 128]
                            rhs = qp[ro:ro + 64, et * 512:(et + 1) * 512]
                            w = (list(pe_w) + qldw + [(aw_sem, ao_ld)]
                                 ) if idx == 0 else []
                            P.emit("tensor", "matmul",
                                   scb.ap[:, idx * 512:(idx + 1) * 512],
                                   lhsT, rhs, start=True, stop=True,
                                   skip_group_check=True, waits=w,
                                   incs=[(scb.prod, 1)] if idx == 1 else [])
                        slot = pring[pr_uses % 3]
                        pw = [(pf_sem, pr_uses - 2)] if pr_uses >= 3 else []
                        diag = (kk1 >= 4 * p)
                        P.emit("scalar", "activation", slot[:], scb.ap, AF.Exp,
                               scale=cpool[:, C_EXPA:C_EXPA + 1],
                               waits=list(ev_w) + pw,
                               incs=([(scb.cons, 1), (ae, 1)] if diag else
                                     [(scb.cons, 1), (pr_sem, 1)]))
                        if diag:
                            expd = ae.n
                            first = True
                            for idx, kk in enumerate((kk0, kk1)):
                                if kk < 4 * p:
                                    continue
                                off = (kk - 4 * p) * 128
                                base = idx * 512
                                if off > 0:
                                    P.emit("vector", "memset",
                                           slot[:, base:base + off], 0.0,
                                           waits=[(ae, expd)] if first else [])
                                    first = False
                                P.emit("vector", "tensor_tensor",
                                       slot[:, base + off:base + off + 128],
                                       slot[:, base + off:base + off + 128],
                                       stair[:], ALU.mult,
                                       waits=[(ae, expd)] if first else [],
                                       incs=[(ve, 1)])
                                first = False
                            P.emit("vector", "nop", waits=[(ve, ve.n)],
                                   incs=[(pr_sem, 1)])
                        pr_uses += 1
                        if dbg.endswith("_sc"):
                            P.emit("vector", "nop", waits=[(pr_sem, pr_uses)],
                                   incs=[(pf_sem, 1)])
                            continue
                        av = mk_av(h, bO, slot, kk0, kk1, pr_uses, o_pe_w,
                                   j == 0, j == nkk // 2 - 1)
                        if pipe:
                            flush_av()
                            if j == 0:
                                emit_norm()
                            pending_av[0] = av
                        else:
                            av()
                            if j == 0:
                                emit_norm()
                    if dbg.endswith("_sc"):
                        continue
                    if dbg.endswith("_av"):
                        flush_av()
                        P.emit("vector", "tensor_copy", bcs[:],
                               bO.ap[0:64, :], waits=o_ev_w,
                               incs=[(bO.cons, 1)])
                        continue
                    pending_norm[0] = (h, o_ev_w)
                flush_av()
                emit_norm()
                aoT_done = ve.n
                if dbg.endswith(("_sc", "_av", "_nr")):
                    P.emit("tensor", "nop",
                           waits=[(bSC[0].prod, bSC[0].uses),
                                  (bSC[1].prod, bSC[1].uses),
                                  (bO.prod, bO.uses)],
                           incs=[(te, 1)])
                    panel_rec.append(te.n)
                    continue
                for qs in range(4):
                    t = 4 * p + qs
                    xslot = xq[qs % 2]
                    xw = [(ve, ve.n)] if (p * 4 + qs) >= 2 else []
                    lw = slot_load(f"xq{qs % 2}", "sync", xslot[:],
                                   xsrc[:, t, :], waits=xw)
                    halves = []
                    for mb in range(2):
                        pairs = [
                            (aoT[:, vt * 512 + qs * 128:
                                 vt * 512 + qs * 128 + 128],
                             aw[:, vt * D + mb * 512: vt * D + (mb + 1) * 512])
                            for vt in range(DT)]
                        evw = mm_chain(bM, pairs,
                                       extra_pe_waits=[(ve, aoT_done)])
                        halves.append((bM.ap, evw, [(bM.cons, 1)], 1.0))
                    residual_store(k, t, halves, xslot, xo, lw, dumpf)
                P.emit("tensor", "nop",
                       waits=[(bM.prod, bM.uses)], incs=[(te, 1)])
                panel_rec.append(te.n)
            emit_cast_units(10 ** 9)
            barrier()

        def phase_ffn(k):
            xsrc = sub_src[k]
            stat_w = stats_derive(first=False)
            bPT = Bank(P, ptp[:], "pt")
            b1 = [Bank(P, pA[:, 0:512], "pA0"),
                  Bank(P, pA[:, 512:1024], "pA1")]
            b2 = [Bank(P, pB[:, 0:512], "pB0"),
                  Bank(P, pB[:, 512:1024], "pB1")]
            bTD = Bank(P, pD[:], "pD")
            bTC = Bank(P, pC[:, 0:512], "pC0")
            if "f_wf" not in slot_sems:
                slot_sems["f_wf"] = P.sem("f_wf")
                slot_sems["f_w12f"] = P.sem("f_w12f")
            wfree = slot_sems["f_wf"]
            w12free = slot_sems["f_w12f"]
            if not hasattr(P, "wload_n"):
                P.wload_n = [0]
                P.w12load_n = [0]
            wload_n = P.wload_n
            w12load_n = P.w12load_n
            rec_ln = 0
            rec_tr = 0
            dbgv = _CONFIG.get("dbg") or ""
            n_pan = 4
            stop_stage = None
            if dbgv.startswith("fp"):
                parts = dbgv[2:].split("_")
                n_pan = int(parts[0])
                if len(parts) > 1:
                    stop_stage = int(parts[1][1:])
            for p in range(n_pan):
                for tl in range(8):
                    t = 8 * p + tl
                    xslot = xtf[tl % 2]
                    xw = [(ve, ve.n)] if (t >= 2 or p > 0) else []
                    lw = slot_load(f"xtf{tl % 2}", "sync", xslot[:],
                                   xsrc[:, t, :], waits=xw)
                    ae_n = ln_tile(xslot[:], t, lnbf, stat_w, lw, bPT)
                    dest = lnT_f[:].rearrange("p (a s) -> p a s", a=DT)[
                        :, :, tl * 128:(tl + 1) * 128]
                    transpose_tile(lnbf, dest, bPT, ae_n,
                                   [(te, rec_tr)] if p > 0 else [])
                lnT_done = ve.n
                for ot in range(HT):
                    wldw = []
                    for wi, wdram in enumerate((wsc_w1, wsc_w2)):
                        i = w12load_n[0]
                        w12load_n[0] += 1
                        slot = w12r[i % 4]
                        wfw = [(w12free, i - 3)] if i >= 4 else []
                        wldw += slot_load(f"w12_{i % 4}", "gpsimd", slot[:],
                                          wdram[ot], waits=wfw)
                    s1b = w12r[(w12load_n[0] - 2) % 4]
                    s2b = w12r[(w12load_n[0] - 1) % 4]
                    evws = []
                    for wb, banks in ((s1b, b1), (s2b, b2)):
                        triples = [
                            (wb[:, it * 128:(it + 1) * 128],
                             lnT_f[:, it * 1024: it * 1024 + 512],
                             lnT_f[:, it * 1024 + 512: it * 1024 + 1024])
                            for it in range(DT)]
                        ev0, ev1 = mm_chain2(banks[0], banks[1], triples,
                                             extra_pe_waits=wldw +
                                             [(ve, lnT_done)])
                        evws.append((banks, ev0, ev1))
                    (bk1s, evw1a, evw1b), (bk2s, evw2a, evw2b) = evws
                    evws = [(bk1s[0], evw1a, bk2s[0], evw2a),
                            (bk1s[1], evw1b, bk2s[1], evw2b)]
                    P.emit("tensor", "nop",
                           waits=[(b1[0].prod, b1[0].uses),
                                  (b1[1].prod, b1[1].uses),
                                  (b2[0].prod, b2[0].uses),
                                  (b2[1].prod, b2[1].uses)],
                           incs=[(w12free, 2)])
                    sl = silub[:, (ot % 2) * 1024:(ot % 2) * 1024 + 1024]
                    for sb_i, (bk1, evw1, bk2, evw2) in enumerate(evws):
                        P.emit("scalar", "activation",
                               sl[:, sb_i * 512:(sb_i + 1) * 512], bk1.ap,
                               AF.Silu, scale=s1c[:, ot:ot + 1],
                               waits=list(evw1) + [(ve, ve.n)],
                               incs=[(bk1.cons, 1), (ae, 1)])
                        P.emit("vector", "scalar_tensor_tensor",
                               aT[:, ot * 1024 + sb_i * 512:
                                  ot * 1024 + (sb_i + 1) * 512],
                               bk2.ap, s2c[:, ot:ot + 1],
                               sl[:, sb_i * 512:(sb_i + 1) * 512],
                               ALU.mult, ALU.mult,
                               waits=list(evw2) + [(ae, ae.n)],
                               incs=[(bk2.cons, 1), (ve, 1)])
                P.emit("tensor", "nop",
                       waits=[(b1[0].prod, b1[0].uses),
                              (b1[1].prod, b1[1].uses),
                              (b2[0].prod, b2[0].uses),
                              (b2[1].prod, b2[1].uses)],
                       incs=[(te, 1)])
                rec_ln = te.n
                if _CONFIG.get("dbg") == "f1":
                    P.emit("vector", "nop", waits=[(ve, ve.n), (ae, ae.n)])
                    barrier()
                    return
                prev_done = ("ve", ve.n)
                if _CONFIG.get("dump") == "a2" and p == n_pan - 1:
                    dumpb(0, aT[:, 0:16384], [(ve, ve.n)])
                for si, (wdram, weng, src, dst, scl) in enumerate(
                        [(wsc_h1, "gpsimd", aT, bT, sh1c),
                         (wsc_h2, "gpsimd", bT, aT, sh2c)]):
                    for ot in range(HT):
                        slot = wring[ot % 2]
                        uses = wload_n[0]
                        wload_n[0] += 1
                        wfw = [(wfree, uses - 1)] if uses >= 2 else []
                        wldw = slot_load(f"wr{ot % 2}_{weng}", weng, slot[:],
                                         wdram[ot], waits=wfw)
                        banks = b1 if ot % 2 == 0 else b2
                        pd = [(ve, prev_done[1])] if prev_done[0] == "ve" \
                            else [(ae, prev_done[1])]
                        triples = [
                            (slot[:, it * 128:(it + 1) * 128],
                             src[:, it * 1024: it * 1024 + 512],
                             src[:, it * 1024 + 512: it * 1024 + 1024])
                            for it in range(HT)]
                        ev0, ev1 = mm_chain2(banks[0], banks[1], triples,
                                             extra_pe_waits=wldw + pd)
                        for sb_i, evw in ((0, ev0), (1, ev1)):
                            bk = banks[sb_i]
                            P.emit("scalar", "activation",
                                   dst[:, ot * 1024 + sb_i * 512:
                                       ot * 1024 + (sb_i + 1) * 512],
                                   bk.ap, AF.Gelu, scale=scl[:, ot:ot + 1],
                                   waits=evw, incs=[(bk.cons, 1), (ae, 1)])
                        P.emit("tensor", "nop",
                               waits=[(banks[0].prod, banks[0].uses),
                                      (banks[1].prod, banks[1].uses)],
                               incs=[(wfree, 1)])
                    if stop_stage == 2 + si and p == n_pan - 1:
                        P.emit("vector", "nop", waits=[(ve, ve.n), (ae, ae.n)])
                        barrier()
                        return
                    prev_done = ("ae", ae.n)
                s4banks = [b1[0], b1[1], b2[0], b2[1]]
                for sh in range(2):
                    for ot in range(DT):
                        slot = wring[ot % 2]
                        uses = wload_n[0]
                        wload_n[0] += 1
                        wfw = [(wfree, uses - 1)] if uses >= 2 else []
                        wldw = slot_load(f"wr{ot % 2}_gpsimd", "gpsimd",
                                         slot[:], wsc_w3[ot], waits=wfw)
                        bk = s4banks[ot % 4]
                        pairs = [
                            (slot[:, it * 128:(it + 1) * 128],
                             aT[:, it * 1024 + sh * 512:
                                it * 1024 + sh * 512 + 512])
                            for it in range(HT)]
                        evw = mm_chain(bk, pairs,
                                       extra_pe_waits=wldw +
                                       [(ae, prev_done[1])])
                        war = [(te, rec_tr)] if (p > 0 or sh > 0) else []
                        P.emit("vector", "tensor_scalar",
                               w3st[:, ot * 512:(ot + 1) * 512],
                               bk.ap, s3c[:, ot:ot + 1], None, ALU.mult,
                               waits=list(evw) + war,
                               incs=[(bk.cons, 1), (ve, 1)])
                        P.emit("tensor", "nop",
                               waits=[(bk.prod, bk.uses)],
                               incs=[(wfree, 1)])
                    w3st_done = ve.n
                    for tlh in range(4):
                        tl = sh * 4 + tlh
                        t = 8 * p + tl
                        xslot = xtf[tl % 2]
                        lw = slot_load(f"xtf{tl % 2}", "sync", xslot[:],
                                       xsrc[:, t, :], waits=[(ve, ve.n)])
                        halves = []
                        for half, bk in ((0, bTD), (1, bTC)):
                            pe_w, ev_w = bk.acquire()
                            for q in range(4):
                                dt = half * 4 + q
                                w = (list(pe_w) + [(ve, w3st_done)]) \
                                    if q == 0 else []
                                P.emit("tensor", "transpose",
                                       bk.ap[:, q * 128:(q + 1) * 128],
                                       w3st[:, dt * 512 + tlh * 128:
                                            dt * 512 + tlh * 128 + 128],
                                       idf[:], waits=w,
                                       incs=[(bk.prod, 1)] if q == 3 else [])
                            halves.append((bk.ap, ev_w, [(bk.cons, 1)], 1.0))
                        residual_store(k, t, halves, xslot, xof, lw,
                                       dumpf2)
                    P.emit("tensor", "nop",
                           waits=[(bTD.prod, bTD.uses),
                                  (bTC.prod, bTC.uses)],
                           incs=[(te, 1)])
                    rec_tr = te.n
            barrier()

        def phase_attn_b(k):
            xsrc = sub_src[k]
            stat_w = stats_derive(first=False)
            bPT = Bank(P, ptp[:], "pt")
            bA0 = Bank(P, pA[:, 0:512], "pA0")
            bA1 = Bank(P, pA[:, 512:1024], "pA1")
            bw_sem = P.sem("bw_sem")
            bw_free = P.sem("bw_free")
            sdma = P.sem("b_sdma")
            rec_ln3 = 0

            def trig(bk, evw, et, c2, sin_dst, cos_dst,
                     sin_war=(), cos_war=()):
                P.emit("vector", "tensor_scalar", theta[:], bk.ap,
                       invwl[:, et:et + 1], c2[:, et:et + 1],
                       ALU.mult, ALU.add,
                       waits=list(evw) + [(ge, ge.n), (ve, ve.n)],
                       incs=[(bk.cons, 1), (ve, 1)])
                th_v = ve.n
                for is_cos, dstap in ((0, sin_dst), (1, cos_dst)):
                    if is_cos:
                        P.emit("vector", "tensor_scalar", ubuf[:], theta[:],
                               INV2PI, 0.25, ALU.mult, ALU.add,
                               waits=[(ge, ge.n), (ve, th_v)],
                               incs=[(ve, 1)])
                    else:
                        P.emit("vector", "tensor_scalar", ubuf[:], theta[:],
                               INV2PI, None, ALU.mult,
                               waits=[(ge, ge.n), (ve, th_v)],
                               incs=[(ve, 1)])
                    uv = ve.n
                    P.emit("vector", "tensor_copy", kint[:], ubuf[:],
                           waits=[(ve, uv)], incs=[(ve, 1)])
                    P.emit("vector", "tensor_copy", kflt[:], kint[:],
                           waits=[(ve, ve.n)], incs=[(ve, 1)])
                    P.emit("vector", "scalar_tensor_tensor", rbuf[:], kflt[:],
                           -TWO_PI, theta[:], ALU.mult, ALU.add,
                           waits=[(ve, ve.n), (ae, ae.n)], incs=[(ve, 1)])
                    hi = (PI / 2) if is_cos else PI
                    lo = (-1.5 * PI) if is_cos else -PI
                    P.emit("vector", "tensor_scalar", ubuf[:], rbuf[:],
                           float(np.float32(hi)), None, ALU.is_gt,
                           waits=[(ve, ve.n), (ge, ge.n)], incs=[(ve, 1)])
                    P.emit("vector", "scalar_tensor_tensor", kflt[:], ubuf[:],
                           -TWO_PI, rbuf[:], ALU.mult, ALU.add,
                           waits=[(ve, ve.n)], incs=[(ve, 1)])
                    P.emit("vector", "tensor_scalar", ubuf[:], kflt[:],
                           float(np.float32(lo)), None, ALU.is_lt,
                           waits=[(ve, ve.n)], incs=[(ve, 1)])
                    P.emit("vector", "scalar_tensor_tensor", rbuf[:], ubuf[:],
                           TWO_PI, kflt[:], ALU.mult, ALU.add,
                           waits=[(ve, ve.n)], incs=[(ve, 1)])
                    rv = ve.n
                    kwargs = {}
                    war = list(cos_war if is_cos else sin_war)
                    if is_cos:
                        kwargs["bias"] = cpool[:, C_HALFPI:C_HALFPI + 1]
                    P.emit("scalar", "activation", dstap, rbuf[:], AF.Sin,
                           waits=[(ve, rv)] + war,
                           incs=[(ae, 1)], **kwargs)
                return ae.n

            for p in range(8):
                for tl in range(4):
                    t = 4 * p + tl
                    xp = xpanB[t % 2]
                    xw = [(ae, ae.n)] if t >= 2 else []
                    lw = slot_load(f"xpB{t % 2}", "sync", xp[:],
                                   xsrc[:, t, :], waits=xw)
                    ae_n = ln_tile(xp[:], t, lnb3, stat_w, lw, bPT)
                    dest = lnT_b[:].rearrange("p (a s) -> p a s", a=DT)[
                        :, :, tl * 128:(tl + 1) * 128]
                    transpose_tile(lnb3, dest, bPT, ae_n,
                                   [(te, rec_ln3)] if p > 0 else [])
                lnT_done = ve.n
                for wi, wdram in enumerate((wsc_bq, wsc_bk, wsc_bv)):
                    nw = 3 * p + wi
                    P.emit("gpsimd", "dma_start", out=bw[:], in_=wdram,
                           waits=[(bw_free, nw)] if nw > 0 else [],
                           incs=[(bw_sem, 16)])
                    wld = bw_sem.n
                    if wi < 2:
                        c2 = c2q if wi == 0 else c2k
                        for et in range(DT):
                            bk = bA0 if et % 2 == 0 else bA1
                            pairs = [
                                (bw[:, it * D + et * 128:
                                    it * D + et * 128 + 128],
                                 lnT_b[:, it * 512:(it + 1) * 512])
                                for it in range(DT)]
                            evw = mm_chain(bk, pairs,
                                           extra_pe_waits=[(bw_sem, wld),
                                                           (ve, lnT_done)])
                            if wi == 0:
                                sin_dst = trst[1][:, et * 512:(et + 1) * 512]
                                cos_dst = trst[0][:, et * 512:(et + 1) * 512]
                                sw, cw2 = slot_w("st_sq"), slot_w("st_cq")
                            elif p < 4:
                                sin_dst = sk1[:, et * 2048 + p * 512:
                                              et * 2048 + (p + 1) * 512]
                                cos_dst = ck1[:, et * 2048 + p * 512:
                                              et * 2048 + (p + 1) * 512]
                                sw, cw2 = [], []
                            else:
                                sin_dst = trst[3][:, et * 512:(et + 1) * 512]
                                cos_dst = trst[2][:, et * 512:(et + 1) * 512]
                                sw, cw2 = slot_w("st_sk"), slot_w("st_ck")
                            trig(bk, evw, et, c2, sin_dst, cos_dst,
                                 sin_war=sw, cos_war=cw2)
                        P.emit("tensor", "nop",
                               waits=[(bA0.prod, bA0.uses),
                                      (bA1.prod, bA1.uses)],
                               incs=[(bw_free, 1)])
                        if wi == 0:
                            slot_load("st_cq", "scalar",
                                      cq_s.rearrange("a p s -> p a s")[
                                          :, :, p * 512:(p + 1) * 512],
                                      trst[0][:].rearrange(
                                          "p (a s) -> p a s", a=DT),
                                      waits=[(ae, ae.n)])
                            slot_load("st_sq", "scalar",
                                      sq_s.rearrange("a p s -> p a s")[
                                          :, :, p * 512:(p + 1) * 512],
                                      trst[1][:].rearrange(
                                          "p (a s) -> p a s", a=DT),
                                      waits=[(ae, ae.n)])
                        elif p >= 4:
                            slot_load("st_ck", "scalar",
                                      ck2_s.rearrange("a p s -> p a s")[
                                          :, :, (p - 4) * 512:(p - 3) * 512],
                                      trst[2][:].rearrange(
                                          "p (a s) -> p a s", a=DT),
                                      waits=[(ae, ae.n)])
                            slot_load("st_sk", "scalar",
                                      sk2_s.rearrange("a p s -> p a s")[
                                          :, :, (p - 4) * 512:(p - 3) * 512],
                                      trst[3][:].rearrange(
                                          "p (a s) -> p a s", a=DT),
                                      waits=[(ae, ae.n)])
                    else:
                        for tl in range(4):
                            t = 4 * p + tl
                            for eb in range(2):
                                bk = bA0 if (2 * tl + eb) % 2 == 0 else bA1
                                pe_w, evw0 = bk.acquire()
                                pairs = [
                                    (lnT_b[:, it * 512 + tl * 128:
                                           it * 512 + tl * 128 + 128],
                                     bw[:, it * D + eb * 512:
                                        it * D + (eb + 1) * 512])
                                    for it in range(DT)]
                                for i, (lh, rh) in enumerate(pairs):
                                    w = (list(pe_w) + [(bw_sem, wld),
                                                       (ve, lnT_done)]
                                         ) if i == 0 else []
                                    P.emit("tensor", "matmul", bk.ap, lh, rh,
                                           start=(i == 0), stop=False,
                                           skip_group_check=True, waits=w)
                                P.emit("tensor", "matmul", bk.ap, ones_rb[:],
                                       bvrow_b[:, eb * 512:(eb + 1) * 512],
                                       start=False, stop=True,
                                       skip_group_check=True,
                                       incs=[(bk.prod, 1)])
                                if p < 4:
                                    dstv = v1r[:, t * D + eb * 512:
                                               t * D + (eb + 1) * 512]
                                    P.emit("vector", "tensor_copy", dstv,
                                           bk.ap, waits=evw0,
                                           incs=[(bk.cons, 1), (ve, 1)])
                                else:
                                    dstv = vst[tl % 2][:, eb * 512:
                                                       (eb + 1) * 512]
                                    P.emit("vector", "tensor_copy", dstv,
                                           bk.ap,
                                           waits=list(evw0) +
                                           slot_w(f"st_v{tl % 2}"),
                                           incs=[(bk.cons, 1), (ve, 1)])
                            if p >= 4:
                                slot_load(f"st_v{tl % 2}", "sync",
                                          v2_s[t - 16], vst[tl % 2][:],
                                          waits=[(ve, ve.n)])
                        P.emit("tensor", "nop",
                               waits=[(bA0.prod, bA0.uses),
                                      (bA1.prod, bA1.uses)],
                               incs=[(bw_free, 1), (te, 1)])
                        rec_ln3 = te.n
            barrier()

            bSC = [Bank(P, pA[:, 0:512], "pA0"),
                   Bank(P, pA[:, 512:1024], "pA1")]
            bAV = [Bank(P, pB[:, 0:512], "pB0"),
                   Bank(P, pB[:, 512:1024], "pB1"),
                   Bank(P, pC[:, 0:512], "pC0"),
                   Bank(P, pC[:, 512:1024], "pC1")]
            bDEN = Bank(P, pD[:, 0:4], "pD4")
            psem = P.sem("b_ps")
            pfree = P.sem("b_pf")
            rec3b = []
            po_store = [0] * 8
            for kh in range(2):
                kldw = []
                if kh == 1:
                    kw = [(te, rec3b[-1])]
                    kldw += slot_load(
                        "kck", "sync",
                        ck1[:].rearrange("p (a s) -> p a s", a=DT),
                        ck2_s.rearrange("a p s -> p a s"), waits=kw)
                    kldw += slot_load(
                        "ksk", "sync",
                        sk1[:].rearrange("p (a s) -> p a s", a=DT),
                        sk2_s.rearrange("a p s -> p a s"), waits=kw)
                    kldw += slot_load(
                        "kv1", "sync",
                        v1r[:].rearrange("p (t d) -> p t d", t=NT // 2),
                        v2_s.rearrange("t p d -> p t d"), waits=kw)
                for p in range(8):
                    idx = kh * 8 + p
                    cqs = cqp[idx % 2]
                    sqs = sqp[idx % 2]
                    qw = [(te, rec3b[idx - 2])] if idx >= 2 else []
                    if idx == 0:
                        qw = qw + slot_w("st_cq") + slot_w("st_sq")
                    qldw = slot_load(
                        f"cq{idx % 2}", "sync",
                        cqs[:].rearrange("p (a s) -> p a s", a=DT),
                        cq_s.rearrange("a p s -> p a s")[
                            :, :, p * 512:(p + 1) * 512], waits=qw)
                    qldw += slot_load(
                        f"sq{idx % 2}", "sync",
                        sqs[:].rearrange("p (a s) -> p a s", a=DT),
                        sq_s.rearrange("a p s -> p a s")[
                            :, :, p * 512:(p + 1) * 512], waits=qw)
                    poldw = []
                    if kh == 1:
                        poldw = slot_load(
                            "pin", "sync",
                            poin[:].rearrange("p (q d) -> p q d", q=4),
                            po_s[p].rearrange("p (q d) -> p q d", q=4),
                            waits=[(ve, ve.n)] + slot_w("pst"))
                        poldw += slot_load("din", "sync", denin[:],
                                           pd_s[p], waits=slot_w("pdst"))
                    d_pe_w, d_ev_w = bDEN.acquire()
                    for kk in range(16):
                        scb = bSC[kk % 2]
                        pairs = []
                        for et in range(DT):
                            pairs.append(
                                (ck1[:, et * 2048 + kk * 128:
                                     et * 2048 + kk * 128 + 128],
                                 cqs[:, et * 512:(et + 1) * 512]))
                            pairs.append(
                                (sk1[:, et * 2048 + kk * 128:
                                     et * 2048 + kk * 128 + 128],
                                 sqs[:, et * 512:(et + 1) * 512]))
                        evw = mm_chain(scb, pairs,
                                       extra_pe_waits=qldw + kldw)
                        pw = [(pfree, (idx - 1) * 16 + kk + 1)] if idx >= 1 \
                            else []
                        P.emit("scalar", "activation",
                               pbuf[:, kk * 512:(kk + 1) * 512], scb.ap,
                               AF.Exp, scale=cpool[:, C_EXPB:C_EXPB + 1],
                               waits=list(evw) + pw,
                               incs=[(scb.cons, 1), (psem, 1)])
                        for qs in range(4):
                            w = [(psem, psem.n)]
                            if kk == 0 and qs == 0:
                                w = w + d_pe_w
                            last_d = (kk == 15 and qs == 3)
                            P.emit("tensor", "matmul",
                                   bDEN.ap[:, qs:qs + 1],
                                   pbuf[:, kk * 512 + qs * 128:
                                        kk * 512 + (qs + 1) * 128],
                                   ones_cb[:], start=(kk == 0),
                                   stop=(kk == 15), skip_group_check=True,
                                   waits=w,
                                   incs=[(bDEN.prod, 1)] if last_d else [])
                    av_start = [(psem, psem.n)]
                    ev_by_qs = [[None, None] for _ in range(4)]
                    for eh in range(2):
                        for qs in range(4):
                            bk = bAV[qs]
                            pe_w, evw = bk.acquire()
                            ev_by_qs[qs][eh] = (bk, evw)
                            for kk in range(16):
                                w = (list(pe_w) + av_start) if kk == 0 else []
                                last = (kk == 15)
                                P.emit("tensor", "matmul", bk.ap,
                                       pbuf[:, kk * 512 + qs * 128:
                                            kk * 512 + (qs + 1) * 128],
                                       v1r[:, kk * D + eh * 512:
                                           kk * D + (eh + 1) * 512],
                                       start=(kk == 0), stop=last,
                                       skip_group_check=True, waits=w,
                                       incs=[(bk.prod, 1)] if last else [])
                            dsl = post[:, qs * 1024 + eh * 512:
                                       qs * 1024 + (eh + 1) * 512]
                            if kh == 0:
                                P.emit("vector", "tensor_copy", dsl, bk.ap,
                                       waits=list(evw) + slot_w("pst"),
                                       incs=[(bk.cons, 1), (ve, 1)])
                            else:
                                P.emit("vector", "tensor_tensor", dsl, bk.ap,
                                       poin[:, qs * 1024 + eh * 512:
                                            qs * 1024 + (eh + 1) * 512],
                                       ALU.add,
                                       waits=list(evw) + poldw,
                                       incs=[(bk.cons, 1), (ve, 1)])
                    post_v = ve.n
                    if kh == 0:
                        P.emit("vector", "tensor_copy", denst[:], bDEN.ap,
                               waits=d_ev_w, incs=[(bDEN.cons, 1), (ve, 1)])
                        dv = ve.n
                        slot_load("pst", "sync",
                                  po_s[p].rearrange("p (q d) -> p q d", q=4),
                                  post[:].rearrange("p (q d) -> p q d", q=4),
                                  waits=[(ve, post_v)])
                        slot_load("pdst", "sync", pd_s[p], denst[:],
                                  waits=[(ve, dv)])
                    else:
                        P.emit("vector", "tensor_tensor", denst[:], bDEN.ap,
                               denin[:], ALU.add,
                               waits=list(d_ev_w) + poldw + [(ve, ve.n)],
                               incs=[(ve, 1)])
                        P.emit("vector", "nop", waits=[(ve, ve.n)],
                               incs=[(bDEN.cons, 1)])
                        P.emit("vector", "reciprocal", drc[:], denst[:],
                               waits=[(ve, ve.n)], incs=[(ve, 1)])
                        drc_v = ve.n
                        for qs in range(4):
                            t = 4 * p + qs
                            xslot = xq3[qs % 2]
                            xw = [(ve, ve.n)] if (p * 4 + qs) >= 2 else []
                            lw = slot_load(f"xq3_{qs % 2}", "sync", xslot[:],
                                           xsrc[:, t, :], waits=xw)
                            halves = [
                                (post[:, qs * 1024:qs * 1024 + 512],
                                 [(ve, post_v), (ve, drc_v)], [],
                                 drc[:, qs:qs + 1]),
                                (post[:, qs * 1024 + 512:(qs + 1) * 1024],
                                 [(ve, post_v), (ve, drc_v)], [],
                                 drc[:, qs:qs + 1]),
                            ]
                            residual_store(k, t, halves, xslot, xo3,
                                           lw, dump3b)
                    P.emit("tensor", "nop",
                           waits=[(bAV[0].prod, bAV[0].uses),
                                  (bAV[1].prod, bAV[1].uses),
                                  (bAV[2].prod, bAV[2].uses),
                                  (bAV[3].prod, bAV[3].uses),
                                  (bDEN.prod, bDEN.uses)],
                           incs=[(pfree, 16), (te, 1)])
                    rec3b.append(te.n)
            barrier()

        if _CONFIG.get("dbg") == "p0":
            pass
        else:
            phase_attn_a(0)
        if n_sub >= 2:
            phase_ffn(1)
        if n_sub >= 3:
            phase_attn_b(2)
        if n_sub >= 4:
            phase_ffn(3)

        P.emit("sync", "nop", waits=[(dS, dS.n)])

        with nc.Block() as block:
            block.sync(lambda e: P.replay("sync", e))
            block.gpsimd(lambda e: P.replay("gpsimd", e))
            block.vector(lambda e: P.replay("vector", e))
            block.scalar(lambda e: P.replay("scalar", e))
            block.tensor(lambda e: P.replay("tensor", e))

    return nc


def _wT_sb(w):
    wt = np.ascontiguousarray(w.T)
    din, dout = wt.shape
    it = din // 128
    return np.ascontiguousarray(
        wt.reshape(it, 128, dout).transpose(1, 0, 2).reshape(128, it * dout))


def _w_slab(w, ot_tiles):
    wt = np.ascontiguousarray(w.T)
    din, dout = wt.shape
    itn = din // 128
    assert dout // 128 == ot_tiles
    a = wt.reshape(itn, 128, ot_tiles, 128).transpose(2, 1, 0, 3)
    return np.ascontiguousarray(a.reshape(ot_tiles, 128, itn * 128))


def _col128(v):
    return np.ascontiguousarray(v.reshape(-1, 128).T)


def kernel(**inputs):
    f32 = np.float32
    x = np.ascontiguousarray(inputs["L0_attn_a_in_features"], dtype=f32)

    common = {}
    common["aq_w"] = _wT_sb(f32(inputs["L0_attn_a_q_proj_weight"]))
    common["ak_w"] = _wT_sb(f32(inputs["L0_attn_a_k_proj_weight"]))
    common["av_w"] = _wT_sb(f32(inputs["L0_attn_a_v_proj_weight"]))
    common["ao_w"] = _wT_sb(f32(inputs["L0_attn_a_o_proj_weight"]))
    common["bq_w"] = _wT_sb(f32(inputs["L0_attn_b_Wq"]))
    common["bk_w"] = _wT_sb(f32(inputs["L0_attn_b_Wk"]))
    common["bv_w"] = _wT_sb(f32(inputs["L0_attn_b_Wv"]))
    common["b_bq"] = _col128(f32(inputs["L0_attn_b_bq"]))
    common["b_bk"] = _col128(f32(inputs["L0_attn_b_bk"]))
    common["b_pb"] = _col128(f32(inputs["L0_attn_b_phase_bias"]))
    common["b_bv_row"] = f32(inputs["L0_attn_b_bv"]).reshape(1, D)
    common["f_w1"] = _w_slab(f32(inputs["L0_ffn_w1_w"]), HT)
    common["f_w2"] = _w_slab(f32(inputs["L0_ffn_w2_w"]), HT)
    common["f_h1"] = _w_slab(f32(inputs["L0_ffn_h1_w"]), HT)
    common["f_h2"] = _w_slab(f32(inputs["L0_ffn_h2_w"]), HT)
    common["f_w3"] = _w_slab(f32(inputs["L0_ffn_w3_w"]), DT)
    common["f_s1"] = _col128(f32(inputs["L0_ffn_w1_s"]))
    common["f_s2"] = _col128(f32(inputs["L0_ffn_w2_s"]))
    common["f_sh1"] = _col128(f32(inputs["L0_ffn_h1_s"]))
    common["f_sh2"] = _col128(f32(inputs["L0_ffn_h2_s"]))
    common["f_s3"] = _col128(f32(inputs["L0_ffn_w3_s"]))

    wavelengths = (np.arange(1, D + 1, dtype=np.float64)
                   * (2.0 * np.pi / D)).astype(f32)
    invwl = (1.0 / (wavelengths + np.float32(1e-8))).astype(f32)
    common["invwl_c"] = _col128(invwl)
    common["c_idb"] = np.eye(128, dtype=ml_dtypes.bfloat16)
    common["c_idf"] = np.eye(128, dtype=f32)
    common["c_stair"] = (np.arange(128)[None, :]
                         >= np.arange(128)[:, None]).astype(ml_dtypes.bfloat16)

    nc = build(_CONFIG["n_sub"])
    in_maps = []
    for b in range(8):
        m = dict(common)
        m["x"] = np.ascontiguousarray(x[b])
        in_maps.append(m)
    res = run_bass_kernel_spmd(nc, in_maps, core_ids=list(range(8)))
    out = np.stack([res.results[b]["y"] for b in range(8)], axis=0)
    return out


# version 11
# speedup vs baseline: 1.2871x; 1.0069x over previous
import sys
if "/opt/trn_rl_repo" not in sys.path:
    sys.path.insert(0, "/opt/trn_rl_repo")
from contextlib import ExitStack

import numpy as np
import ml_dtypes

import concourse.bass as bass
import concourse.mybir as mybir
from concourse.bass_utils import run_bass_kernel_spmd

F32 = mybir.dt.float32
BF16 = mybir.dt.bfloat16
I32 = mybir.dt.int32
AF = mybir.ActivationFunctionType
ALU = mybir.AluOpType
AX = mybir.AxisListType

S = 4096
D = 1024
H = 16
DH = 64
VG = 66
DHID = 4096
NT = S // 128
DT = D // 128
HT = DHID // 128
EPS = 1e-5
PI = float(np.float32(np.pi))
TWO_PI = float(np.float32(2 * np.pi))
INV2PI = float(np.float32(1.0 / TWO_PI))

_CONFIG = {"n_sub": 4}
_ENGINES = ("sync", "scalar", "vector", "tensor", "gpsimd")


class Sem:
    __slots__ = ("h", "n", "name")

    def __init__(self, st, nc, name):
        self.h = st.enter_context(nc.semaphore(name))
        self.n = 0
        self.name = name


class Prog:
    def __init__(self, nc, st):
        self.nc = nc
        self.st = st
        self.streams = {e: [] for e in _ENGINES}

    def sem(self, name):
        s = Sem(self.st, self.nc, name)
        if not hasattr(self, "all_sems"):
            self.all_sems = []
        self.all_sems.append(s)
        return s

    def emit(self, engine, method, *args, waits=(), incs=(), **kw):
        ser = _CONFIG.get("serial")
        if ser == "aw_ve" and engine == "scalar":
            waits = list(waits) + [(s, s.n) for s in self.all_sems
                                   if s.name == "ve" and s.n > 0]
        elif ser == "vw_ae" and engine == "vector":
            waits = list(waits) + [(s, s.n) for s in self.all_sems
                                   if s.name == "ae" and s.n > 0]
        elif ser and ser.startswith("eng:"):
            _, eng_n, ctr = ser.split(":")
            if engine == eng_n:
                waits = list(waits) + [(s, s.n) for s in self.all_sems
                                       if s.n > 0 and s.name == ctr]
        elif ser and ser not in ("aw_ve", "vw_ae"):
            wl = tuple(ser.split(","))
            waits = list(waits) + [(s, s.n) for s in self.all_sems
                                   if s.n > 0 and s.name in wl]
        lw = [(s.h, int(v)) for (s, v) in waits if v > 0]
        li = []
        for (s, k) in incs:
            s.n += k
            li.append((s.h, k, s.n))
        self.streams[engine].append((method, args, kw, lw, li))

    def replay(self, engine, eng):
        for m, args, kw, lw, li in self.streams[engine]:
            for h, v in lw:
                eng.wait_ge(h, v)
            ins = eng.nop() if m == "nop" else getattr(eng, m)(*args, **kw)
            for i, (h, k, tgt) in enumerate(li):
                if i == 0:
                    ins.then_inc(h, k)
                    if len(li) > 1 and m == "nop" and not lw:
                        pass
                elif i == 1 and m != "nop":
                    eng.wait_ge(li[0][0], li[0][2])
                    eng.nop().then_inc(h, k)
                else:
                    eng.nop().then_inc(h, k)


class Bank:

    _states = {}

    def __init__(self, P, ap, key):
        self.ap = ap
        st_ = Bank._states.get((id(P), key))
        if st_ is None:
            st_ = {"p": P.sem(key + "_p"), "c": P.sem(key + "_c"), "u": 0}
            Bank._states[(id(P), key)] = st_
        self._s = st_

    @property
    def prod(self):
        return self._s["p"]

    @property
    def cons(self):
        return self._s["c"]

    @property
    def uses(self):
        return self._s["u"]

    def acquire(self):
        pe_waits = [(self.cons, self.uses)] if self.uses > 0 else []
        self._s["u"] += 1
        return pe_waits, [(self.prod, self.uses)]


def _alias(nc, arena, name, shape, dtype, offset):
    base = nc.lookup_mloc(arena).addr
    return nc.alloc_sbuf_tensor_at(name, shape, dtype, offset=base + offset)


class ArenaGroup:
    def __init__(self, nc, arena_h, size, base=0):
        self.nc = nc
        self.arena_h = arena_h
        self.size = size
        self.off = base

    def alloc(self, name, shape, dtype):
        per_part = int(np.prod(shape[1:])) * mybir.dt.size(dtype)
        off = self.off
        self.off = (off + per_part + 127) & ~127
        assert self.off <= self.size, (name, self.off, self.size)
        if not hasattr(self, "offsets"):
            self.offsets = {}
        self.offsets[name] = off
        return _alias(self.nc, self.arena_h, name, shape, dtype, off)

    def alias_at(self, at_name, name, shape, dtype):
        off = self.offsets[at_name]
        per_part = int(np.prod(shape[1:])) * mybir.dt.size(dtype)
        assert off + per_part <= self.size
        return _alias(self.nc, self.arena_h, name, shape, dtype, off)


def build(n_sub=4):
    Bank._states.clear()
    n_sub = max(1, min(4, int(n_sub)))
    nc = bass.Bass("TRN2", target_bir_lowering=False,
                   detect_race_conditions=False)

    def din(name, shape, dtype=F32):
        return nc.dram_tensor(name, shape, dtype, kind="ExternalInput").ap()

    def dscr(name, shape, dtype=F32):
        return nc.dram_tensor(name, shape, dtype, kind="Internal").ap()

    x_in = din("x", [S, D])
    y_out = nc.dram_tensor("y", [S, D], F32, kind="ExternalOutput").ap()
    dbgf = nc.dram_tensor("dbgf", [128, 256], F32, kind="ExternalOutput").ap()
    dbgb = nc.dram_tensor("dbgb", [128, 16384], BF16,
                          kind="ExternalOutput").ap()

    aq_w = din("aq_w", [128, DT * D])
    ak_w = din("ak_w", [128, DT * D])
    av_w = din("av_w", [128, DT * D])
    ao_w = din("ao_w", [128, DT * D])
    bq_w = din("bq_w", [128, DT * D])
    bk_w = din("bk_w", [128, DT * D])
    bv_w = din("bv_w", [128, DT * D])
    b_bq = din("b_bq", [128, DT])
    b_bk = din("b_bk", [128, DT])
    b_pb = din("b_pb", [128, DT])
    b_bv_row = din("b_bv_row", [1, D])
    invwl_in = din("invwl_c", [128, DT])
    f_w1 = din("f_w1", [HT, 128, DT * 128])
    f_w2 = din("f_w2", [HT, 128, DT * 128])
    f_h1 = din("f_h1", [HT, 128, HT * 128])
    f_h2 = din("f_h2", [HT, 128, HT * 128])
    f_w3 = din("f_w3", [DT, 128, HT * 128])
    f_s1 = din("f_s1", [128, HT])
    f_s2 = din("f_s2", [128, HT])
    f_sh1 = din("f_sh1", [128, HT])
    f_sh2 = din("f_sh2", [128, HT])
    f_s3 = din("f_s3", [128, DT])
    c_idb = din("c_idb", [128, 128], BF16)
    c_idf = din("c_idf", [128, 128], F32)
    c_stair = din("c_stair", [128, 128], BF16)

    xs = dscr("xs", [S, D])
    qt_s = dscr("qt_s", [DT, 128, S], BF16)
    cq_s = dscr("cq_s", [DT, 128, S], BF16)
    sq_s = dscr("sq_s", [DT, 128, S], BF16)
    ck2_s = dscr("ck2_s", [DT, 128, S // 2], BF16)
    sk2_s = dscr("sk2_s", [DT, 128, S // 2], BF16)
    v2_s = dscr("v2_s", [NT // 2, 128, D], BF16)
    po_s = dscr("po_s", [8, 128, 4 * D])
    pd_s = dscr("pd_s", [8, 128, 4])
    wsc_w1 = dscr("wsc_w1", [HT, 128, DT * 128], BF16)
    wsc_w2 = dscr("wsc_w2", [HT, 128, DT * 128], BF16)
    wsc_h1 = dscr("wsc_h1", [HT, 128, HT * 128], BF16)
    wsc_h2 = dscr("wsc_h2", [HT, 128, HT * 128], BF16)
    wsc_w3 = dscr("wsc_w3", [DT, 128, HT * 128], BF16)
    wsc_bq = dscr("wsc_bq", [128, DT * D], BF16)
    wsc_bk = dscr("wsc_bk", [128, DT * D], BF16)
    wsc_bv = dscr("wsc_bv", [128, DT * D], BF16)

    st = ExitStack()
    with st:
        P = Prog(nc, st)

        def sb(name, shape, dtype):
            return st.enter_context(nc.sbuf_tensor(name, shape, dtype))

        idb = sb("idb", [128, 128], BF16)
        idf = sb("idf", [128, 128], F32)
        stair = sb("stair", [128, 128], BF16)
        cpool = sb("cpool", [128, 8], F32)
        C_EXPA, C_EXPB, C_HALFPI = 0, 1, 2
        invwl = sb("invwl", [128, DT], F32)
        bqc = sb("bqc", [128, DT], F32)
        bkc = sb("bkc", [128, DT], F32)
        pbcn = sb("pbcn", [128, DT], F32)
        c2q = sb("c2q", [128, DT], F32)
        c2k = sb("c2k", [128, DT], F32)
        bvrow = sb("bvrow", [1, D], F32)
        bvrow_b = sb("bvrow_b", [1, D], BF16)
        ones_rf = sb("ones_rf", [1, 64], F32)
        ones_rb = sb("ones_rb", [1, 128], BF16)
        ones_cb = sb("ones_cb", [128, 1], BF16)
        s1c = sb("s1c", [128, HT], F32)
        s2c = sb("s2c", [128, HT], F32)
        sh1c = sb("sh1c", [128, HT], F32)
        sh2c = sb("sh2c", [128, HT], F32)
        s3c = sb("s3c", [128, DT], F32)
        sxh = sb("sxh", [128, 2 * NT], F32)
        ssq = sb("ssq", [128, NT], F32)
        sxs = sb("sxs", [128, NT], F32)
        stA = sb("stA", [128, NT], F32)
        rstd = sb("rstd", [128, NT], F32)
        nmr = sb("nmr", [128, NT], F32)

        rem = nc.sbuf_bytes_remaining
        ARENA = rem & ~127
        arena_h = st.enter_context(
            nc.sbuf_tensor("arena", [128, ARENA], mybir.dt.uint8))

        gA = ArenaGroup(nc, arena_h, ARENA)
        ktr = gA.alloc("ktr", [128, DT * S], BF16)
        vres = gA.alloc("vres", [128, NT * H * VG + 128], BF16)
        aw = gA.alloc("aw", [128, DT * D], BF16)
        dumpf = gA.alloc("dumpf", [128, D], BF16)
        gA_tail = gA.off
        g1a = ArenaGroup(nc, arena_h, ARENA, base=gA_tail)
        lnT_a = g1a.alloc("lnT_a", [128, DT * 512], BF16)
        xpanA = [g1a.alloc(f"xpanA{i}", [128, D], F32) for i in range(2)]
        lnb = g1a.alloc("lnb", [128, D], BF16)
        qtst = g1a.alloc("qtst", [128, DT * 512], BF16)
        g1b = ArenaGroup(nc, arena_h, ARENA, base=gA_tail)
        qtp = [g1b.alloc(f"qtp{i}", [128, DT * 512], BF16) for i in range(2)]
        pring = [g1b.alloc(f"pring{i}", [128, 1024], BF16) for i in range(3)]
        aoT = g1b.alloc("aoT", [128, DT * 512], BF16)
        bcs = g1b.alloc("bcs", [64, 512], F32)
        rcp = g1b.alloc("rcp", [1, 512], F32)
        xq = [g1b.alloc(f"xq{i}", [128, D], F32) for i in range(2)]
        xo = g1b.alloc("xo", [128, D], F32)

        gF = ArenaGroup(nc, arena_h, ARENA)
        aT = gF.alloc("aT", [128, HT * 1024], BF16)
        bT = gF.alloc("bT", [128, HT * 1024], BF16)
        lnT_f = gF.alloc("lnT_f", [128, DT * 1024], BF16)
        wring = [gF.alloc(f"wring{i}", [128, HT * 128], BF16) for i in range(2)]
        w12r = [gF.alloc(f"w12r{i}", [128, DT * 128], BF16) for i in range(4)]
        silub = gF.alloc("silub", [128, 2048], F32)
        xtf = [gF.alloc(f"xtf{i}", [128, D], F32) for i in range(2)]
        xof = gF.alloc("xof", [128, D], F32)
        lnbf = gF.alloc("lnbf", [128, D], BF16)
        dumpf2 = gF.alloc("dumpf2", [128, D], BF16)
        w3st = gF.alias_at("lnT_f", "w3st", [128, DT * 512], F32)

        gB = ArenaGroup(nc, arena_h, ARENA)
        ck1 = gB.alloc("ck1", [128, DT * 2048], BF16)
        sk1 = gB.alloc("sk1", [128, DT * 2048], BF16)
        v1r = gB.alloc("v1r", [128, (NT // 2) * D], BF16)
        resid_end = gB.off
        g3a = ArenaGroup(nc, arena_h, ARENA, base=resid_end)
        lnT_b = g3a.alloc("lnT_b", [128, DT * 512], BF16)
        bw = g3a.alloc("bw", [128, DT * D], BF16)
        xpanB = [g3a.alloc(f"xpanB{i}", [128, D], F32) for i in range(2)]
        theta = g3a.alloc("theta", [128, 512], F32)
        ubuf = g3a.alloc("ubuf", [128, 512], F32)
        kint = g3a.alloc("kint", [128, 512], I32)
        kflt = g3a.alloc("kflt", [128, 512], F32)
        rbuf = g3a.alloc("rbuf", [128, 512], F32)
        trst = [g3a.alloc(f"trst{i}", [128, DT * 512], BF16) for i in range(4)]
        vst = [g3a.alloc(f"vst{i}", [128, D], BF16) for i in range(2)]
        lnb3 = g3a.alloc("lnb3", [128, D], BF16)
        g3b = ArenaGroup(nc, arena_h, ARENA, base=resid_end)
        pbuf = g3b.alloc("pbuf", [128, 16 * 512], BF16)
        cqp = [g3b.alloc(f"cqp{i}", [128, DT * 512], BF16) for i in range(2)]
        sqp = [g3b.alloc(f"sqp{i}", [128, DT * 512], BF16) for i in range(2)]
        post = g3b.alloc("post", [128, 4 * D], F32)
        poin = g3b.alloc("poin", [128, 4 * D], F32)
        denst = g3b.alloc("denst", [128, 4], F32)
        denin = g3b.alloc("denin", [128, 4], F32)
        drc = g3b.alloc("drc", [128, 4], F32)
        xq3 = [g3b.alloc(f"xq3{i}", [128, D], F32) for i in range(2)]
        xo3 = g3b.alloc("xo3", [128, D], F32)
        dump3b = g3b.alloc("dump3b", [128, D], BF16)

        pA = st.enter_context(nc.psum_tensor("pA", [128, 1024], F32))
        pB = st.enter_context(nc.psum_tensor("pB", [128, 1024], F32))
        pC = st.enter_context(nc.psum_tensor("pC", [128, 1024], F32))
        pD = st.enter_context(nc.psum_tensor("pD", [128, 512], F32))
        ptp = st.enter_context(nc.psum_tensor("ptp", [128, 1024], BF16))

        dX = P.sem("dX")
        dS = P.sem("dS")
        dQ = P.sem("dQ")
        ph = P.sem("ph")
        ve = P.sem("ve")
        ae = P.sem("ae")
        ge = P.sem("ge")
        te = P.sem("te")

        x_in_t = x_in.rearrange("(t p) d -> p t d", p=128)
        xs_t = xs.rearrange("(t p) d -> p t d", p=128)
        y_t = y_out.rearrange("(t p) d -> p t d", p=128)
        sub_src = [x_in_t, xs_t, xs_t, xs_t]

        slot_sems = {}

        def slot_load(name, engine, out, in_, waits=()):
            s = slot_sems.get(name)
            if s is None:
                s = P.sem("sl_" + name)
                slot_sems[name] = s
            P.emit(engine, "dma_start", out=out, in_=in_, waits=waits,
                   incs=[(s, 16)])
            return [(s, s.n)]

        def slot_w(name):
            s = slot_sems.get(name)
            return [(s, s.n)] if s is not None and s.n > 0 else []

        def barrier():
            ws = [(s, s.n) for s in P.all_sems if s is not ph and s.n > 0]
            P.emit("vector", "nop", waits=ws)
            for e in _ENGINES:
                P.emit(e, "nop", incs=[(ph, 1)])
            tgt = ph.n
            for e in _ENGINES:
                P.emit(e, "nop", waits=[(ph, tgt)])

        def mm_chain2(bankA, bankB, triples, extra_pe_waits=()):
            pe_wA, ev_wA = bankA.acquire()
            pe_wB, ev_wB = bankB.acquire()
            n = len(triples)
            for i, (lhsT, rhsA, rhsB) in enumerate(triples):
                w = (list(pe_wA) + list(pe_wB) + list(extra_pe_waits))                     if i == 0 else []
                P.emit("tensor", "matmul", bankA.ap, lhsT, rhsA,
                       start=(i == 0), stop=(i == n - 1),
                       skip_group_check=True, waits=w,
                       incs=[(bankA.prod, 1)] if i == n - 1 else [])
                P.emit("tensor", "matmul", bankB.ap, lhsT, rhsB,
                       start=(i == 0), stop=(i == n - 1),
                       skip_group_check=True,
                       incs=[(bankB.prod, 1)] if i == n - 1 else [])
            return ev_wA, ev_wB

        def mm_chain(bank, pairs, extra_pe_waits=(), out_ap=None):
            pe_w, ev_w = bank.acquire()
            out = bank.ap if out_ap is None else out_ap
            n = len(pairs)
            for i, (lhsT, rhs) in enumerate(pairs):
                w = (list(pe_w) + list(extra_pe_waits)) if i == 0 else []
                inc = [(bank.prod, 1)] if i == n - 1 else []
                P.emit("tensor", "matmul", out, lhsT, rhs,
                       start=(i == 0), stop=(i == n - 1),
                       skip_group_check=True, waits=w, incs=inc)
            return ev_w

        P.emit("vector", "memset", ones_rf[:], 1.0)
        P.emit("vector", "memset", ones_rb[:], 1.0)
        P.emit("vector", "memset", ones_cb[:], 1.0)
        P.emit("vector", "memset", cpool[:, 0:1], 0.125)
        P.emit("vector", "memset", cpool[:, 1:2], 1.0 / 32.0)
        P.emit("vector", "memset", cpool[:, 2:3], PI / 2)

        for dst, src in [(idb, c_idb), (idf, c_idf), (stair, c_stair),
                         (invwl, invwl_in), (bqc, b_bq), (bkc, b_bk),
                         (pbcn, b_pb), (bvrow, b_bv_row),
                         (s1c, f_s1), (s2c, f_s2), (sh1c, f_sh1),
                         (sh2c, f_sh2), (s3c, f_s3)]:
            cw = slot_load("const", "sync", dst[:], src)
        const_w = cw

        P.emit("vector", "tensor_tensor", c2q[:], bqc[:], invwl[:], ALU.mult,
               waits=const_w, incs=[(ve, 1)])
        P.emit("vector", "tensor_tensor", c2q[:], c2q[:], pbcn[:], ALU.add,
               waits=[(ve, ve.n)], incs=[(ve, 1)])
        P.emit("vector", "tensor_tensor", c2k[:], bkc[:], invwl[:], ALU.mult,
               incs=[(ve, 1)])
        P.emit("vector", "tensor_tensor", c2k[:], c2k[:], pbcn[:], ALU.add,
               waits=[(ve, ve.n)], incs=[(ve, 1)])
        P.emit("vector", "tensor_copy", bvrow_b[:], bvrow[:], incs=[(ve, 1)])

        cast_units = []
        for srcw, dstw in ((f_w1, wsc_w1), (f_w2, wsc_w2),
                           (f_h1, wsc_h1), (f_h2, wsc_h2),
                           (f_w3, wsc_w3)):
            for t_ in range(srcw.shape[0]):
                cast_units.append((srcw[t_], dstw[t_]))
        for srcw, dstw in ((bq_w, wsc_bq), (bk_w, wsc_bk), (bv_w, wsc_bv)):
            cast_units.append((srcw, dstw))
        cast_pos = [0]

        def emit_cast_units(n):
            for _ in range(n):
                i = cast_pos[0]
                if i >= len(cast_units):
                    return
                cast_pos[0] = i + 1
                sin_, dout = cast_units[i]
                slot_load("wcast", "gpsimd", dout, sin_)

        p0_done = []
        for t in range(NT):
            slot = xpanA[t % 2]
            wfree = list(p0_done[t - 2]) if t >= 2 else []
            lw = slot_load(f"xpA{t % 2}", "sync", slot[:, 0:D],
                           x_in_t[:, t, :], waits=wfree)
            emit_cast_units(3)
            P.emit("vector", "reduce_sum", sxs[:, t:t + 1], slot[:, 0:D],
                   AX.X, waits=lw, incs=[(ve, 1)])
            P.emit("scalar", "activation", dumpf[:], slot[:, 0:D], AF.Square,
                   accum_out=ssq[:, t:t + 1], waits=lw, incs=[(ae, 1)])
            p0_done.append([(ve, ve.n), (ae, ae.n)])
        barrier()

        def stats_derive(first):
            def vstep(method, *args, waits=(), **kw):
                P.emit("vector", method, *args,
                       waits=list(waits) + [(ve, ve.n)], incs=[(ve, 1)], **kw)
            if not first:
                vstep("tensor_tensor", sxs[:], sxh[:, 0:2 * NT:2],
                      sxh[:, 1:2 * NT:2], ALU.add)
            vstep("tensor_scalar", sxs[:], sxs[:], 1.0 / D, None, ALU.mult)
            vstep("tensor_tensor", stA[:], sxs[:], sxs[:], ALU.mult)
            vstep("scalar_tensor_tensor", stA[:], ssq[:], 1.0 / D,
                  stA[:], ALU.mult, ALU.subtract)
            vstep("tensor_scalar", stA[:], stA[:], EPS, None, ALU.add)
            P.emit("scalar", "activation", stA[:], stA[:], AF.Sqrt,
                   waits=[(ve, ve.n)], incs=[(ae, 1)])
            vstep("reciprocal", rstd[:], stA[:], waits=[(ae, ae.n)])
            vstep("tensor_tensor", nmr[:], sxs[:], rstd[:], ALU.mult)
            vstep("tensor_scalar", nmr[:], nmr[:], -1.0, None, ALU.mult)
            return ve.n

        def ln_tile(xsrc_ap, t, lnbuf, stat_w, x_wait, pt_bank):
            P.emit("scalar", "activation", lnbuf[:], xsrc_ap, AF.Identity,
                   bias=nmr[:, t:t + 1], scale=rstd[:, t:t + 1],
                   waits=[(ve, stat_w), (pt_bank.prod, pt_bank.uses)] + x_wait,
                   incs=[(ae, 1)])
            return ae.n

        def transpose_tile(lnbuf, dest_view, bank_pt, ae_wait, evac_extra):
            pe_w, ev_w = bank_pt.acquire()
            for dt in range(DT):
                w = (list(pe_w) + [(ae, ae_wait)]) if dt == 0 else []
                inc = [(bank_pt.prod, 1)] if dt == DT - 1 else []
                P.emit("tensor", "transpose", ptp[:, dt * 128:(dt + 1) * 128],
                       lnbuf[:, dt * 128:(dt + 1) * 128], idb[:],
                       waits=w, incs=inc)
            P.emit("vector", "tensor_copy", dest_view,
                   ptp[:].rearrange("p (a b) -> p a b", a=DT),
                   waits=list(ev_w) + list(evac_extra),
                   incs=[(bank_pt.cons, 1), (ve, 1)])
            return ve.n

        def residual_store(k, t, halves, xold_slot, xout_buf, x_wait, dump):
            last = (k == n_sub - 1)
            for j, (pap, w, extra_incs, scal) in enumerate(halves):
                incs = [(ve, 1)] + list(extra_incs)
                acc = {} if last else {
                    "accum_out": sxh[:, 2 * t + j:2 * t + j + 1]}
                war = [(dS, dS.n)] if j == 0 else [(ve, ve.n)]
                P.emit("vector", "scalar_tensor_tensor",
                       xout_buf[:, j * 512:(j + 1) * 512], pap, scal,
                       xold_slot[:, j * 512:(j + 1) * 512], ALU.mult, ALU.add,
                       waits=list(w) + list(x_wait) + war, incs=incs, **acc)
            vdone = ve.n
            aw_ = []
            if not last:
                P.emit("scalar", "activation", dump[:], xout_buf[:], AF.Square,
                       accum_out=ssq[:, t:t + 1], waits=[(ve, vdone)],
                       incs=[(ae, 1)])
                aw_ = [(ae, ae.n)]
            dst = y_t if last else xs_t
            P.emit("sync", "dma_start", out=dst[:, t, :], in_=xout_buf[:],
                   waits=[(ve, vdone)] + aw_, incs=[(dS, 16)])

        DUMP = _CONFIG.get("dump")

        def dumpb(col, ap, waits):
            P.emit("sync", "dma_start", out=dbgb[:, col:col + ap.shape[-1]],
                   in_=ap, waits=waits, incs=[(dS, 16)])

        def phase_attn_a(k):
            xsrc = sub_src[k]
            stat_w = stats_derive(first=(k == 0))
            if DUMP:
                P.emit("sync", "dma_start", out=dbgf[:, 0:32], in_=rstd[:],
                       waits=[(ve, stat_w)], incs=[(dS, 16)])
                P.emit("sync", "dma_start", out=dbgf[:, 32:64], in_=nmr[:],
                       waits=[(ve, stat_w)], incs=[(dS, 16)])
                P.emit("sync", "dma_start", out=dbgf[:, 64:96], in_=sxs[:],
                       waits=[(ve, stat_w)], incs=[(dS, 16)])
                P.emit("sync", "dma_start", out=dbgf[:, 96:128], in_=ssq[:],
                       waits=[(ve, stat_w)], incs=[(dS, 16)])
            bA0 = Bank(P, pA[:, 0:512], "pA0")
            bA1 = Bank(P, pA[:, 512:1024], "pA1")
            bPT = Bank(P, ptp[:], "pt")
            P.emit("vector", "memset", vres[:, 64:NT * H * VG:VG], 1.0)
            P.emit("vector", "memset", vres[:, 65:NT * H * VG:VG], 0.0)
            P.emit("vector", "memset",
                   vres[:, NT * H * VG:NT * H * VG + 128], 0.0,
                   incs=[(ve, 1)])
            vones_done = ve.n

            aw_sem = P.sem("aw_sem")
            aw_free = P.sem("aw_free")
            qdma = P.sem("qdma")
            lnT_free_rec = 0
            for p in range(8):
                for tl in range(4):
                    t = 4 * p + tl
                    xp = xpanA[t % 2]
                    xw = [(ae, ae.n)] if t >= 2 else []
                    lw = slot_load(f"xpA{t % 2}", "sync", xp[:],
                                   xsrc[:, t, :], waits=xw)
                    ae_n = ln_tile(xp[:], t, lnb, stat_w, lw, bPT)
                    dest = lnT_a[:].rearrange("p (a s) -> p a s", a=DT)[
                        :, :, tl * 128:(tl + 1) * 128]
                    transpose_tile(lnb, dest, bPT, ae_n,
                                   [(te, lnT_free_rec)] if p > 0 else [])
                lnT_done = ve.n
                for wi, wdram in enumerate((aq_w, ak_w, av_w)):
                    nw = 3 * p + wi
                    P.emit("gpsimd", "dma_start", out=aw[:], in_=wdram,
                           waits=[(aw_free, nw)] if nw > 0 else [],
                           incs=[(aw_sem, 16)])
                    wld = aw_sem.n
                    if wi < 2:
                        for et in range(DT):
                            bk = bA0 if et % 2 == 0 else bA1
                            pairs = [
                                (aw[:, it * D + et * 128: it * D + et * 128 + 128],
                                 lnT_a[:, it * 512:(it + 1) * 512])
                                for it in range(DT)]
                            evw = mm_chain(bk, pairs,
                                           extra_pe_waits=[(aw_sem, wld),
                                                           (ve, lnT_done)])
                            if wi == 0:
                                P.emit("vector", "tensor_copy",
                                       qtst[:, et * 512:(et + 1) * 512],
                                       bk.ap,
                                       waits=list(evw) + [(qdma, qdma.n)],
                                       incs=[(bk.cons, 1), (ve, 1)])
                            else:
                                P.emit("vector", "tensor_copy",
                                       ktr[:, et * S + p * 512:
                                           et * S + (p + 1) * 512], bk.ap,
                                       waits=evw,
                                       incs=[(bk.cons, 1), (ve, 1)])
                        if wi == 0:
                            P.emit("gpsimd", "dma_start",
                                   out=qt_s.rearrange("a p s -> p a s")[
                                       :, :, p * 512:(p + 1) * 512],
                                   in_=qtst[:].rearrange("p (a s) -> p a s",
                                                         a=DT),
                                   waits=[(ve, ve.n)], incs=[(qdma, 16)])
                        P.emit("tensor", "nop",
                               waits=[(bA0.prod, bA0.uses),
                                      (bA1.prod, bA1.uses)],
                               incs=[(aw_free, 1)])
                        emit_cast_units(2)
                    else:
                        for tl in range(4):
                            t = 4 * p + tl
                            for eb in range(2):
                                bk = bA0 if (2 * tl + eb) % 2 == 0 else bA1
                                pairs = [
                                    (lnT_a[:, it * 512 + tl * 128:
                                           it * 512 + tl * 128 + 128],
                                     aw[:, it * D + eb * 512:
                                        it * D + (eb + 1) * 512])
                                    for it in range(DT)]
                                evw = mm_chain(bk, pairs,
                                               extra_pe_waits=[(aw_sem, wld),
                                                               (ve, lnT_done)])
                                dv = vres[:, 0:NT * H * VG].rearrange(
                                    "p (kt h g) -> p kt h g", kt=NT, h=H)[
                                    :, t, eb * 8:(eb + 1) * 8, 0:64]
                                P.emit("vector", "tensor_copy", dv,
                                       bk.ap.rearrange("p (h r) -> p h r", h=8),
                                       waits=list(evw) + [(ve, vones_done)],
                                       incs=[(bk.cons, 1), (ve, 1)])
                        P.emit("tensor", "nop",
                               waits=[(bA0.prod, bA0.uses),
                                      (bA1.prod, bA1.uses)],
                               incs=[(aw_free, 1), (te, 1)])
                        lnT_free_rec = te.n
            p1a_v = ve.n
            p1a_q = qdma.n
            if DUMP:
                w = [(ve, p1a_v)]
                dumpb(0, lnT_a[:], w)
                dumpb(4096, ktr[:, 0:512], w)
                dumpb(4608, vres[:, 0:65], w)
                dumpb(5664, qtst[:], w)
            if _CONFIG.get("dbg") == "1a":
                P.emit("vector", "nop", waits=[(ve, p1a_v)])
                P.emit("scalar", "nop", waits=[(qdma, p1a_q)])
                barrier()
                return

            P.emit("gpsimd", "dma_start", out=aw[:], in_=ao_w,
                   waits=[(aw_free, aw_free.n)], incs=[(aw_sem, 16)])
            ao_ld = aw_sem.n
            bSC = [Bank(P, pA[:], "pAf"), Bank(P, pB[:], "pBf")]
            bOp = [Bank(P, pC[:, 0:512], "pC0"),
                   Bank(P, pC[:, 512:1024], "pC1")]
            bBC = Bank(P, pD[0:64, :], "pD")
            bM = Bank(P, pD[:], "pD")
            pr_sem = P.sem("a_pr")
            pf_sem = P.sem("a_pf")
            pr_uses = 0
            panel_rec = []
            dbg = _CONFIG.get("dbg") or ""
            base = dbg.split("_")[0]
            np_1b = int(base[2:]) if base.startswith("1b") and len(base) > 2 else 8
            for p in range(np_1b):
                qp = qtp[p % 2]
                qw = [(te, panel_rec[p - 2])] if p >= 2 else []
                qldw = slot_load(
                    f"qtp{p % 2}", "sync",
                    qp[:].rearrange("p (a s) -> p a s", a=DT),
                    qt_s.rearrange("a p s -> p a s")[
                        :, :, p * 512:(p + 1) * 512],
                    waits=qw + ([(qdma, p1a_q), (ve, p1a_v)]
                                if p == 0 else []))
                emit_cast_units(12 if p > 0 else 0)
                nkk = 4 * (p + 1)
                pending_norm = [None]

                def emit_norm(p=p):
                    if pending_norm[0] is None:
                        return
                    h_, o_ev_w_ = pending_norm[0]
                    pending_norm[0] = None
                    bO_ = bOp[h_ % 2]
                    et_ = h_ // 2
                    ro_ = (h_ % 2) * 64
                    P.emit("vector", "reciprocal", rcp[:], bO_.ap[64:65, :],
                           waits=list(o_ev_w_) + [(bBC.prod, bBC.uses)],
                           incs=[(ve, 1)])
                    rcp_v = ve.n
                    bc_pe_w, bc_ev_w = bBC.acquire()
                    P.emit("tensor", "matmul", bBC.ap[0:64, :], ones_rf[:],
                           rcp[:], start=True, stop=True,
                           skip_group_check=True,
                           waits=list(bc_pe_w) + [(ve, rcp_v)],
                           incs=[(bBC.prod, 1)])
                    P.emit("vector", "tensor_copy", bcs[:], bBC.ap[0:64, :],
                           waits=list(bc_ev_w) + [(ve, ve.n)],
                           incs=[(bBC.cons, 1), (ve, 1)])
                    aoT_w = [(te, panel_rec[p - 1])] if p >= 1 else []
                    P.emit("vector", "tensor_tensor",
                           aoT[ro_:ro_ + 64, et_ * 512:(et_ + 1) * 512],
                           bO_.ap[0:64, :], bcs[:], ALU.mult,
                           waits=aoT_w + [(ve, ve.n)],
                           incs=[(bO_.cons, 1), (ve, 1)])

                pipe = not dbg.endswith(("_sc", "_av"))
                pending_av = [None]

                def flush_av():
                    if pending_av[0] is not None:
                        go = pending_av[0]
                        pending_av[0] = None
                        go()

                def mk_av(h, bO, slot, kk0, kk1, pr_t, o_pe_w,
                          is_first, is_last):
                    def go():
                        for idx, kk in enumerate((kk0, kk1)):
                            vap = vres[:, kk * (H * VG) + h * VG:
                                       kk * (H * VG) + h * VG + 128]
                            w = [(pr_sem, pr_t)]
                            if is_first and idx == 0:
                                w = w + list(o_pe_w)
                            last = (is_last and idx == 1)
                            inc = []
                            if last:
                                inc.append((bO.prod, 1))
                            if idx == 1:
                                inc.append((pf_sem, 1))
                            P.emit("tensor", "matmul", bO.ap[0:128, :], vap,
                                   slot[:, idx * 512:(idx + 1) * 512],
                                   start=(is_first and idx == 0), stop=last,
                                   skip_group_check=True, waits=w, incs=inc)
                    return go

                for h in range(H):
                    et = h // 2
                    ro = (h % 2) * 64
                    bO = bOp[h % 2]
                    o_pe_w, o_ev_w = bO.acquire()
                    for j in range(nkk // 2):
                        kk0, kk1 = 2 * j, 2 * j + 1
                        scb = bSC[j % 2]
                        pe_w, ev_w = scb.acquire()
                        for idx, kk in enumerate((kk0, kk1)):
                            lhsT = ktr[ro:ro + 64,
                                       et * S + kk * 128: et * S + kk * 128 + 128]
                            rhs = qp[ro:ro + 64, et * 512:(et + 1) * 512]
                            w = (list(pe_w) + qldw + [(aw_sem, ao_ld)]
                                 ) if idx == 0 else []
                            P.emit("tensor", "matmul",
                                   scb.ap[:, idx * 512:(idx + 1) * 512],
                                   lhsT, rhs, start=True, stop=True,
                                   skip_group_check=True, waits=w,
                                   incs=[(scb.prod, 1)] if idx == 1 else [])
                        slot = pring[pr_uses % 3]
                        pw = [(pf_sem, pr_uses - 2)] if pr_uses >= 3 else []
                        diag = (kk1 >= 4 * p)
                        P.emit("scalar", "activation", slot[:], scb.ap, AF.Exp,
                               scale=cpool[:, C_EXPA:C_EXPA + 1],
                               waits=list(ev_w) + pw,
                               incs=([(scb.cons, 1), (ae, 1)] if diag else
                                     [(scb.cons, 1), (pr_sem, 1)]))
                        if diag:
                            expd = ae.n
                            first = True
                            for idx, kk in enumerate((kk0, kk1)):
                                if kk < 4 * p:
                                    continue
                                off = (kk - 4 * p) * 128
                                base = idx * 512
                                if off > 0:
                                    P.emit("vector", "memset",
                                           slot[:, base:base + off], 0.0,
                                           waits=[(ae, expd)] if first else [])
                                    first = False
                                P.emit("vector", "tensor_tensor",
                                       slot[:, base + off:base + off + 128],
                                       slot[:, base + off:base + off + 128],
                                       stair[:], ALU.mult,
                                       waits=[(ae, expd)] if first else [],
                                       incs=[(ve, 1)])
                                first = False
                            P.emit("vector", "nop", waits=[(ve, ve.n)],
                                   incs=[(pr_sem, 1)])
                        pr_uses += 1
                        if dbg.endswith("_sc"):
                            P.emit("vector", "nop", waits=[(pr_sem, pr_uses)],
                                   incs=[(pf_sem, 1)])
                            continue
                        av = mk_av(h, bO, slot, kk0, kk1, pr_uses, o_pe_w,
                                   j == 0, j == nkk // 2 - 1)
                        if pipe:
                            flush_av()
                            if j == 0:
                                emit_norm()
                            pending_av[0] = av
                        else:
                            av()
                            if j == 0:
                                emit_norm()
                    if dbg.endswith("_sc"):
                        continue
                    if dbg.endswith("_av"):
                        flush_av()
                        P.emit("vector", "tensor_copy", bcs[:],
                               bO.ap[0:64, :], waits=o_ev_w,
                               incs=[(bO.cons, 1)])
                        continue
                    pending_norm[0] = (h, o_ev_w)
                flush_av()
                emit_norm()
                aoT_done = ve.n
                if dbg.endswith(("_sc", "_av", "_nr")):
                    P.emit("tensor", "nop",
                           waits=[(bSC[0].prod, bSC[0].uses),
                                  (bSC[1].prod, bSC[1].uses),
                                  (bO.prod, bO.uses)],
                           incs=[(te, 1)])
                    panel_rec.append(te.n)
                    continue
                for qs in range(4):
                    t = 4 * p + qs
                    xslot = xq[qs % 2]
                    xw = [(ve, ve.n)] if (p * 4 + qs) >= 2 else []
                    lw = slot_load(f"xq{qs % 2}", "sync", xslot[:],
                                   xsrc[:, t, :], waits=xw)
                    halves = []
                    for mb in range(2):
                        pairs = [
                            (aoT[:, vt * 512 + qs * 128:
                                 vt * 512 + qs * 128 + 128],
                             aw[:, vt * D + mb * 512: vt * D + (mb + 1) * 512])
                            for vt in range(DT)]
                        evw = mm_chain(bM, pairs,
                                       extra_pe_waits=[(ve, aoT_done)])
                        halves.append((bM.ap, evw, [(bM.cons, 1)], 1.0))
                    residual_store(k, t, halves, xslot, xo, lw, dumpf)
                P.emit("tensor", "nop",
                       waits=[(bM.prod, bM.uses)], incs=[(te, 1)])
                panel_rec.append(te.n)
            emit_cast_units(10 ** 9)
            barrier()

        def phase_ffn(k):
            xsrc = sub_src[k]
            stat_w = stats_derive(first=False)
            bPT = Bank(P, ptp[:], "pt")
            b1 = [Bank(P, pA[:, 0:512], "pA0"),
                  Bank(P, pA[:, 512:1024], "pA1")]
            b2 = [Bank(P, pB[:, 0:512], "pB0"),
                  Bank(P, pB[:, 512:1024], "pB1")]
            bTD = Bank(P, pD[:], "pD")
            bTC = Bank(P, pC[:, 0:512], "pC0")
            if "f_wf" not in slot_sems:
                slot_sems["f_wf"] = P.sem("f_wf")
                slot_sems["f_w12f"] = P.sem("f_w12f")
            wfree = slot_sems["f_wf"]
            w12free = slot_sems["f_w12f"]
            if not hasattr(P, "wload_n"):
                P.wload_n = [0]
                P.w12load_n = [0]
            wload_n = P.wload_n
            w12load_n = P.w12load_n
            rec_ln = 0
            rec_tr = 0
            dbgv = _CONFIG.get("dbg") or ""
            n_pan = 4
            stop_stage = None
            if dbgv.startswith("fp"):
                parts = dbgv[2:].split("_")
                n_pan = int(parts[0])
                if len(parts) > 1:
                    stop_stage = int(parts[1][1:])
            for p in range(n_pan):
                for tl in range(8):
                    t = 8 * p + tl
                    xslot = xtf[tl % 2]
                    xw = [(ve, ve.n)] if (t >= 2 or p > 0) else []
                    lw = slot_load(f"xtf{tl % 2}", "sync", xslot[:],
                                   xsrc[:, t, :], waits=xw)
                    ae_n = ln_tile(xslot[:], t, lnbf, stat_w, lw, bPT)
                    dest = lnT_f[:].rearrange("p (a s) -> p a s", a=DT)[
                        :, :, tl * 128:(tl + 1) * 128]
                    transpose_tile(lnbf, dest, bPT, ae_n,
                                   [(te, rec_tr)] if p > 0 else [])
                lnT_done = ve.n
                for ot in range(HT):
                    wldw = []
                    for wi, wdram in enumerate((wsc_w1, wsc_w2)):
                        i = w12load_n[0]
                        w12load_n[0] += 1
                        slot = w12r[i % 4]
                        wfw = [(w12free, i - 3)] if i >= 4 else []
                        wldw += slot_load(f"w12_{i % 4}", "gpsimd", slot[:],
                                          wdram[ot], waits=wfw)
                    s1b = w12r[(w12load_n[0] - 2) % 4]
                    s2b = w12r[(w12load_n[0] - 1) % 4]
                    evws = []
                    for wb, banks in ((s1b, b1), (s2b, b2)):
                        triples = [
                            (wb[:, it * 128:(it + 1) * 128],
                             lnT_f[:, it * 1024: it * 1024 + 512],
                             lnT_f[:, it * 1024 + 512: it * 1024 + 1024])
                            for it in range(DT)]
                        ev0, ev1 = mm_chain2(banks[0], banks[1], triples,
                                             extra_pe_waits=wldw +
                                             [(ve, lnT_done)])
                        evws.append((banks, ev0, ev1))
                    (bk1s, evw1a, evw1b), (bk2s, evw2a, evw2b) = evws
                    evws = [(bk1s[0], evw1a, bk2s[0], evw2a),
                            (bk1s[1], evw1b, bk2s[1], evw2b)]
                    P.emit("tensor", "nop",
                           waits=[(b1[0].prod, b1[0].uses),
                                  (b1[1].prod, b1[1].uses),
                                  (b2[0].prod, b2[0].uses),
                                  (b2[1].prod, b2[1].uses)],
                           incs=[(w12free, 2)])
                    sl = silub[:, (ot % 2) * 1024:(ot % 2) * 1024 + 1024]
                    for sb_i, (bk1, evw1, bk2, evw2) in enumerate(evws):
                        P.emit("scalar", "activation",
                               sl[:, sb_i * 512:(sb_i + 1) * 512], bk1.ap,
                               AF.Silu, scale=s1c[:, ot:ot + 1],
                               waits=list(evw1) + [(ve, ve.n)],
                               incs=[(bk1.cons, 1), (ae, 1)])
                        P.emit("vector", "scalar_tensor_tensor",
                               aT[:, ot * 1024 + sb_i * 512:
                                  ot * 1024 + (sb_i + 1) * 512],
                               bk2.ap, s2c[:, ot:ot + 1],
                               sl[:, sb_i * 512:(sb_i + 1) * 512],
                               ALU.mult, ALU.mult,
                               waits=list(evw2) + [(ae, ae.n)],
                               incs=[(bk2.cons, 1), (ve, 1)])
                P.emit("tensor", "nop",
                       waits=[(b1[0].prod, b1[0].uses),
                              (b1[1].prod, b1[1].uses),
                              (b2[0].prod, b2[0].uses),
                              (b2[1].prod, b2[1].uses)],
                       incs=[(te, 1)])
                rec_ln = te.n
                if _CONFIG.get("dbg") == "f1":
                    P.emit("vector", "nop", waits=[(ve, ve.n), (ae, ae.n)])
                    barrier()
                    return
                prev_done = ("ve", ve.n)
                if _CONFIG.get("dump") == "a2" and p == n_pan - 1:
                    dumpb(0, aT[:, 0:16384], [(ve, ve.n)])
                for si, (wdram, weng, src, dst, scl) in enumerate(
                        [(wsc_h1, "gpsimd", aT, bT, sh1c),
                         (wsc_h2, "gpsimd", bT, aT, sh2c)]):
                    for ot in range(HT):
                        slot = wring[ot % 2]
                        uses = wload_n[0]
                        wload_n[0] += 1
                        wfw = [(wfree, uses - 1)] if uses >= 2 else []
                        wldw = slot_load(f"wr{ot % 2}_{weng}", weng, slot[:],
                                         wdram[ot], waits=wfw)
                        banks = b1 if ot % 2 == 0 else b2
                        pd = [(ve, prev_done[1])] if prev_done[0] == "ve" \
                            else [(ae, prev_done[1])]
                        triples = [
                            (slot[:, it * 128:(it + 1) * 128],
                             src[:, it * 1024: it * 1024 + 512],
                             src[:, it * 1024 + 512: it * 1024 + 1024])
                            for it in range(HT)]
                        ev0, ev1 = mm_chain2(banks[0], banks[1], triples,
                                             extra_pe_waits=wldw + pd)
                        for sb_i, evw in ((0, ev0), (1, ev1)):
                            bk = banks[sb_i]
                            P.emit("scalar", "activation",
                                   dst[:, ot * 1024 + sb_i * 512:
                                       ot * 1024 + (sb_i + 1) * 512],
                                   bk.ap, AF.Gelu, scale=scl[:, ot:ot + 1],
                                   waits=evw, incs=[(bk.cons, 1), (ae, 1)])
                        P.emit("tensor", "nop",
                               waits=[(banks[0].prod, banks[0].uses),
                                      (banks[1].prod, banks[1].uses)],
                               incs=[(wfree, 1)])
                    if stop_stage == 2 + si and p == n_pan - 1:
                        P.emit("vector", "nop", waits=[(ve, ve.n), (ae, ae.n)])
                        barrier()
                        return
                    prev_done = ("ae", ae.n)
                s4banks = [b1[0], b1[1], b2[0], b2[1]]
                for sh in range(2):
                    for ot in range(DT):
                        slot = wring[ot % 2]
                        uses = wload_n[0]
                        wload_n[0] += 1
                        wfw = [(wfree, uses - 1)] if uses >= 2 else []
                        wldw = slot_load(f"wr{ot % 2}_gpsimd", "gpsimd",
                                         slot[:], wsc_w3[ot], waits=wfw)
                        bk = s4banks[ot % 4]
                        pairs = [
                            (slot[:, it * 128:(it + 1) * 128],
                             aT[:, it * 1024 + sh * 512:
                                it * 1024 + sh * 512 + 512])
                            for it in range(HT)]
                        evw = mm_chain(bk, pairs,
                                       extra_pe_waits=wldw +
                                       [(ae, prev_done[1])])
                        war = [(te, rec_tr)] if (p > 0 or sh > 0) else []
                        P.emit("vector", "tensor_scalar",
                               w3st[:, ot * 512:(ot + 1) * 512],
                               bk.ap, s3c[:, ot:ot + 1], None, ALU.mult,
                               waits=list(evw) + war,
                               incs=[(bk.cons, 1), (ve, 1)])
                        P.emit("tensor", "nop",
                               waits=[(bk.prod, bk.uses)],
                               incs=[(wfree, 1)])
                    w3st_done = ve.n
                    for tlh in range(4):
                        tl = sh * 4 + tlh
                        t = 8 * p + tl
                        xslot = xtf[tl % 2]
                        lw = slot_load(f"xtf{tl % 2}", "sync", xslot[:],
                                       xsrc[:, t, :], waits=[(ve, ve.n)])
                        halves = []
                        for half, bk in ((0, bTD), (1, bTC)):
                            pe_w, ev_w = bk.acquire()
                            for q in range(4):
                                dt = half * 4 + q
                                w = (list(pe_w) + [(ve, w3st_done)]) \
                                    if q == 0 else []
                                P.emit("tensor", "transpose",
                                       bk.ap[:, q * 128:(q + 1) * 128],
                                       w3st[:, dt * 512 + tlh * 128:
                                            dt * 512 + tlh * 128 + 128],
                                       idf[:], waits=w,
                                       incs=[(bk.prod, 1)] if q == 3 else [])
                            halves.append((bk.ap, ev_w, [(bk.cons, 1)], 1.0))
                        residual_store(k, t, halves, xslot, xof, lw,
                                       dumpf2)
                    P.emit("tensor", "nop",
                           waits=[(bTD.prod, bTD.uses),
                                  (bTC.prod, bTC.uses)],
                           incs=[(te, 1)])
                    rec_tr = te.n
            barrier()

        def phase_attn_b(k):
            xsrc = sub_src[k]
            stat_w = stats_derive(first=False)
            bPT = Bank(P, ptp[:], "pt")
            bA0 = Bank(P, pA[:, 0:512], "pA0")
            bA1 = Bank(P, pA[:, 512:1024], "pA1")
            bw_sem = P.sem("bw_sem")
            bw_free = P.sem("bw_free")
            sdma = P.sem("b_sdma")
            rec_ln3 = 0

            def trig(bk, evw, et, c2, sin_dst, cos_dst,
                     sin_war=(), cos_war=()):
                P.emit("vector", "tensor_scalar", theta[:], bk.ap,
                       invwl[:, et:et + 1], c2[:, et:et + 1],
                       ALU.mult, ALU.add,
                       waits=list(evw) + [(ge, ge.n), (ve, ve.n)],
                       incs=[(bk.cons, 1), (ve, 1)])
                th_v = ve.n
                for is_cos, dstap in ((0, sin_dst), (1, cos_dst)):
                    if is_cos:
                        P.emit("vector", "tensor_scalar", ubuf[:], theta[:],
                               INV2PI, 0.25, ALU.mult, ALU.add,
                               waits=[(ge, ge.n), (ve, th_v)],
                               incs=[(ve, 1)])
                    else:
                        P.emit("vector", "tensor_scalar", ubuf[:], theta[:],
                               INV2PI, None, ALU.mult,
                               waits=[(ge, ge.n), (ve, th_v)],
                               incs=[(ve, 1)])
                    uv = ve.n
                    P.emit("vector", "tensor_copy", kint[:], ubuf[:],
                           waits=[(ve, uv)], incs=[(ve, 1)])
                    P.emit("vector", "tensor_copy", kflt[:], kint[:],
                           waits=[(ve, ve.n)], incs=[(ve, 1)])
                    P.emit("vector", "scalar_tensor_tensor", rbuf[:], kflt[:],
                           -TWO_PI, theta[:], ALU.mult, ALU.add,
                           waits=[(ve, ve.n), (ae, ae.n)], incs=[(ve, 1)])
                    hi = (PI / 2) if is_cos else PI
                    lo = (-1.5 * PI) if is_cos else -PI
                    P.emit("vector", "tensor_scalar", ubuf[:], rbuf[:],
                           float(np.float32(hi)), None, ALU.is_gt,
                           waits=[(ve, ve.n), (ge, ge.n)], incs=[(ve, 1)])
                    P.emit("vector", "scalar_tensor_tensor", kflt[:], ubuf[:],
                           -TWO_PI, rbuf[:], ALU.mult, ALU.add,
                           waits=[(ve, ve.n)], incs=[(ve, 1)])
                    P.emit("vector", "tensor_scalar", ubuf[:], kflt[:],
                           float(np.float32(lo)), None, ALU.is_lt,
                           waits=[(ve, ve.n)], incs=[(ve, 1)])
                    P.emit("vector", "scalar_tensor_tensor", rbuf[:], ubuf[:],
                           TWO_PI, kflt[:], ALU.mult, ALU.add,
                           waits=[(ve, ve.n)], incs=[(ve, 1)])
                    rv = ve.n
                    kwargs = {}
                    war = list(cos_war if is_cos else sin_war)
                    if is_cos:
                        kwargs["bias"] = cpool[:, C_HALFPI:C_HALFPI + 1]
                    P.emit("scalar", "activation", dstap, rbuf[:], AF.Sin,
                           waits=[(ve, rv)] + war,
                           incs=[(ae, 1)], **kwargs)
                return ae.n

            for p in range(8):
                for tl in range(4):
                    t = 4 * p + tl
                    xp = xpanB[t % 2]
                    xw = [(ae, ae.n)] if t >= 2 else []
                    lw = slot_load(f"xpB{t % 2}", "sync", xp[:],
                                   xsrc[:, t, :], waits=xw)
                    ae_n = ln_tile(xp[:], t, lnb3, stat_w, lw, bPT)
                    dest = lnT_b[:].rearrange("p (a s) -> p a s", a=DT)[
                        :, :, tl * 128:(tl + 1) * 128]
                    transpose_tile(lnb3, dest, bPT, ae_n,
                                   [(te, rec_ln3)] if p > 0 else [])
                lnT_done = ve.n
                for wi, wdram in enumerate((wsc_bq, wsc_bk, wsc_bv)):
                    nw = 3 * p + wi
                    P.emit("gpsimd", "dma_start", out=bw[:], in_=wdram,
                           waits=[(bw_free, nw)] if nw > 0 else [],
                           incs=[(bw_sem, 16)])
                    wld = bw_sem.n
                    if wi < 2:
                        c2 = c2q if wi == 0 else c2k
                        for et in range(DT):
                            bk = bA0 if et % 2 == 0 else bA1
                            pairs = [
                                (bw[:, it * D + et * 128:
                                    it * D + et * 128 + 128],
                                 lnT_b[:, it * 512:(it + 1) * 512])
                                for it in range(DT)]
                            evw = mm_chain(bk, pairs,
                                           extra_pe_waits=[(bw_sem, wld),
                                                           (ve, lnT_done)])
                            if wi == 0:
                                sin_dst = trst[1][:, et * 512:(et + 1) * 512]
                                cos_dst = trst[0][:, et * 512:(et + 1) * 512]
                                sw, cw2 = slot_w("st_sq"), slot_w("st_cq")
                            elif p < 4:
                                sin_dst = sk1[:, et * 2048 + p * 512:
                                              et * 2048 + (p + 1) * 512]
                                cos_dst = ck1[:, et * 2048 + p * 512:
                                              et * 2048 + (p + 1) * 512]
                                sw, cw2 = [], []
                            else:
                                sin_dst = trst[3][:, et * 512:(et + 1) * 512]
                                cos_dst = trst[2][:, et * 512:(et + 1) * 512]
                                sw, cw2 = slot_w("st_sk"), slot_w("st_ck")
                            trig(bk, evw, et, c2, sin_dst, cos_dst,
                                 sin_war=sw, cos_war=cw2)
                        P.emit("tensor", "nop",
                               waits=[(bA0.prod, bA0.uses),
                                      (bA1.prod, bA1.uses)],
                               incs=[(bw_free, 1)])
                        if wi == 0:
                            slot_load("st_cq", "scalar",
                                      cq_s.rearrange("a p s -> p a s")[
                                          :, :, p * 512:(p + 1) * 512],
                                      trst[0][:].rearrange(
                                          "p (a s) -> p a s", a=DT),
                                      waits=[(ae, ae.n)])
                            slot_load("st_sq", "scalar",
                                      sq_s.rearrange("a p s -> p a s")[
                                          :, :, p * 512:(p + 1) * 512],
                                      trst[1][:].rearrange(
                                          "p (a s) -> p a s", a=DT),
                                      waits=[(ae, ae.n)])
                        elif p >= 4:
                            slot_load("st_ck", "scalar",
                                      ck2_s.rearrange("a p s -> p a s")[
                                          :, :, (p - 4) * 512:(p - 3) * 512],
                                      trst[2][:].rearrange(
                                          "p (a s) -> p a s", a=DT),
                                      waits=[(ae, ae.n)])
                            slot_load("st_sk", "scalar",
                                      sk2_s.rearrange("a p s -> p a s")[
                                          :, :, (p - 4) * 512:(p - 3) * 512],
                                      trst[3][:].rearrange(
                                          "p (a s) -> p a s", a=DT),
                                      waits=[(ae, ae.n)])
                    else:
                        for tl in range(4):
                            t = 4 * p + tl
                            for eb in range(2):
                                bk = bA0 if (2 * tl + eb) % 2 == 0 else bA1
                                pe_w, evw0 = bk.acquire()
                                pairs = [
                                    (lnT_b[:, it * 512 + tl * 128:
                                           it * 512 + tl * 128 + 128],
                                     bw[:, it * D + eb * 512:
                                        it * D + (eb + 1) * 512])
                                    for it in range(DT)]
                                for i, (lh, rh) in enumerate(pairs):
                                    w = (list(pe_w) + [(bw_sem, wld),
                                                       (ve, lnT_done)]
                                         ) if i == 0 else []
                                    P.emit("tensor", "matmul", bk.ap, lh, rh,
                                           start=(i == 0), stop=False,
                                           skip_group_check=True, waits=w)
                                P.emit("tensor", "matmul", bk.ap, ones_rb[:],
                                       bvrow_b[:, eb * 512:(eb + 1) * 512],
                                       start=False, stop=True,
                                       skip_group_check=True,
                                       incs=[(bk.prod, 1)])
                                if p < 4:
                                    dstv = v1r[:, t * D + eb * 512:
                                               t * D + (eb + 1) * 512]
                                    P.emit("vector", "tensor_copy", dstv,
                                           bk.ap, waits=evw0,
                                           incs=[(bk.cons, 1), (ve, 1)])
                                else:
                                    dstv = vst[tl % 2][:, eb * 512:
                                                       (eb + 1) * 512]
                                    P.emit("vector", "tensor_copy", dstv,
                                           bk.ap,
                                           waits=list(evw0) +
                                           slot_w(f"st_v{tl % 2}"),
                                           incs=[(bk.cons, 1), (ve, 1)])
                            if p >= 4:
                                slot_load(f"st_v{tl % 2}", "sync",
                                          v2_s[t - 16], vst[tl % 2][:],
                                          waits=[(ve, ve.n)])
                        P.emit("tensor", "nop",
                               waits=[(bA0.prod, bA0.uses),
                                      (bA1.prod, bA1.uses)],
                               incs=[(bw_free, 1), (te, 1)])
                        rec_ln3 = te.n
            barrier()

            bSC = [Bank(P, pA[:, 0:512], "pA0"),
                   Bank(P, pA[:, 512:1024], "pA1")]
            bAV = [Bank(P, pB[:, 0:512], "pB0"),
                   Bank(P, pB[:, 512:1024], "pB1"),
                   Bank(P, pC[:, 0:512], "pC0"),
                   Bank(P, pC[:, 512:1024], "pC1")]
            bDEN = Bank(P, pD[:, 0:4], "pD4")
            psem = P.sem("b_ps")
            pfree = P.sem("b_pf")
            rec3b = []
            po_store = [0] * 8
            for kh in range(2):
                kldw = []
                if kh == 1:
                    kw = [(te, rec3b[-1])]
                    kldw += slot_load(
                        "kck", "sync",
                        ck1[:].rearrange("p (a s) -> p a s", a=DT),
                        ck2_s.rearrange("a p s -> p a s"), waits=kw)
                    kldw += slot_load(
                        "ksk", "sync",
                        sk1[:].rearrange("p (a s) -> p a s", a=DT),
                        sk2_s.rearrange("a p s -> p a s"), waits=kw)
                    kldw += slot_load(
                        "kv1", "sync",
                        v1r[:].rearrange("p (t d) -> p t d", t=NT // 2),
                        v2_s.rearrange("t p d -> p t d"), waits=kw)
                for p in range(8):
                    idx = kh * 8 + p
                    cqs = cqp[idx % 2]
                    sqs = sqp[idx % 2]
                    qw = [(te, rec3b[idx - 2])] if idx >= 2 else []
                    if idx == 0:
                        qw = qw + slot_w("st_cq") + slot_w("st_sq")
                    qldw = slot_load(
                        f"cq{idx % 2}", "sync",
                        cqs[:].rearrange("p (a s) -> p a s", a=DT),
                        cq_s.rearrange("a p s -> p a s")[
                            :, :, p * 512:(p + 1) * 512], waits=qw)
                    qldw += slot_load(
                        f"sq{idx % 2}", "sync",
                        sqs[:].rearrange("p (a s) -> p a s", a=DT),
                        sq_s.rearrange("a p s -> p a s")[
                            :, :, p * 512:(p + 1) * 512], waits=qw)
                    poldw = []
                    if kh == 1:
                        poldw = slot_load(
                            "pin", "sync",
                            poin[:].rearrange("p (q d) -> p q d", q=4),
                            po_s[p].rearrange("p (q d) -> p q d", q=4),
                            waits=[(ve, ve.n)] + slot_w("pst"))
                        poldw += slot_load("din", "sync", denin[:],
                                           pd_s[p], waits=slot_w("pdst"))
                    d_pe_w, d_ev_w = bDEN.acquire()
                    pending_den = [None]

                    def flush_den():
                        if pending_den[0] is not None:
                            go = pending_den[0]
                            pending_den[0] = None
                            go()

                    def mk_den(kk, ps_t):
                        def go():
                            for qs in range(4):
                                w = [(psem, ps_t)] if qs == 0 else []
                                if kk == 0 and qs == 0:
                                    w = w + list(d_pe_w)
                                last_d = (kk == 15 and qs == 3)
                                P.emit("tensor", "matmul",
                                       bDEN.ap[:, qs:qs + 1],
                                       pbuf[:, kk * 512 + qs * 128:
                                            kk * 512 + (qs + 1) * 128],
                                       ones_cb[:], start=(kk == 0),
                                       stop=(kk == 15),
                                       skip_group_check=True, waits=w,
                                       incs=([(bDEN.prod, 1)]
                                             if last_d else []))
                        return go

                    for kk in range(16):
                        scb = bSC[kk % 2]
                        pairs = []
                        for et in range(DT):
                            pairs.append(
                                (ck1[:, et * 2048 + kk * 128:
                                     et * 2048 + kk * 128 + 128],
                                 cqs[:, et * 512:(et + 1) * 512]))
                            pairs.append(
                                (sk1[:, et * 2048 + kk * 128:
                                     et * 2048 + kk * 128 + 128],
                                 sqs[:, et * 512:(et + 1) * 512]))
                        evw = mm_chain(scb, pairs,
                                       extra_pe_waits=qldw + kldw)
                        pw = [(pfree, (idx - 1) * 16 + kk + 1)] if idx >= 1 \
                            else []
                        P.emit("scalar", "activation",
                               pbuf[:, kk * 512:(kk + 1) * 512], scb.ap,
                               AF.Exp, scale=cpool[:, C_EXPB:C_EXPB + 1],
                               waits=list(evw) + pw,
                               incs=[(scb.cons, 1), (psem, 1)])
                        flush_den()
                        pending_den[0] = mk_den(kk, psem.n)
                    flush_den()
                    av_start = [(psem, psem.n)]
                    ev_by_qs = [[None, None] for _ in range(4)]
                    for eh in range(2):
                        for qs in range(4):
                            bk = bAV[qs]
                            pe_w, evw = bk.acquire()
                            ev_by_qs[qs][eh] = (bk, evw)
                            for kk in range(16):
                                w = (list(pe_w) + av_start) if kk == 0 else []
                                last = (kk == 15)
                                P.emit("tensor", "matmul", bk.ap,
                                       pbuf[:, kk * 512 + qs * 128:
                                            kk * 512 + (qs + 1) * 128],
                                       v1r[:, kk * D + eh * 512:
                                           kk * D + (eh + 1) * 512],
                                       start=(kk == 0), stop=last,
                                       skip_group_check=True, waits=w,
                                       incs=[(bk.prod, 1)] if last else [])
                            dsl = post[:, qs * 1024 + eh * 512:
                                       qs * 1024 + (eh + 1) * 512]
                            if kh == 0:
                                P.emit("vector", "tensor_copy", dsl, bk.ap,
                                       waits=list(evw) + slot_w("pst"),
                                       incs=[(bk.cons, 1), (ve, 1)])
                            else:
                                P.emit("vector", "tensor_tensor", dsl, bk.ap,
                                       poin[:, qs * 1024 + eh * 512:
                                            qs * 1024 + (eh + 1) * 512],
                                       ALU.add,
                                       waits=list(evw) + poldw,
                                       incs=[(bk.cons, 1), (ve, 1)])
                    post_v = ve.n
                    if kh == 0:
                        P.emit("vector", "tensor_copy", denst[:], bDEN.ap,
                               waits=d_ev_w, incs=[(bDEN.cons, 1), (ve, 1)])
                        dv = ve.n
                        slot_load("pst", "sync",
                                  po_s[p].rearrange("p (q d) -> p q d", q=4),
                                  post[:].rearrange("p (q d) -> p q d", q=4),
                                  waits=[(ve, post_v)])
                        slot_load("pdst", "sync", pd_s[p], denst[:],
                                  waits=[(ve, dv)])
                    else:
                        P.emit("vector", "tensor_tensor", denst[:], bDEN.ap,
                               denin[:], ALU.add,
                               waits=list(d_ev_w) + poldw + [(ve, ve.n)],
                               incs=[(ve, 1)])
                        P.emit("vector", "nop", waits=[(ve, ve.n)],
                               incs=[(bDEN.cons, 1)])
                        P.emit("vector", "reciprocal", drc[:], denst[:],
                               waits=[(ve, ve.n)], incs=[(ve, 1)])
                        drc_v = ve.n
                        for qs in range(4):
                            t = 4 * p + qs
                            xslot = xq3[qs % 2]
                            xw = [(ve, ve.n)] if (p * 4 + qs) >= 2 else []
                            lw = slot_load(f"xq3_{qs % 2}", "sync", xslot[:],
                                           xsrc[:, t, :], waits=xw)
                            halves = [
                                (post[:, qs * 1024:qs * 1024 + 512],
                                 [(ve, post_v), (ve, drc_v)], [],
                                 drc[:, qs:qs + 1]),
                                (post[:, qs * 1024 + 512:(qs + 1) * 1024],
                                 [(ve, post_v), (ve, drc_v)], [],
                                 drc[:, qs:qs + 1]),
                            ]
                            residual_store(k, t, halves, xslot, xo3,
                                           lw, dump3b)
                    P.emit("tensor", "nop",
                           waits=[(bAV[0].prod, bAV[0].uses),
                                  (bAV[1].prod, bAV[1].uses),
                                  (bAV[2].prod, bAV[2].uses),
                                  (bAV[3].prod, bAV[3].uses),
                                  (bDEN.prod, bDEN.uses)],
                           incs=[(pfree, 16), (te, 1)])
                    rec3b.append(te.n)
            barrier()

        if _CONFIG.get("dbg") == "p0":
            pass
        else:
            phase_attn_a(0)
        if n_sub >= 2:
            phase_ffn(1)
        if n_sub >= 3:
            phase_attn_b(2)
        if n_sub >= 4:
            phase_ffn(3)

        P.emit("sync", "nop", waits=[(dS, dS.n)])

        with nc.Block() as block:
            block.sync(lambda e: P.replay("sync", e))
            block.gpsimd(lambda e: P.replay("gpsimd", e))
            block.vector(lambda e: P.replay("vector", e))
            block.scalar(lambda e: P.replay("scalar", e))
            block.tensor(lambda e: P.replay("tensor", e))

    return nc


def _wT_sb(w):
    wt = np.ascontiguousarray(w.T)
    din, dout = wt.shape
    it = din // 128
    return np.ascontiguousarray(
        wt.reshape(it, 128, dout).transpose(1, 0, 2).reshape(128, it * dout))


def _w_slab(w, ot_tiles):
    wt = np.ascontiguousarray(w.T)
    din, dout = wt.shape
    itn = din // 128
    assert dout // 128 == ot_tiles
    a = wt.reshape(itn, 128, ot_tiles, 128).transpose(2, 1, 0, 3)
    return np.ascontiguousarray(a.reshape(ot_tiles, 128, itn * 128))


def _col128(v):
    return np.ascontiguousarray(v.reshape(-1, 128).T)


def kernel(**inputs):
    f32 = np.float32
    x = np.ascontiguousarray(inputs["L0_attn_a_in_features"], dtype=f32)

    common = {}
    common["aq_w"] = _wT_sb(f32(inputs["L0_attn_a_q_proj_weight"]))
    common["ak_w"] = _wT_sb(f32(inputs["L0_attn_a_k_proj_weight"]))
    common["av_w"] = _wT_sb(f32(inputs["L0_attn_a_v_proj_weight"]))
    common["ao_w"] = _wT_sb(f32(inputs["L0_attn_a_o_proj_weight"]))
    common["bq_w"] = _wT_sb(f32(inputs["L0_attn_b_Wq"]))
    common["bk_w"] = _wT_sb(f32(inputs["L0_attn_b_Wk"]))
    common["bv_w"] = _wT_sb(f32(inputs["L0_attn_b_Wv"]))
    common["b_bq"] = _col128(f32(inputs["L0_attn_b_bq"]))
    common["b_bk"] = _col128(f32(inputs["L0_attn_b_bk"]))
    common["b_pb"] = _col128(f32(inputs["L0_attn_b_phase_bias"]))
    common["b_bv_row"] = f32(inputs["L0_attn_b_bv"]).reshape(1, D)
    common["f_w1"] = _w_slab(f32(inputs["L0_ffn_w1_w"]), HT)
    common["f_w2"] = _w_slab(f32(inputs["L0_ffn_w2_w"]), HT)
    common["f_h1"] = _w_slab(f32(inputs["L0_ffn_h1_w"]), HT)
    common["f_h2"] = _w_slab(f32(inputs["L0_ffn_h2_w"]), HT)
    common["f_w3"] = _w_slab(f32(inputs["L0_ffn_w3_w"]), DT)
    common["f_s1"] = _col128(f32(inputs["L0_ffn_w1_s"]))
    common["f_s2"] = _col128(f32(inputs["L0_ffn_w2_s"]))
    common["f_sh1"] = _col128(f32(inputs["L0_ffn_h1_s"]))
    common["f_sh2"] = _col128(f32(inputs["L0_ffn_h2_s"]))
    common["f_s3"] = _col128(f32(inputs["L0_ffn_w3_s"]))

    wavelengths = (np.arange(1, D + 1, dtype=np.float64)
                   * (2.0 * np.pi / D)).astype(f32)
    invwl = (1.0 / (wavelengths + np.float32(1e-8))).astype(f32)
    common["invwl_c"] = _col128(invwl)
    common["c_idb"] = np.eye(128, dtype=ml_dtypes.bfloat16)
    common["c_idf"] = np.eye(128, dtype=f32)
    common["c_stair"] = (np.arange(128)[None, :]
                         >= np.arange(128)[:, None]).astype(ml_dtypes.bfloat16)

    nc = build(_CONFIG["n_sub"])
    in_maps = []
    for b in range(8):
        m = dict(common)
        m["x"] = np.ascontiguousarray(x[b])
        in_maps.append(m)
    res = run_bass_kernel_spmd(nc, in_maps, core_ids=list(range(8)))
    out = np.stack([res.results[b]["y"] for b in range(8)], axis=0)
    return out


# version 12
# speedup vs baseline: 1.3613x; 1.0576x over previous
import sys
if "/opt/trn_rl_repo" not in sys.path:
    sys.path.insert(0, "/opt/trn_rl_repo")
from contextlib import ExitStack

import numpy as np
import ml_dtypes

import concourse.bass as bass
import concourse.mybir as mybir
from concourse.bass_utils import run_bass_kernel_spmd

F32 = mybir.dt.float32
BF16 = mybir.dt.bfloat16
I32 = mybir.dt.int32
AF = mybir.ActivationFunctionType
ALU = mybir.AluOpType
AX = mybir.AxisListType

S = 4096
D = 1024
H = 16
DH = 64
VG = 66
DHID = 4096
NT = S // 128
DT = D // 128
HT = DHID // 128
EPS = 1e-5
PI = float(np.float32(np.pi))
TWO_PI = float(np.float32(2 * np.pi))
INV2PI = float(np.float32(1.0 / TWO_PI))

_CONFIG = {"n_sub": 4}
_ENGINES = ("sync", "scalar", "vector", "tensor", "gpsimd")


class Sem:
    __slots__ = ("h", "n", "name")

    def __init__(self, st, nc, name):
        self.h = st.enter_context(nc.semaphore(name))
        self.n = 0
        self.name = name


class Prog:
    def __init__(self, nc, st):
        self.nc = nc
        self.st = st
        self.streams = {e: [] for e in _ENGINES}

    def sem(self, name):
        s = Sem(self.st, self.nc, name)
        if not hasattr(self, "all_sems"):
            self.all_sems = []
        self.all_sems.append(s)
        return s

    def emit(self, engine, method, *args, waits=(), incs=(), **kw):
        ser = _CONFIG.get("serial")
        if ser == "aw_ve" and engine == "scalar":
            waits = list(waits) + [(s, s.n) for s in self.all_sems
                                   if s.name == "ve" and s.n > 0]
        elif ser == "vw_ae" and engine == "vector":
            waits = list(waits) + [(s, s.n) for s in self.all_sems
                                   if s.name == "ae" and s.n > 0]
        elif ser and ser.startswith("eng:"):
            _, eng_n, ctr = ser.split(":")
            if engine == eng_n:
                waits = list(waits) + [(s, s.n) for s in self.all_sems
                                       if s.n > 0 and s.name == ctr]
        elif ser and ser not in ("aw_ve", "vw_ae"):
            wl = tuple(ser.split(","))
            waits = list(waits) + [(s, s.n) for s in self.all_sems
                                   if s.n > 0 and s.name in wl]
        lw = [(s.h, int(v)) for (s, v) in waits if v > 0]
        li = []
        for (s, k) in incs:
            s.n += k
            li.append((s.h, k, s.n))
        self.streams[engine].append((method, args, kw, lw, li))

    def replay(self, engine, eng):
        for m, args, kw, lw, li in self.streams[engine]:
            for h, v in lw:
                eng.wait_ge(h, v)
            ins = eng.nop() if m == "nop" else getattr(eng, m)(*args, **kw)
            for i, (h, k, tgt) in enumerate(li):
                if i == 0:
                    ins.then_inc(h, k)
                    if len(li) > 1 and m == "nop" and not lw:
                        pass
                elif i == 1 and m != "nop":
                    eng.wait_ge(li[0][0], li[0][2])
                    eng.nop().then_inc(h, k)
                else:
                    eng.nop().then_inc(h, k)


class Bank:

    _states = {}

    def __init__(self, P, ap, key):
        self.ap = ap
        st_ = Bank._states.get((id(P), key))
        if st_ is None:
            st_ = {"p": P.sem(key + "_p"), "c": P.sem(key + "_c"), "u": 0}
            Bank._states[(id(P), key)] = st_
        self._s = st_

    @property
    def prod(self):
        return self._s["p"]

    @property
    def cons(self):
        return self._s["c"]

    @property
    def uses(self):
        return self._s["u"]

    def acquire(self):
        pe_waits = [(self.cons, self.uses)] if self.uses > 0 else []
        self._s["u"] += 1
        return pe_waits, [(self.prod, self.uses)]


def _alias(nc, arena, name, shape, dtype, offset):
    base = nc.lookup_mloc(arena).addr
    return nc.alloc_sbuf_tensor_at(name, shape, dtype, offset=base + offset)


class ArenaGroup:
    def __init__(self, nc, arena_h, size, base=0):
        self.nc = nc
        self.arena_h = arena_h
        self.size = size
        self.off = base

    def alloc(self, name, shape, dtype):
        per_part = int(np.prod(shape[1:])) * mybir.dt.size(dtype)
        off = self.off
        self.off = (off + per_part + 127) & ~127
        assert self.off <= self.size, (name, self.off, self.size)
        if not hasattr(self, "offsets"):
            self.offsets = {}
        self.offsets[name] = off
        return _alias(self.nc, self.arena_h, name, shape, dtype, off)

    def alias_at(self, at_name, name, shape, dtype):
        off = self.offsets[at_name]
        per_part = int(np.prod(shape[1:])) * mybir.dt.size(dtype)
        assert off + per_part <= self.size
        return _alias(self.nc, self.arena_h, name, shape, dtype, off)


def build(n_sub=4):
    Bank._states.clear()
    n_sub = max(1, min(4, int(n_sub)))
    nc = bass.Bass("TRN2", target_bir_lowering=False,
                   detect_race_conditions=False)

    def din(name, shape, dtype=F32):
        return nc.dram_tensor(name, shape, dtype, kind="ExternalInput").ap()

    def dscr(name, shape, dtype=F32):
        return nc.dram_tensor(name, shape, dtype, kind="Internal").ap()

    x_in = din("x", [S, D])
    y_out = nc.dram_tensor("y", [S, D], F32, kind="ExternalOutput").ap()
    dbgf = nc.dram_tensor("dbgf", [128, 256], F32, kind="ExternalOutput").ap()
    dbgb = nc.dram_tensor("dbgb", [128, 16384], BF16,
                          kind="ExternalOutput").ap()

    aq_w = din("aq_w", [128, DT * D])
    ak_w = din("ak_w", [128, DT * D])
    av_w = din("av_w", [128, DT * D])
    ao_w = din("ao_w", [128, DT * D])
    bq_w = din("bq_w", [128, DT * D])
    bk_w = din("bk_w", [128, DT * D])
    bv_w = din("bv_w", [128, DT * D])
    b_bq = din("b_bq", [128, DT])
    b_bk = din("b_bk", [128, DT])
    b_pb = din("b_pb", [128, DT])
    b_bv_row = din("b_bv_row", [1, D])
    invwl_in = din("invwl_c", [128, DT])
    f_w1 = din("f_w1", [HT, 128, DT * 128])
    f_w2 = din("f_w2", [HT, 128, DT * 128])
    f_h1 = din("f_h1", [HT, 128, HT * 128])
    f_h2 = din("f_h2", [HT, 128, HT * 128])
    f_w3 = din("f_w3", [DT, 128, HT * 128])
    f_s1 = din("f_s1", [128, HT])
    f_s2 = din("f_s2", [128, HT])
    f_sh1 = din("f_sh1", [128, HT])
    f_sh2 = din("f_sh2", [128, HT])
    f_s3 = din("f_s3", [128, DT])
    c_idb = din("c_idb", [128, 128], BF16)
    c_idf = din("c_idf", [128, 128], F32)
    c_stair = din("c_stair", [128, 128], BF16)

    xs = dscr("xs", [S, D])
    qt_s = dscr("qt_s", [DT, 128, S], BF16)
    cq_s = dscr("cq_s", [DT, 128, S], BF16)
    sq_s = dscr("sq_s", [DT, 128, S], BF16)
    ck2_s = dscr("ck2_s", [DT, 128, S // 2], BF16)
    sk2_s = dscr("sk2_s", [DT, 128, S // 2], BF16)
    v2_s = dscr("v2_s", [NT // 2, 128, D], BF16)
    po_s = dscr("po_s", [8, 128, 4 * D])
    pd_s = dscr("pd_s", [8, 128, 4])
    wsc_w1 = dscr("wsc_w1", [HT, 128, DT * 128], BF16)
    wsc_w2 = dscr("wsc_w2", [HT, 128, DT * 128], BF16)
    wsc_h1 = dscr("wsc_h1", [HT, 128, HT * 128], BF16)
    wsc_h2 = dscr("wsc_h2", [HT, 128, HT * 128], BF16)
    wsc_w3 = dscr("wsc_w3", [DT, 128, HT * 128], BF16)
    wsc_bq = dscr("wsc_bq", [128, DT * D], BF16)
    wsc_bk = dscr("wsc_bk", [128, DT * D], BF16)
    wsc_bv = dscr("wsc_bv", [128, DT * D], BF16)

    st = ExitStack()
    with st:
        P = Prog(nc, st)

        def sb(name, shape, dtype):
            return st.enter_context(nc.sbuf_tensor(name, shape, dtype))

        idb = sb("idb", [128, 128], BF16)
        idf = sb("idf", [128, 128], F32)
        stair = sb("stair", [128, 128], BF16)
        cpool = sb("cpool", [128, 8], F32)
        C_EXPA, C_EXPB, C_HALFPI = 0, 1, 2
        invwl = sb("invwl", [128, DT], F32)
        bqc = sb("bqc", [128, DT], F32)
        bkc = sb("bkc", [128, DT], F32)
        pbcn = sb("pbcn", [128, DT], F32)
        c2q = sb("c2q", [128, DT], F32)
        c2k = sb("c2k", [128, DT], F32)
        bvrow = sb("bvrow", [1, D], F32)
        bvrow_b = sb("bvrow_b", [1, D], BF16)
        ones_rf = sb("ones_rf", [1, 64], F32)
        ones_rb = sb("ones_rb", [1, 128], BF16)
        ones_cb = sb("ones_cb", [128, 1], BF16)
        s1c = sb("s1c", [128, HT], F32)
        s2c = sb("s2c", [128, HT], F32)
        sh1c = sb("sh1c", [128, HT], F32)
        sh2c = sb("sh2c", [128, HT], F32)
        s3c = sb("s3c", [128, DT], F32)
        sxh = sb("sxh", [128, 2 * NT], F32)
        ssq = sb("ssq", [128, NT], F32)
        sxs = sb("sxs", [128, NT], F32)
        stA = sb("stA", [128, NT], F32)
        rstd = sb("rstd", [128, NT], F32)
        nmr = sb("nmr", [128, NT], F32)

        rem = nc.sbuf_bytes_remaining
        ARENA = rem & ~127
        arena_h = st.enter_context(
            nc.sbuf_tensor("arena", [128, ARENA], mybir.dt.uint8))

        gA = ArenaGroup(nc, arena_h, ARENA)
        ktr = gA.alloc("ktr", [128, DT * S], BF16)
        vres = gA.alloc("vres", [128, NT * H * VG + 128], BF16)
        aw = gA.alloc("aw", [128, DT * D], BF16)
        dumpf = gA.alloc("dumpf", [128, D], BF16)
        gA_tail = gA.off
        g1a = ArenaGroup(nc, arena_h, ARENA, base=gA_tail)
        lnT_a = g1a.alloc("lnT_a", [128, DT * 512], BF16)
        xpanA = [g1a.alloc(f"xpanA{i}", [128, D], F32) for i in range(2)]
        lnb = g1a.alloc("lnb", [128, D], BF16)
        qtst = g1a.alloc("qtst", [128, DT * 512], BF16)
        g1b = ArenaGroup(nc, arena_h, ARENA, base=gA_tail)
        qtp = [g1b.alloc(f"qtp{i}", [128, DT * 512], BF16) for i in range(2)]
        pring = [g1b.alloc(f"pring{i}", [128, 1024], BF16) for i in range(3)]
        aoT = g1b.alloc("aoT", [128, DT * 512], BF16)
        bcs = g1b.alloc("bcs", [64, 512], F32)
        rcp = g1b.alloc("rcp", [1, 512], F32)
        xq = [g1b.alloc(f"xq{i}", [128, D], F32) for i in range(2)]
        xo = g1b.alloc("xo", [128, D], F32)

        gF = ArenaGroup(nc, arena_h, ARENA)
        aT = gF.alloc("aT", [128, HT * 1024], BF16)
        bT = gF.alloc("bT", [128, HT * 1024], BF16)
        lnT_f = gF.alloc("lnT_f", [128, DT * 1024], BF16)
        wring = [gF.alloc(f"wring{i}", [128, HT * 128], BF16) for i in range(2)]
        w12r = [gF.alloc(f"w12r{i}", [128, DT * 128], BF16) for i in range(4)]
        silub = gF.alloc("silub", [128, 2048], F32)
        xtf = [gF.alloc(f"xtf{i}", [128, D], F32) for i in range(2)]
        xof = gF.alloc("xof", [128, D], F32)
        lnbf = gF.alloc("lnbf", [128, D], BF16)
        dumpf2 = gF.alloc("dumpf2", [128, D], BF16)
        w3st = gF.alias_at("lnT_f", "w3st", [128, DT * 512], F32)

        gB = ArenaGroup(nc, arena_h, ARENA)
        ck1 = gB.alloc("ck1", [128, DT * 2048], BF16)
        sk1 = gB.alloc("sk1", [128, DT * 2048], BF16)
        v1r = gB.alloc("v1r", [128, (NT // 2) * D], BF16)
        resid_end = gB.off
        g3a = ArenaGroup(nc, arena_h, ARENA, base=resid_end)
        lnT_b = g3a.alloc("lnT_b", [128, DT * 512], BF16)
        bw = g3a.alloc("bw", [128, DT * D], BF16)
        xpanB = [g3a.alloc(f"xpanB{i}", [128, D], F32) for i in range(2)]
        theta = g3a.alloc("theta", [128, 512], F32)
        ubuf = g3a.alloc("ubuf", [128, 512], F32)
        kint = g3a.alloc("kint", [128, 512], I32)
        kflt = g3a.alloc("kflt", [128, 512], F32)
        rbuf = g3a.alloc("rbuf", [128, 512], F32)
        trst = [g3a.alloc(f"trst{i}", [128, DT * 512], BF16) for i in range(4)]
        vst = [g3a.alloc(f"vst{i}", [128, D], BF16) for i in range(2)]
        lnb3 = g3a.alloc("lnb3", [128, D], BF16)
        g3b = ArenaGroup(nc, arena_h, ARENA, base=resid_end)
        pbuf = g3b.alloc("pbuf", [128, 16 * 512], BF16)
        cqp = [g3b.alloc(f"cqp{i}", [128, DT * 512], BF16) for i in range(2)]
        sqp = [g3b.alloc(f"sqp{i}", [128, DT * 512], BF16) for i in range(2)]
        post = g3b.alloc("post", [128, 4 * D], F32)
        poin = g3b.alloc("poin", [128, 4 * D], F32)
        denst = g3b.alloc("denst", [128, 4], F32)
        denin = g3b.alloc("denin", [128, 4], F32)
        drc = g3b.alloc("drc", [128, 4], F32)
        xq3 = [g3b.alloc(f"xq3{i}", [128, D], F32) for i in range(2)]
        xo3 = g3b.alloc("xo3", [128, D], F32)
        dump3b = g3b.alloc("dump3b", [128, D], BF16)

        pA = st.enter_context(nc.psum_tensor("pA", [128, 1024], F32))
        pB = st.enter_context(nc.psum_tensor("pB", [128, 1024], F32))
        pC = st.enter_context(nc.psum_tensor("pC", [128, 1024], F32))
        pD = st.enter_context(nc.psum_tensor("pD", [128, 512], F32))
        ptp = st.enter_context(nc.psum_tensor("ptp", [128, 1024], BF16))

        dX = P.sem("dX")
        dS = P.sem("dS")
        dQ = P.sem("dQ")
        ph = P.sem("ph")
        ve = P.sem("ve")
        ae = P.sem("ae")
        ge = P.sem("ge")
        te = P.sem("te")

        x_in_t = x_in.rearrange("(t p) d -> p t d", p=128)
        xs_t = xs.rearrange("(t p) d -> p t d", p=128)
        y_t = y_out.rearrange("(t p) d -> p t d", p=128)
        sub_src = [x_in_t, xs_t, xs_t, xs_t]

        slot_sems = {}

        def slot_load(name, engine, out, in_, waits=()):
            s = slot_sems.get(name)
            if s is None:
                s = P.sem("sl_" + name)
                slot_sems[name] = s
            P.emit(engine, "dma_start", out=out, in_=in_, waits=waits,
                   incs=[(s, 16)])
            return [(s, s.n)]

        def slot_w(name):
            s = slot_sems.get(name)
            return [(s, s.n)] if s is not None and s.n > 0 else []

        def barrier():
            ws = [(s, s.n) for s in P.all_sems if s is not ph and s.n > 0]
            P.emit("vector", "nop", waits=ws)
            for e in _ENGINES:
                P.emit(e, "nop", incs=[(ph, 1)])
            tgt = ph.n
            for e in _ENGINES:
                P.emit(e, "nop", waits=[(ph, tgt)])

        def mm_chain2(bankA, bankB, triples, extra_pe_waits=()):
            pe_wA, ev_wA = bankA.acquire()
            pe_wB, ev_wB = bankB.acquire()
            n = len(triples)
            for i, (lhsT, rhsA, rhsB) in enumerate(triples):
                w = (list(pe_wA) + list(pe_wB) + list(extra_pe_waits))                     if i == 0 else []
                P.emit("tensor", "matmul", bankA.ap, lhsT, rhsA,
                       start=(i == 0), stop=(i == n - 1),
                       skip_group_check=True, waits=w,
                       incs=[(bankA.prod, 1)] if i == n - 1 else [])
                P.emit("tensor", "matmul", bankB.ap, lhsT, rhsB,
                       start=(i == 0), stop=(i == n - 1),
                       skip_group_check=True,
                       incs=[(bankB.prod, 1)] if i == n - 1 else [])
            return ev_wA, ev_wB

        def mm_chain(bank, pairs, extra_pe_waits=(), out_ap=None):
            pe_w, ev_w = bank.acquire()
            out = bank.ap if out_ap is None else out_ap
            n = len(pairs)
            for i, (lhsT, rhs) in enumerate(pairs):
                w = (list(pe_w) + list(extra_pe_waits)) if i == 0 else []
                inc = [(bank.prod, 1)] if i == n - 1 else []
                P.emit("tensor", "matmul", out, lhsT, rhs,
                       start=(i == 0), stop=(i == n - 1),
                       skip_group_check=True, waits=w, incs=inc)
            return ev_w

        P.emit("vector", "memset", ones_rf[:], 1.0)
        P.emit("vector", "memset", ones_rb[:], 1.0)
        P.emit("vector", "memset", ones_cb[:], 1.0)
        P.emit("vector", "memset", cpool[:, 0:1], 0.125)
        P.emit("vector", "memset", cpool[:, 1:2], 1.0 / 32.0)
        P.emit("vector", "memset", cpool[:, 2:3], PI / 2)

        for dst, src in [(idb, c_idb), (idf, c_idf), (stair, c_stair),
                         (invwl, invwl_in), (bqc, b_bq), (bkc, b_bk),
                         (pbcn, b_pb), (bvrow, b_bv_row),
                         (s1c, f_s1), (s2c, f_s2), (sh1c, f_sh1),
                         (sh2c, f_sh2), (s3c, f_s3)]:
            cw = slot_load("const", "sync", dst[:], src)
        const_w = cw

        P.emit("vector", "tensor_tensor", c2q[:], bqc[:], invwl[:], ALU.mult,
               waits=const_w, incs=[(ve, 1)])
        P.emit("vector", "tensor_tensor", c2q[:], c2q[:], pbcn[:], ALU.add,
               waits=[(ve, ve.n)], incs=[(ve, 1)])
        P.emit("vector", "tensor_tensor", c2k[:], bkc[:], invwl[:], ALU.mult,
               incs=[(ve, 1)])
        P.emit("vector", "tensor_tensor", c2k[:], c2k[:], pbcn[:], ALU.add,
               waits=[(ve, ve.n)], incs=[(ve, 1)])
        P.emit("vector", "tensor_copy", bvrow_b[:], bvrow[:], incs=[(ve, 1)])

        cast_units = []
        for srcw, dstw in ((f_w1, wsc_w1), (f_w2, wsc_w2),
                           (f_h1, wsc_h1), (f_h2, wsc_h2),
                           (f_w3, wsc_w3)):
            for t_ in range(srcw.shape[0]):
                cast_units.append((srcw[t_], dstw[t_]))
        for srcw, dstw in ((bq_w, wsc_bq), (bk_w, wsc_bk), (bv_w, wsc_bv)):
            cast_units.append((srcw, dstw))
        cast_pos = [0]

        def emit_cast_units(n):
            for _ in range(n):
                i = cast_pos[0]
                if i >= len(cast_units):
                    return
                cast_pos[0] = i + 1
                sin_, dout = cast_units[i]
                slot_load("wcast", "gpsimd", dout, sin_)

        p0_done = []
        for t in range(NT):
            slot = xpanA[t % 2]
            wfree = list(p0_done[t - 2]) if t >= 2 else []
            lw = slot_load(f"xpA{t % 2}", "sync", slot[:, 0:D],
                           x_in_t[:, t, :], waits=wfree)
            emit_cast_units(1 if t % 4 == 3 else 0)
            P.emit("vector", "reduce_sum", sxs[:, t:t + 1], slot[:, 0:D],
                   AX.X, waits=lw, incs=[(ve, 1)])
            P.emit("scalar", "activation", dumpf[:], slot[:, 0:D], AF.Square,
                   accum_out=ssq[:, t:t + 1], waits=lw, incs=[(ae, 1)])
            p0_done.append([(ve, ve.n), (ae, ae.n)])
        barrier()

        def stats_derive(first):
            def vstep(method, *args, waits=(), **kw):
                P.emit("vector", method, *args,
                       waits=list(waits) + [(ve, ve.n)], incs=[(ve, 1)], **kw)
            if not first:
                vstep("tensor_tensor", sxs[:], sxh[:, 0:2 * NT:2],
                      sxh[:, 1:2 * NT:2], ALU.add)
            vstep("tensor_scalar", sxs[:], sxs[:], 1.0 / D, None, ALU.mult)
            vstep("tensor_tensor", stA[:], sxs[:], sxs[:], ALU.mult)
            vstep("scalar_tensor_tensor", stA[:], ssq[:], 1.0 / D,
                  stA[:], ALU.mult, ALU.subtract)
            vstep("tensor_scalar", stA[:], stA[:], EPS, None, ALU.add)
            P.emit("scalar", "activation", stA[:], stA[:], AF.Sqrt,
                   waits=[(ve, ve.n)], incs=[(ae, 1)])
            vstep("reciprocal", rstd[:], stA[:], waits=[(ae, ae.n)])
            vstep("tensor_tensor", nmr[:], sxs[:], rstd[:], ALU.mult)
            vstep("tensor_scalar", nmr[:], nmr[:], -1.0, None, ALU.mult)
            return ve.n

        def ln_tile(xsrc_ap, t, lnbuf, stat_w, x_wait, pt_bank):
            P.emit("scalar", "activation", lnbuf[:], xsrc_ap, AF.Identity,
                   bias=nmr[:, t:t + 1], scale=rstd[:, t:t + 1],
                   waits=[(ve, stat_w), (pt_bank.prod, pt_bank.uses)] + x_wait,
                   incs=[(ae, 1)])
            return ae.n

        def transpose_tile(lnbuf, dest_view, bank_pt, ae_wait, evac_extra):
            pe_w, ev_w = bank_pt.acquire()
            for dt in range(DT):
                w = (list(pe_w) + [(ae, ae_wait)]) if dt == 0 else []
                inc = [(bank_pt.prod, 1)] if dt == DT - 1 else []
                P.emit("tensor", "transpose", ptp[:, dt * 128:(dt + 1) * 128],
                       lnbuf[:, dt * 128:(dt + 1) * 128], idb[:],
                       waits=w, incs=inc)
            P.emit("vector", "tensor_copy", dest_view,
                   ptp[:].rearrange("p (a b) -> p a b", a=DT),
                   waits=list(ev_w) + list(evac_extra),
                   incs=[(bank_pt.cons, 1), (ve, 1)])
            return ve.n

        def residual_store(k, t, halves, xold_slot, xout_buf, x_wait, dump):
            last = (k == n_sub - 1)
            for j, (pap, w, extra_incs, scal) in enumerate(halves):
                incs = [(ve, 1)] + list(extra_incs)
                acc = {} if last else {
                    "accum_out": sxh[:, 2 * t + j:2 * t + j + 1]}
                war = [(dS, dS.n)] if j == 0 else [(ve, ve.n)]
                P.emit("vector", "scalar_tensor_tensor",
                       xout_buf[:, j * 512:(j + 1) * 512], pap, scal,
                       xold_slot[:, j * 512:(j + 1) * 512], ALU.mult, ALU.add,
                       waits=list(w) + list(x_wait) + war, incs=incs, **acc)
            vdone = ve.n
            aw_ = []
            if not last:
                P.emit("scalar", "activation", dump[:], xout_buf[:], AF.Square,
                       accum_out=ssq[:, t:t + 1], waits=[(ve, vdone)],
                       incs=[(ae, 1)])
                aw_ = [(ae, ae.n)]
            dst = y_t if last else xs_t
            P.emit("sync", "dma_start", out=dst[:, t, :], in_=xout_buf[:],
                   waits=[(ve, vdone)] + aw_, incs=[(dS, 16)])

        DUMP = _CONFIG.get("dump")

        def dumpb(col, ap, waits):
            P.emit("sync", "dma_start", out=dbgb[:, col:col + ap.shape[-1]],
                   in_=ap, waits=waits, incs=[(dS, 16)])

        def phase_attn_a(k):
            xsrc = sub_src[k]
            stat_w = stats_derive(first=(k == 0))
            if DUMP:
                P.emit("sync", "dma_start", out=dbgf[:, 0:32], in_=rstd[:],
                       waits=[(ve, stat_w)], incs=[(dS, 16)])
                P.emit("sync", "dma_start", out=dbgf[:, 32:64], in_=nmr[:],
                       waits=[(ve, stat_w)], incs=[(dS, 16)])
                P.emit("sync", "dma_start", out=dbgf[:, 64:96], in_=sxs[:],
                       waits=[(ve, stat_w)], incs=[(dS, 16)])
                P.emit("sync", "dma_start", out=dbgf[:, 96:128], in_=ssq[:],
                       waits=[(ve, stat_w)], incs=[(dS, 16)])
            bA0 = Bank(P, pA[:, 0:512], "pA0")
            bA1 = Bank(P, pA[:, 512:1024], "pA1")
            bPT = Bank(P, ptp[:], "pt")
            P.emit("vector", "memset", vres[:, 64:NT * H * VG:VG], 1.0)
            P.emit("vector", "memset", vres[:, 65:NT * H * VG:VG], 0.0)
            P.emit("vector", "memset",
                   vres[:, NT * H * VG:NT * H * VG + 128], 0.0,
                   incs=[(ve, 1)])
            vones_done = ve.n

            aw_sem = P.sem("aw_sem")
            aw_free = P.sem("aw_free")
            qdma = P.sem("qdma")
            lnT_free_rec = 0
            for p in range(8):
                for tl in range(4):
                    t = 4 * p + tl
                    xp = xpanA[t % 2]
                    xw = [(ae, ae.n)] if t >= 2 else []
                    lw = slot_load(f"xpA{t % 2}", "sync", xp[:],
                                   xsrc[:, t, :], waits=xw)
                    ae_n = ln_tile(xp[:], t, lnb, stat_w, lw, bPT)
                    dest = lnT_a[:].rearrange("p (a s) -> p a s", a=DT)[
                        :, :, tl * 128:(tl + 1) * 128]
                    transpose_tile(lnb, dest, bPT, ae_n,
                                   [(te, lnT_free_rec)] if p > 0 else [])
                lnT_done = ve.n
                for wi, wdram in enumerate((aq_w, ak_w, av_w)):
                    nw = 3 * p + wi
                    P.emit("gpsimd", "dma_start", out=aw[:], in_=wdram,
                           waits=[(aw_free, nw)] if nw > 0 else [],
                           incs=[(aw_sem, 16)])
                    wld = aw_sem.n
                    if wi < 2:
                        for et in range(DT):
                            bk = bA0 if et % 2 == 0 else bA1
                            pairs = [
                                (aw[:, it * D + et * 128: it * D + et * 128 + 128],
                                 lnT_a[:, it * 512:(it + 1) * 512])
                                for it in range(DT)]
                            evw = mm_chain(bk, pairs,
                                           extra_pe_waits=[(aw_sem, wld),
                                                           (ve, lnT_done)])
                            if wi == 0:
                                P.emit("vector", "tensor_copy",
                                       qtst[:, et * 512:(et + 1) * 512],
                                       bk.ap,
                                       waits=list(evw) + [(qdma, qdma.n)],
                                       incs=[(bk.cons, 1), (ve, 1)])
                            else:
                                P.emit("vector", "tensor_copy",
                                       ktr[:, et * S + p * 512:
                                           et * S + (p + 1) * 512], bk.ap,
                                       waits=evw,
                                       incs=[(bk.cons, 1), (ve, 1)])
                        if wi == 0:
                            P.emit("gpsimd", "dma_start",
                                   out=qt_s.rearrange("a p s -> p a s")[
                                       :, :, p * 512:(p + 1) * 512],
                                   in_=qtst[:].rearrange("p (a s) -> p a s",
                                                         a=DT),
                                   waits=[(ve, ve.n)], incs=[(qdma, 16)])
                        P.emit("tensor", "nop",
                               waits=[(bA0.prod, bA0.uses),
                                      (bA1.prod, bA1.uses)],
                               incs=[(aw_free, 1)])
                        emit_cast_units(2)
                    else:
                        for tl in range(4):
                            t = 4 * p + tl
                            for eb in range(2):
                                bk = bA0 if (2 * tl + eb) % 2 == 0 else bA1
                                pairs = [
                                    (lnT_a[:, it * 512 + tl * 128:
                                           it * 512 + tl * 128 + 128],
                                     aw[:, it * D + eb * 512:
                                        it * D + (eb + 1) * 512])
                                    for it in range(DT)]
                                evw = mm_chain(bk, pairs,
                                               extra_pe_waits=[(aw_sem, wld),
                                                               (ve, lnT_done)])
                                dv = vres[:, 0:NT * H * VG].rearrange(
                                    "p (kt h g) -> p kt h g", kt=NT, h=H)[
                                    :, t, eb * 8:(eb + 1) * 8, 0:64]
                                P.emit("vector", "tensor_copy", dv,
                                       bk.ap.rearrange("p (h r) -> p h r", h=8),
                                       waits=list(evw) + [(ve, vones_done)],
                                       incs=[(bk.cons, 1), (ve, 1)])
                        P.emit("tensor", "nop",
                               waits=[(bA0.prod, bA0.uses),
                                      (bA1.prod, bA1.uses)],
                               incs=[(aw_free, 1), (te, 1)])
                        lnT_free_rec = te.n
            p1a_v = ve.n
            p1a_q = qdma.n
            if DUMP:
                w = [(ve, p1a_v)]
                dumpb(0, lnT_a[:], w)
                dumpb(4096, ktr[:, 0:512], w)
                dumpb(4608, vres[:, 0:65], w)
                dumpb(5664, qtst[:], w)
            if _CONFIG.get("dbg") == "1a":
                P.emit("vector", "nop", waits=[(ve, p1a_v)])
                P.emit("scalar", "nop", waits=[(qdma, p1a_q)])
                barrier()
                return

            P.emit("gpsimd", "dma_start", out=aw[:], in_=ao_w,
                   waits=[(aw_free, aw_free.n)], incs=[(aw_sem, 16)])
            ao_ld = aw_sem.n
            bSC = [Bank(P, pA[:], "pAf"), Bank(P, pB[:], "pBf")]
            bOp = [Bank(P, pC[:, 0:512], "pC0"),
                   Bank(P, pC[:, 512:1024], "pC1")]
            bBC = Bank(P, pD[0:64, :], "pD")
            bM = Bank(P, pD[:], "pD")
            pr_sem = P.sem("a_pr")
            pf_sem = P.sem("a_pf")
            pr_uses = 0
            panel_rec = []
            dbg = _CONFIG.get("dbg") or ""
            base = dbg.split("_")[0]
            np_1b = int(base[2:]) if base.startswith("1b") and len(base) > 2 else 8
            for p in range(np_1b):
                qp = qtp[p % 2]
                qw = [(te, panel_rec[p - 2])] if p >= 2 else []
                qldw = slot_load(
                    f"qtp{p % 2}", "sync",
                    qp[:].rearrange("p (a s) -> p a s", a=DT),
                    qt_s.rearrange("a p s -> p a s")[
                        :, :, p * 512:(p + 1) * 512],
                    waits=qw + ([(qdma, p1a_q), (ve, p1a_v)]
                                if p == 0 else []))
                emit_cast_units(12 if p > 0 else 0)
                nkk = 4 * (p + 1)
                pending_norm = [None]

                def emit_norm(p=p):
                    if pending_norm[0] is None:
                        return
                    h_, o_ev_w_ = pending_norm[0]
                    pending_norm[0] = None
                    bO_ = bOp[h_ % 2]
                    et_ = h_ // 2
                    ro_ = (h_ % 2) * 64
                    P.emit("vector", "reciprocal", rcp[:], bO_.ap[64:65, :],
                           waits=list(o_ev_w_) + [(bBC.prod, bBC.uses)],
                           incs=[(ve, 1)])
                    rcp_v = ve.n
                    bc_pe_w, bc_ev_w = bBC.acquire()
                    P.emit("tensor", "matmul", bBC.ap[0:64, :], ones_rf[:],
                           rcp[:], start=True, stop=True,
                           skip_group_check=True,
                           waits=list(bc_pe_w) + [(ve, rcp_v)],
                           incs=[(bBC.prod, 1)])
                    P.emit("vector", "tensor_copy", bcs[:], bBC.ap[0:64, :],
                           waits=list(bc_ev_w) + [(ve, ve.n)],
                           incs=[(bBC.cons, 1), (ve, 1)])
                    aoT_w = [(te, panel_rec[p - 1])] if p >= 1 else []
                    P.emit("vector", "tensor_tensor",
                           aoT[ro_:ro_ + 64, et_ * 512:(et_ + 1) * 512],
                           bO_.ap[0:64, :], bcs[:], ALU.mult,
                           waits=aoT_w + [(ve, ve.n)],
                           incs=[(bO_.cons, 1), (ve, 1)])

                pipe = not dbg.endswith(("_sc", "_av"))
                pending_av = [None]

                def flush_av():
                    if pending_av[0] is not None:
                        go = pending_av[0]
                        pending_av[0] = None
                        go()

                def mk_av(h, bO, slot, kk0, kk1, pr_t, o_pe_w,
                          is_first, is_last):
                    def go():
                        for idx, kk in enumerate((kk0, kk1)):
                            vap = vres[:, kk * (H * VG) + h * VG:
                                       kk * (H * VG) + h * VG + 128]
                            w = [(pr_sem, pr_t)]
                            if is_first and idx == 0:
                                w = w + list(o_pe_w)
                            last = (is_last and idx == 1)
                            inc = []
                            if last:
                                inc.append((bO.prod, 1))
                            if idx == 1:
                                inc.append((pf_sem, 1))
                            P.emit("tensor", "matmul", bO.ap[0:128, :], vap,
                                   slot[:, idx * 512:(idx + 1) * 512],
                                   start=(is_first and idx == 0), stop=last,
                                   skip_group_check=True, waits=w, incs=inc)
                    return go

                for h in range(H):
                    et = h // 2
                    ro = (h % 2) * 64
                    bO = bOp[h % 2]
                    o_pe_w, o_ev_w = bO.acquire()
                    for j in range(nkk // 2):
                        kk0, kk1 = 2 * j, 2 * j + 1
                        scb = bSC[j % 2]
                        pe_w, ev_w = scb.acquire()
                        for idx, kk in enumerate((kk0, kk1)):
                            lhsT = ktr[ro:ro + 64,
                                       et * S + kk * 128: et * S + kk * 128 + 128]
                            rhs = qp[ro:ro + 64, et * 512:(et + 1) * 512]
                            w = (list(pe_w) + qldw + [(aw_sem, ao_ld)]
                                 ) if idx == 0 else []
                            P.emit("tensor", "matmul",
                                   scb.ap[:, idx * 512:(idx + 1) * 512],
                                   lhsT, rhs, start=True, stop=True,
                                   skip_group_check=True, waits=w,
                                   incs=[(scb.prod, 1)] if idx == 1 else [])
                        slot = pring[pr_uses % 3]
                        pw = [(pf_sem, pr_uses - 2)] if pr_uses >= 3 else []
                        diag = (kk1 >= 4 * p)
                        P.emit("scalar", "activation", slot[:], scb.ap, AF.Exp,
                               scale=cpool[:, C_EXPA:C_EXPA + 1],
                               waits=list(ev_w) + pw,
                               incs=([(scb.cons, 1), (ae, 1)] if diag else
                                     [(scb.cons, 1), (pr_sem, 1)]))
                        if diag:
                            expd = ae.n
                            first = True
                            for idx, kk in enumerate((kk0, kk1)):
                                if kk < 4 * p:
                                    continue
                                off = (kk - 4 * p) * 128
                                base = idx * 512
                                if off > 0:
                                    P.emit("vector", "memset",
                                           slot[:, base:base + off], 0.0,
                                           waits=[(ae, expd)] if first else [])
                                    first = False
                                P.emit("vector", "tensor_tensor",
                                       slot[:, base + off:base + off + 128],
                                       slot[:, base + off:base + off + 128],
                                       stair[:], ALU.mult,
                                       waits=[(ae, expd)] if first else [],
                                       incs=[(ve, 1)])
                                first = False
                            P.emit("vector", "nop", waits=[(ve, ve.n)],
                                   incs=[(pr_sem, 1)])
                        pr_uses += 1
                        if dbg.endswith("_sc"):
                            P.emit("vector", "nop", waits=[(pr_sem, pr_uses)],
                                   incs=[(pf_sem, 1)])
                            continue
                        av = mk_av(h, bO, slot, kk0, kk1, pr_uses, o_pe_w,
                                   j == 0, j == nkk // 2 - 1)
                        if pipe:
                            flush_av()
                            if j == 0:
                                emit_norm()
                            pending_av[0] = av
                        else:
                            av()
                            if j == 0:
                                emit_norm()
                    if dbg.endswith("_sc"):
                        continue
                    if dbg.endswith("_av"):
                        flush_av()
                        P.emit("vector", "tensor_copy", bcs[:],
                               bO.ap[0:64, :], waits=o_ev_w,
                               incs=[(bO.cons, 1)])
                        continue
                    pending_norm[0] = (h, o_ev_w)
                flush_av()
                emit_norm()
                aoT_done = ve.n
                if dbg.endswith(("_sc", "_av", "_nr")):
                    P.emit("tensor", "nop",
                           waits=[(bSC[0].prod, bSC[0].uses),
                                  (bSC[1].prod, bSC[1].uses),
                                  (bO.prod, bO.uses)],
                           incs=[(te, 1)])
                    panel_rec.append(te.n)
                    continue
                for qs in range(4):
                    t = 4 * p + qs
                    xslot = xq[qs % 2]
                    xw = [(ve, ve.n)] if (p * 4 + qs) >= 2 else []
                    lw = slot_load(f"xq{qs % 2}", "sync", xslot[:],
                                   xsrc[:, t, :], waits=xw)
                    halves = []
                    for mb in range(2):
                        pairs = [
                            (aoT[:, vt * 512 + qs * 128:
                                 vt * 512 + qs * 128 + 128],
                             aw[:, vt * D + mb * 512: vt * D + (mb + 1) * 512])
                            for vt in range(DT)]
                        evw = mm_chain(bM, pairs,
                                       extra_pe_waits=[(ve, aoT_done)])
                        halves.append((bM.ap, evw, [(bM.cons, 1)], 1.0))
                    residual_store(k, t, halves, xslot, xo, lw, dumpf)
                P.emit("tensor", "nop",
                       waits=[(bM.prod, bM.uses)], incs=[(te, 1)])
                panel_rec.append(te.n)
            emit_cast_units(10 ** 9)
            barrier()

        def phase_ffn(k):
            xsrc = sub_src[k]
            stat_w = stats_derive(first=False)
            bPT = Bank(P, ptp[:], "pt")
            b1 = [Bank(P, pA[:, 0:512], "pA0"),
                  Bank(P, pA[:, 512:1024], "pA1")]
            b2 = [Bank(P, pB[:, 0:512], "pB0"),
                  Bank(P, pB[:, 512:1024], "pB1")]
            bTD = Bank(P, pD[:], "pD")
            bTC = Bank(P, pC[:, 0:512], "pC0")
            if "f_wf" not in slot_sems:
                slot_sems["f_wf"] = P.sem("f_wf")
                slot_sems["f_w12f"] = P.sem("f_w12f")
            wfree = slot_sems["f_wf"]
            w12free = slot_sems["f_w12f"]
            if not hasattr(P, "wload_n"):
                P.wload_n = [0]
                P.w12load_n = [0]
            wload_n = P.wload_n
            w12load_n = P.w12load_n
            rec_ln = 0
            rec_tr = 0
            dbgv = _CONFIG.get("dbg") or ""
            n_pan = 4
            stop_stage = None
            if dbgv.startswith("fp"):
                parts = dbgv[2:].split("_")
                n_pan = int(parts[0])
                if len(parts) > 1:
                    stop_stage = int(parts[1][1:])
            for p in range(n_pan):
                for tl in range(8):
                    t = 8 * p + tl
                    xslot = xtf[tl % 2]
                    xw = [(ve, ve.n)] if (t >= 2 or p > 0) else []
                    lw = slot_load(f"xtf{tl % 2}", "sync", xslot[:],
                                   xsrc[:, t, :], waits=xw)
                    ae_n = ln_tile(xslot[:], t, lnbf, stat_w, lw, bPT)
                    dest = lnT_f[:].rearrange("p (a s) -> p a s", a=DT)[
                        :, :, tl * 128:(tl + 1) * 128]
                    transpose_tile(lnbf, dest, bPT, ae_n,
                                   [(te, rec_tr)] if p > 0 else [])
                lnT_done = ve.n
                for ot in range(HT):
                    wldw = []
                    for wi, wdram in enumerate((wsc_w1, wsc_w2)):
                        i = w12load_n[0]
                        w12load_n[0] += 1
                        slot = w12r[i % 4]
                        wfw = [(w12free, i - 3)] if i >= 4 else []
                        wldw += slot_load(f"w12_{i % 4}", "gpsimd", slot[:],
                                          wdram[ot], waits=wfw)
                    s1b = w12r[(w12load_n[0] - 2) % 4]
                    s2b = w12r[(w12load_n[0] - 1) % 4]
                    evws = []
                    for wb, banks in ((s1b, b1), (s2b, b2)):
                        triples = [
                            (wb[:, it * 128:(it + 1) * 128],
                             lnT_f[:, it * 1024: it * 1024 + 512],
                             lnT_f[:, it * 1024 + 512: it * 1024 + 1024])
                            for it in range(DT)]
                        ev0, ev1 = mm_chain2(banks[0], banks[1], triples,
                                             extra_pe_waits=wldw +
                                             [(ve, lnT_done)])
                        evws.append((banks, ev0, ev1))
                    (bk1s, evw1a, evw1b), (bk2s, evw2a, evw2b) = evws
                    evws = [(bk1s[0], evw1a, bk2s[0], evw2a),
                            (bk1s[1], evw1b, bk2s[1], evw2b)]
                    P.emit("tensor", "nop",
                           waits=[(b1[0].prod, b1[0].uses),
                                  (b1[1].prod, b1[1].uses),
                                  (b2[0].prod, b2[0].uses),
                                  (b2[1].prod, b2[1].uses)],
                           incs=[(w12free, 2)])
                    sl = silub[:, (ot % 2) * 1024:(ot % 2) * 1024 + 1024]
                    for sb_i, (bk1, evw1, bk2, evw2) in enumerate(evws):
                        P.emit("scalar", "activation",
                               sl[:, sb_i * 512:(sb_i + 1) * 512], bk1.ap,
                               AF.Silu, scale=s1c[:, ot:ot + 1],
                               waits=list(evw1) + [(ve, ve.n)],
                               incs=[(bk1.cons, 1), (ae, 1)])
                        P.emit("vector", "scalar_tensor_tensor",
                               aT[:, ot * 1024 + sb_i * 512:
                                  ot * 1024 + (sb_i + 1) * 512],
                               bk2.ap, s2c[:, ot:ot + 1],
                               sl[:, sb_i * 512:(sb_i + 1) * 512],
                               ALU.mult, ALU.mult,
                               waits=list(evw2) + [(ae, ae.n)],
                               incs=[(bk2.cons, 1), (ve, 1)])
                P.emit("tensor", "nop",
                       waits=[(b1[0].prod, b1[0].uses),
                              (b1[1].prod, b1[1].uses),
                              (b2[0].prod, b2[0].uses),
                              (b2[1].prod, b2[1].uses)],
                       incs=[(te, 1)])
                rec_ln = te.n
                if _CONFIG.get("dbg") == "f1":
                    P.emit("vector", "nop", waits=[(ve, ve.n), (ae, ae.n)])
                    barrier()
                    return
                prev_done = ("ve", ve.n)
                if _CONFIG.get("dump") == "a2" and p == n_pan - 1:
                    dumpb(0, aT[:, 0:16384], [(ve, ve.n)])
                for si, (wdram, weng, src, dst, scl) in enumerate(
                        [(wsc_h1, "gpsimd", aT, bT, sh1c),
                         (wsc_h2, "gpsimd", bT, aT, sh2c)]):
                    for ot in range(HT):
                        slot = wring[ot % 2]
                        uses = wload_n[0]
                        wload_n[0] += 1
                        wfw = [(wfree, uses - 1)] if uses >= 2 else []
                        wldw = slot_load(f"wr{ot % 2}_{weng}", weng, slot[:],
                                         wdram[ot], waits=wfw)
                        banks = b1 if ot % 2 == 0 else b2
                        pd = [(ve, prev_done[1])] if prev_done[0] == "ve" \
                            else [(ae, prev_done[1])]
                        triples = [
                            (slot[:, it * 128:(it + 1) * 128],
                             src[:, it * 1024: it * 1024 + 512],
                             src[:, it * 1024 + 512: it * 1024 + 1024])
                            for it in range(HT)]
                        ev0, ev1 = mm_chain2(banks[0], banks[1], triples,
                                             extra_pe_waits=wldw + pd)
                        for sb_i, evw in ((0, ev0), (1, ev1)):
                            bk = banks[sb_i]
                            P.emit("scalar", "activation",
                                   dst[:, ot * 1024 + sb_i * 512:
                                       ot * 1024 + (sb_i + 1) * 512],
                                   bk.ap, AF.Gelu, scale=scl[:, ot:ot + 1],
                                   waits=evw, incs=[(bk.cons, 1), (ae, 1)])
                        P.emit("tensor", "nop",
                               waits=[(banks[0].prod, banks[0].uses),
                                      (banks[1].prod, banks[1].uses)],
                               incs=[(wfree, 1)])
                    if stop_stage == 2 + si and p == n_pan - 1:
                        P.emit("vector", "nop", waits=[(ve, ve.n), (ae, ae.n)])
                        barrier()
                        return
                    prev_done = ("ae", ae.n)
                s4banks = [b1[0], b1[1], b2[0], b2[1]]
                for sh in range(2):
                    for ot in range(DT):
                        slot = wring[ot % 2]
                        uses = wload_n[0]
                        wload_n[0] += 1
                        wfw = [(wfree, uses - 1)] if uses >= 2 else []
                        wldw = slot_load(f"wr{ot % 2}_gpsimd", "gpsimd",
                                         slot[:], wsc_w3[ot], waits=wfw)
                        bk = s4banks[ot % 4]
                        pairs = [
                            (slot[:, it * 128:(it + 1) * 128],
                             aT[:, it * 1024 + sh * 512:
                                it * 1024 + sh * 512 + 512])
                            for it in range(HT)]
                        evw = mm_chain(bk, pairs,
                                       extra_pe_waits=wldw +
                                       [(ae, prev_done[1])])
                        war = [(te, rec_tr)] if (p > 0 or sh > 0) else []
                        P.emit("vector", "tensor_scalar",
                               w3st[:, ot * 512:(ot + 1) * 512],
                               bk.ap, s3c[:, ot:ot + 1], None, ALU.mult,
                               waits=list(evw) + war,
                               incs=[(bk.cons, 1), (ve, 1)])
                        P.emit("tensor", "nop",
                               waits=[(bk.prod, bk.uses)],
                               incs=[(wfree, 1)])
                    w3st_done = ve.n
                    for tlh in range(4):
                        tl = sh * 4 + tlh
                        t = 8 * p + tl
                        xslot = xtf[tl % 2]
                        lw = slot_load(f"xtf{tl % 2}", "sync", xslot[:],
                                       xsrc[:, t, :], waits=[(ve, ve.n)])
                        halves = []
                        for half, bk in ((0, bTD), (1, bTC)):
                            pe_w, ev_w = bk.acquire()
                            for q in range(4):
                                dt = half * 4 + q
                                w = (list(pe_w) + [(ve, w3st_done)]) \
                                    if q == 0 else []
                                P.emit("tensor", "transpose",
                                       bk.ap[:, q * 128:(q + 1) * 128],
                                       w3st[:, dt * 512 + tlh * 128:
                                            dt * 512 + tlh * 128 + 128],
                                       idf[:], waits=w,
                                       incs=[(bk.prod, 1)] if q == 3 else [])
                            halves.append((bk.ap, ev_w, [(bk.cons, 1)], 1.0))
                        residual_store(k, t, halves, xslot, xof, lw,
                                       dumpf2)
                    P.emit("tensor", "nop",
                           waits=[(bTD.prod, bTD.uses),
                                  (bTC.prod, bTC.uses)],
                           incs=[(te, 1)])
                    rec_tr = te.n
            barrier()

        def phase_attn_b(k):
            xsrc = sub_src[k]
            stat_w = stats_derive(first=False)
            bPT = Bank(P, ptp[:], "pt")
            bA0 = Bank(P, pA[:, 0:512], "pA0")
            bA1 = Bank(P, pA[:, 512:1024], "pA1")
            bw_sem = P.sem("bw_sem")
            bw_free = P.sem("bw_free")
            sdma = P.sem("b_sdma")
            rec_ln3 = 0

            def trig(bk, evw, et, c2, sin_dst, cos_dst,
                     sin_war=(), cos_war=()):
                P.emit("vector", "tensor_scalar", theta[:], bk.ap,
                       invwl[:, et:et + 1], c2[:, et:et + 1],
                       ALU.mult, ALU.add,
                       waits=list(evw) + [(ge, ge.n), (ve, ve.n)],
                       incs=[(bk.cons, 1), (ve, 1)])
                th_v = ve.n
                for is_cos, dstap in ((0, sin_dst), (1, cos_dst)):
                    if is_cos:
                        P.emit("vector", "tensor_scalar", ubuf[:], theta[:],
                               INV2PI, 0.25, ALU.mult, ALU.add,
                               waits=[(ge, ge.n), (ve, th_v)],
                               incs=[(ve, 1)])
                    else:
                        P.emit("vector", "tensor_scalar", ubuf[:], theta[:],
                               INV2PI, None, ALU.mult,
                               waits=[(ge, ge.n), (ve, th_v)],
                               incs=[(ve, 1)])
                    uv = ve.n
                    P.emit("vector", "tensor_copy", kint[:], ubuf[:],
                           waits=[(ve, uv)], incs=[(ve, 1)])
                    P.emit("vector", "tensor_copy", kflt[:], kint[:],
                           waits=[(ve, ve.n)], incs=[(ve, 1)])
                    P.emit("vector", "scalar_tensor_tensor", rbuf[:], kflt[:],
                           -TWO_PI, theta[:], ALU.mult, ALU.add,
                           waits=[(ve, ve.n), (ae, ae.n)], incs=[(ve, 1)])
                    hi = (PI / 2) if is_cos else PI
                    lo = (-1.5 * PI) if is_cos else -PI
                    P.emit("vector", "tensor_scalar", ubuf[:], rbuf[:],
                           float(np.float32(hi)), None, ALU.is_gt,
                           waits=[(ve, ve.n), (ge, ge.n)], incs=[(ve, 1)])
                    P.emit("vector", "scalar_tensor_tensor", kflt[:], ubuf[:],
                           -TWO_PI, rbuf[:], ALU.mult, ALU.add,
                           waits=[(ve, ve.n)], incs=[(ve, 1)])
                    P.emit("vector", "tensor_scalar", ubuf[:], kflt[:],
                           float(np.float32(lo)), None, ALU.is_lt,
                           waits=[(ve, ve.n)], incs=[(ve, 1)])
                    P.emit("vector", "scalar_tensor_tensor", rbuf[:], ubuf[:],
                           TWO_PI, kflt[:], ALU.mult, ALU.add,
                           waits=[(ve, ve.n)], incs=[(ve, 1)])
                    rv = ve.n
                    kwargs = {}
                    war = list(cos_war if is_cos else sin_war)
                    if is_cos:
                        kwargs["bias"] = cpool[:, C_HALFPI:C_HALFPI + 1]
                    P.emit("scalar", "activation", dstap, rbuf[:], AF.Sin,
                           waits=[(ve, rv)] + war,
                           incs=[(ae, 1)], **kwargs)
                return ae.n

            for p in range(8):
                for tl in range(4):
                    t = 4 * p + tl
                    xp = xpanB[t % 2]
                    xw = [(ae, ae.n)] if t >= 2 else []
                    lw = slot_load(f"xpB{t % 2}", "sync", xp[:],
                                   xsrc[:, t, :], waits=xw)
                    ae_n = ln_tile(xp[:], t, lnb3, stat_w, lw, bPT)
                    dest = lnT_b[:].rearrange("p (a s) -> p a s", a=DT)[
                        :, :, tl * 128:(tl + 1) * 128]
                    transpose_tile(lnb3, dest, bPT, ae_n,
                                   [(te, rec_ln3)] if p > 0 else [])
                lnT_done = ve.n
                for wi, wdram in enumerate((wsc_bq, wsc_bk, wsc_bv)):
                    nw = 3 * p + wi
                    P.emit("gpsimd", "dma_start", out=bw[:], in_=wdram,
                           waits=[(bw_free, nw)] if nw > 0 else [],
                           incs=[(bw_sem, 16)])
                    wld = bw_sem.n
                    if wi < 2:
                        c2 = c2q if wi == 0 else c2k
                        for et in range(DT):
                            bk = bA0 if et % 2 == 0 else bA1
                            pairs = [
                                (bw[:, it * D + et * 128:
                                    it * D + et * 128 + 128],
                                 lnT_b[:, it * 512:(it + 1) * 512])
                                for it in range(DT)]
                            evw = mm_chain(bk, pairs,
                                           extra_pe_waits=[(bw_sem, wld),
                                                           (ve, lnT_done)])
                            if wi == 0:
                                sin_dst = trst[1][:, et * 512:(et + 1) * 512]
                                cos_dst = trst[0][:, et * 512:(et + 1) * 512]
                                sw, cw2 = slot_w("st_sq"), slot_w("st_cq")
                            elif p < 4:
                                sin_dst = sk1[:, et * 2048 + p * 512:
                                              et * 2048 + (p + 1) * 512]
                                cos_dst = ck1[:, et * 2048 + p * 512:
                                              et * 2048 + (p + 1) * 512]
                                sw, cw2 = [], []
                            else:
                                sin_dst = trst[3][:, et * 512:(et + 1) * 512]
                                cos_dst = trst[2][:, et * 512:(et + 1) * 512]
                                sw, cw2 = slot_w("st_sk"), slot_w("st_ck")
                            trig(bk, evw, et, c2, sin_dst, cos_dst,
                                 sin_war=sw, cos_war=cw2)
                        P.emit("tensor", "nop",
                               waits=[(bA0.prod, bA0.uses),
                                      (bA1.prod, bA1.uses)],
                               incs=[(bw_free, 1)])
                        if wi == 0:
                            slot_load("st_cq", "scalar",
                                      cq_s.rearrange("a p s -> p a s")[
                                          :, :, p * 512:(p + 1) * 512],
                                      trst[0][:].rearrange(
                                          "p (a s) -> p a s", a=DT),
                                      waits=[(ae, ae.n)])
                            slot_load("st_sq", "scalar",
                                      sq_s.rearrange("a p s -> p a s")[
                                          :, :, p * 512:(p + 1) * 512],
                                      trst[1][:].rearrange(
                                          "p (a s) -> p a s", a=DT),
                                      waits=[(ae, ae.n)])
                        elif p >= 4:
                            slot_load("st_ck", "scalar",
                                      ck2_s.rearrange("a p s -> p a s")[
                                          :, :, (p - 4) * 512:(p - 3) * 512],
                                      trst[2][:].rearrange(
                                          "p (a s) -> p a s", a=DT),
                                      waits=[(ae, ae.n)])
                            slot_load("st_sk", "scalar",
                                      sk2_s.rearrange("a p s -> p a s")[
                                          :, :, (p - 4) * 512:(p - 3) * 512],
                                      trst[3][:].rearrange(
                                          "p (a s) -> p a s", a=DT),
                                      waits=[(ae, ae.n)])
                    else:
                        for tl in range(4):
                            t = 4 * p + tl
                            for eb in range(2):
                                bk = bA0 if (2 * tl + eb) % 2 == 0 else bA1
                                pe_w, evw0 = bk.acquire()
                                pairs = [
                                    (lnT_b[:, it * 512 + tl * 128:
                                           it * 512 + tl * 128 + 128],
                                     bw[:, it * D + eb * 512:
                                        it * D + (eb + 1) * 512])
                                    for it in range(DT)]
                                for i, (lh, rh) in enumerate(pairs):
                                    w = (list(pe_w) + [(bw_sem, wld),
                                                       (ve, lnT_done)]
                                         ) if i == 0 else []
                                    P.emit("tensor", "matmul", bk.ap, lh, rh,
                                           start=(i == 0), stop=False,
                                           skip_group_check=True, waits=w)
                                P.emit("tensor", "matmul", bk.ap, ones_rb[:],
                                       bvrow_b[:, eb * 512:(eb + 1) * 512],
                                       start=False, stop=True,
                                       skip_group_check=True,
                                       incs=[(bk.prod, 1)])
                                if p < 4:
                                    dstv = v1r[:, t * D + eb * 512:
                                               t * D + (eb + 1) * 512]
                                    P.emit("vector", "tensor_copy", dstv,
                                           bk.ap, waits=evw0,
                                           incs=[(bk.cons, 1), (ve, 1)])
                                else:
                                    dstv = vst[tl % 2][:, eb * 512:
                                                       (eb + 1) * 512]
                                    P.emit("vector", "tensor_copy", dstv,
                                           bk.ap,
                                           waits=list(evw0) +
                                           slot_w(f"st_v{tl % 2}"),
                                           incs=[(bk.cons, 1), (ve, 1)])
                            if p >= 4:
                                slot_load(f"st_v{tl % 2}", "sync",
                                          v2_s[t - 16], vst[tl % 2][:],
                                          waits=[(ve, ve.n)])
                        P.emit("tensor", "nop",
                               waits=[(bA0.prod, bA0.uses),
                                      (bA1.prod, bA1.uses)],
                               incs=[(bw_free, 1), (te, 1)])
                        rec_ln3 = te.n
            barrier()

            bSC = [Bank(P, pA[:, 0:512], "pA0"),
                   Bank(P, pA[:, 512:1024], "pA1")]
            bAV = [Bank(P, pB[:, 0:512], "pB0"),
                   Bank(P, pB[:, 512:1024], "pB1"),
                   Bank(P, pC[:, 0:512], "pC0"),
                   Bank(P, pC[:, 512:1024], "pC1")]
            bDEN = Bank(P, pD[:, 0:4], "pD4")
            psem = P.sem("b_ps")
            pfree = P.sem("b_pf")
            rec3b = []
            po_store = [0] * 8
            for kh in range(2):
                kldw = []
                if kh == 1:
                    kw = [(te, rec3b[-1])]
                    kldw += slot_load(
                        "kck", "sync",
                        ck1[:].rearrange("p (a s) -> p a s", a=DT),
                        ck2_s.rearrange("a p s -> p a s"), waits=kw)
                    kldw += slot_load(
                        "ksk", "sync",
                        sk1[:].rearrange("p (a s) -> p a s", a=DT),
                        sk2_s.rearrange("a p s -> p a s"), waits=kw)
                    kldw += slot_load(
                        "kv1", "sync",
                        v1r[:].rearrange("p (t d) -> p t d", t=NT // 2),
                        v2_s.rearrange("t p d -> p t d"), waits=kw)
                for p in range(8):
                    idx = kh * 8 + p
                    cqs = cqp[idx % 2]
                    sqs = sqp[idx % 2]
                    qw = [(te, rec3b[idx - 2])] if idx >= 2 else []
                    if idx == 0:
                        qw = qw + slot_w("st_cq") + slot_w("st_sq")
                    qldw = slot_load(
                        f"cq{idx % 2}", "sync",
                        cqs[:].rearrange("p (a s) -> p a s", a=DT),
                        cq_s.rearrange("a p s -> p a s")[
                            :, :, p * 512:(p + 1) * 512], waits=qw)
                    qldw += slot_load(
                        f"sq{idx % 2}", "sync",
                        sqs[:].rearrange("p (a s) -> p a s", a=DT),
                        sq_s.rearrange("a p s -> p a s")[
                            :, :, p * 512:(p + 1) * 512], waits=qw)
                    poldw = []
                    if kh == 1:
                        poldw = slot_load(
                            "pin", "sync",
                            poin[:].rearrange("p (q d) -> p q d", q=4),
                            po_s[p].rearrange("p (q d) -> p q d", q=4),
                            waits=[(ve, ve.n)] + slot_w("pst"))
                        poldw += slot_load("din", "sync", denin[:],
                                           pd_s[p], waits=slot_w("pdst"))
                    d_pe_w, d_ev_w = bDEN.acquire()
                    pending_den = [None]

                    def flush_den():
                        if pending_den[0] is not None:
                            go = pending_den[0]
                            pending_den[0] = None
                            go()

                    def mk_den(kk, ps_t):
                        def go():
                            for qs in range(4):
                                w = [(psem, ps_t)] if qs == 0 else []
                                if kk == 0 and qs == 0:
                                    w = w + list(d_pe_w)
                                last_d = (kk == 15 and qs == 3)
                                P.emit("tensor", "matmul",
                                       bDEN.ap[:, qs:qs + 1],
                                       pbuf[:, kk * 512 + qs * 128:
                                            kk * 512 + (qs + 1) * 128],
                                       ones_cb[:], start=(kk == 0),
                                       stop=(kk == 15),
                                       skip_group_check=True, waits=w,
                                       incs=([(bDEN.prod, 1)]
                                             if last_d else []))
                        return go

                    for kk in range(16):
                        scb = bSC[kk % 2]
                        pairs = []
                        for et in range(DT):
                            pairs.append(
                                (ck1[:, et * 2048 + kk * 128:
                                     et * 2048 + kk * 128 + 128],
                                 cqs[:, et * 512:(et + 1) * 512]))
                            pairs.append(
                                (sk1[:, et * 2048 + kk * 128:
                                     et * 2048 + kk * 128 + 128],
                                 sqs[:, et * 512:(et + 1) * 512]))
                        evw = mm_chain(scb, pairs,
                                       extra_pe_waits=qldw + kldw)
                        pw = [(pfree, (idx - 1) * 16 + kk + 1)] if idx >= 1 \
                            else []
                        P.emit("scalar", "activation",
                               pbuf[:, kk * 512:(kk + 1) * 512], scb.ap,
                               AF.Exp, scale=cpool[:, C_EXPB:C_EXPB + 1],
                               waits=list(evw) + pw,
                               incs=[(scb.cons, 1), (psem, 1)])
                        flush_den()
                        pending_den[0] = mk_den(kk, psem.n)
                    flush_den()
                    av_start = [(psem, psem.n)]
                    ev_by_qs = [[None, None] for _ in range(4)]
                    for eh in range(2):
                        for qs in range(4):
                            bk = bAV[qs]
                            pe_w, evw = bk.acquire()
                            ev_by_qs[qs][eh] = (bk, evw)
                            for kk in range(16):
                                w = (list(pe_w) + av_start) if kk == 0 else []
                                last = (kk == 15)
                                P.emit("tensor", "matmul", bk.ap,
                                       pbuf[:, kk * 512 + qs * 128:
                                            kk * 512 + (qs + 1) * 128],
                                       v1r[:, kk * D + eh * 512:
                                           kk * D + (eh + 1) * 512],
                                       start=(kk == 0), stop=last,
                                       skip_group_check=True, waits=w,
                                       incs=[(bk.prod, 1)] if last else [])
                            dsl = post[:, qs * 1024 + eh * 512:
                                       qs * 1024 + (eh + 1) * 512]
                            if kh == 0:
                                P.emit("vector", "tensor_copy", dsl, bk.ap,
                                       waits=list(evw) + slot_w("pst"),
                                       incs=[(bk.cons, 1), (ve, 1)])
                            else:
                                P.emit("vector", "tensor_tensor", dsl, bk.ap,
                                       poin[:, qs * 1024 + eh * 512:
                                            qs * 1024 + (eh + 1) * 512],
                                       ALU.add,
                                       waits=list(evw) + poldw,
                                       incs=[(bk.cons, 1), (ve, 1)])
                    post_v = ve.n
                    if kh == 0:
                        P.emit("vector", "tensor_copy", denst[:], bDEN.ap,
                               waits=d_ev_w, incs=[(bDEN.cons, 1), (ve, 1)])
                        dv = ve.n
                        slot_load("pst", "sync",
                                  po_s[p].rearrange("p (q d) -> p q d", q=4),
                                  post[:].rearrange("p (q d) -> p q d", q=4),
                                  waits=[(ve, post_v)])
                        slot_load("pdst", "sync", pd_s[p], denst[:],
                                  waits=[(ve, dv)])
                    else:
                        P.emit("vector", "tensor_tensor", denst[:], bDEN.ap,
                               denin[:], ALU.add,
                               waits=list(d_ev_w) + poldw + [(ve, ve.n)],
                               incs=[(ve, 1)])
                        P.emit("vector", "nop", waits=[(ve, ve.n)],
                               incs=[(bDEN.cons, 1)])
                        P.emit("vector", "reciprocal", drc[:], denst[:],
                               waits=[(ve, ve.n)], incs=[(ve, 1)])
                        drc_v = ve.n
                        for qs in range(4):
                            t = 4 * p + qs
                            xslot = xq3[qs % 2]
                            xw = [(ve, ve.n)] if (p * 4 + qs) >= 2 else []
                            lw = slot_load(f"xq3_{qs % 2}", "sync", xslot[:],
                                           xsrc[:, t, :], waits=xw)
                            halves = [
                                (post[:, qs * 1024:qs * 1024 + 512],
                                 [(ve, post_v), (ve, drc_v)], [],
                                 drc[:, qs:qs + 1]),
                                (post[:, qs * 1024 + 512:(qs + 1) * 1024],
                                 [(ve, post_v), (ve, drc_v)], [],
                                 drc[:, qs:qs + 1]),
                            ]
                            residual_store(k, t, halves, xslot, xo3,
                                           lw, dump3b)
                    P.emit("tensor", "nop",
                           waits=[(bAV[0].prod, bAV[0].uses),
                                  (bAV[1].prod, bAV[1].uses),
                                  (bAV[2].prod, bAV[2].uses),
                                  (bAV[3].prod, bAV[3].uses),
                                  (bDEN.prod, bDEN.uses)],
                           incs=[(pfree, 16), (te, 1)])
                    rec3b.append(te.n)
            barrier()

        if _CONFIG.get("dbg") == "p0":
            pass
        else:
            phase_attn_a(0)
        if n_sub >= 2:
            phase_ffn(1)
        if n_sub >= 3:
            phase_attn_b(2)
        if n_sub >= 4:
            phase_ffn(3)

        P.emit("sync", "nop", waits=[(dS, dS.n)])

        with nc.Block() as block:
            block.sync(lambda e: P.replay("sync", e))
            block.gpsimd(lambda e: P.replay("gpsimd", e))
            block.vector(lambda e: P.replay("vector", e))
            block.scalar(lambda e: P.replay("scalar", e))
            block.tensor(lambda e: P.replay("tensor", e))

    return nc


def _wT_sb(w):
    wt = np.ascontiguousarray(w.T)
    din, dout = wt.shape
    it = din // 128
    return np.ascontiguousarray(
        wt.reshape(it, 128, dout).transpose(1, 0, 2).reshape(128, it * dout))


def _w_slab(w, ot_tiles):
    wt = np.ascontiguousarray(w.T)
    din, dout = wt.shape
    itn = din // 128
    assert dout // 128 == ot_tiles
    a = wt.reshape(itn, 128, ot_tiles, 128).transpose(2, 1, 0, 3)
    return np.ascontiguousarray(a.reshape(ot_tiles, 128, itn * 128))


def _col128(v):
    return np.ascontiguousarray(v.reshape(-1, 128).T)


def kernel(**inputs):
    f32 = np.float32
    x = np.ascontiguousarray(inputs["L0_attn_a_in_features"], dtype=f32)

    common = {}
    common["aq_w"] = _wT_sb(f32(inputs["L0_attn_a_q_proj_weight"]))
    common["ak_w"] = _wT_sb(f32(inputs["L0_attn_a_k_proj_weight"]))
    common["av_w"] = _wT_sb(f32(inputs["L0_attn_a_v_proj_weight"]))
    common["ao_w"] = _wT_sb(f32(inputs["L0_attn_a_o_proj_weight"]))
    common["bq_w"] = _wT_sb(f32(inputs["L0_attn_b_Wq"]))
    common["bk_w"] = _wT_sb(f32(inputs["L0_attn_b_Wk"]))
    common["bv_w"] = _wT_sb(f32(inputs["L0_attn_b_Wv"]))
    common["b_bq"] = _col128(f32(inputs["L0_attn_b_bq"]))
    common["b_bk"] = _col128(f32(inputs["L0_attn_b_bk"]))
    common["b_pb"] = _col128(f32(inputs["L0_attn_b_phase_bias"]))
    common["b_bv_row"] = f32(inputs["L0_attn_b_bv"]).reshape(1, D)
    common["f_w1"] = _w_slab(f32(inputs["L0_ffn_w1_w"]), HT)
    common["f_w2"] = _w_slab(f32(inputs["L0_ffn_w2_w"]), HT)
    common["f_h1"] = _w_slab(f32(inputs["L0_ffn_h1_w"]), HT)
    common["f_h2"] = _w_slab(f32(inputs["L0_ffn_h2_w"]), HT)
    common["f_w3"] = _w_slab(f32(inputs["L0_ffn_w3_w"]), DT)
    common["f_s1"] = _col128(f32(inputs["L0_ffn_w1_s"]))
    common["f_s2"] = _col128(f32(inputs["L0_ffn_w2_s"]))
    common["f_sh1"] = _col128(f32(inputs["L0_ffn_h1_s"]))
    common["f_sh2"] = _col128(f32(inputs["L0_ffn_h2_s"]))
    common["f_s3"] = _col128(f32(inputs["L0_ffn_w3_s"]))

    wavelengths = (np.arange(1, D + 1, dtype=np.float64)
                   * (2.0 * np.pi / D)).astype(f32)
    invwl = (1.0 / (wavelengths + np.float32(1e-8))).astype(f32)
    common["invwl_c"] = _col128(invwl)
    common["c_idb"] = np.eye(128, dtype=ml_dtypes.bfloat16)
    common["c_idf"] = np.eye(128, dtype=f32)
    common["c_stair"] = (np.arange(128)[None, :]
                         >= np.arange(128)[:, None]).astype(ml_dtypes.bfloat16)

    nc = build(_CONFIG["n_sub"])
    in_maps = []
    for b in range(8):
        m = dict(common)
        m["x"] = np.ascontiguousarray(x[b])
        in_maps.append(m)
    res = run_bass_kernel_spmd(nc, in_maps, core_ids=list(range(8)))
    out = np.stack([res.results[b]["y"] for b in range(8)], axis=0)
    return out
